# version 05
# baseline (speedup 1.0000x reference)
import numpy as np
import ml_dtypes
import concourse.bass as bass
import concourse.mybir as mybir
from concourse.bass_utils import run_bass_kernel_spmd

F32 = mybir.dt.float32
BF16 = mybir.dt.bfloat16
I32 = mybir.dt.int32
AF = mybir.ActivationFunctionType
ALU = mybir.AluOpType
AX = mybir.AxisListType

B, NSEQ, CD = 2, 8192, 2048
TOKC = 2048
FF = 5632
NT1 = 16
NT2 = 4
NM = 4
NCJ = 16
NFB2 = 22
NFC = 44
EPS_LN = 1e-5
EPS_RMS = 1e-5
EPS_ATT = 1e-15
GU_CNT = float(2 * FF * CD)
DN_CNT = float(CD * FF)
GUH = 44 * 128 * 256
N_ABS_GU = 32
N_ABS_DN = 16
LNC = 1.0 / CD


def build_program(dbg=False):
    nc = bass.Bass("TRN2", target_bir_lowering=False, detect_race_conditions=False)

    x_in = nc.dram_tensor("x_in", [TOKC, CD], F32, kind="ExternalInput").ap()
    tkv = nc.dram_tensor("tkv", [CD, 2 * CD], F32, kind="ExternalInput").ap()
    tq = nc.dram_tensor("tq", [CD, CD], F32, kind="ExternalInput").ap()
    tproj = nc.dram_tensor("tproj", [CD, CD], F32, kind="ExternalInput").ap()
    agu_in = nc.dram_tensor("agu_in", [512, 2 * FF], F32, kind="ExternalInput").ap()
    adn_in = nc.dram_tensor("adn_in", [1408, CD], F32, kind="ExternalInput").ap()
    qgu_in = nc.dram_tensor("qgu_in", [256, 2 * FF], F32, kind="ExternalInput").ap()
    qdn_in = nc.dram_tensor("qdn_in", [704, CD], F32, kind="ExternalInput").ap()
    gam_in = nc.dram_tensor("gam_in", [128, CD], F32, kind="ExternalInput").ap()
    bpr_in = nc.dram_tensor("bpr_in", [128, CD], F32, kind="ExternalInput").ap()
    id_in = nc.dram_tensor("id_in", [128, 128], F32, kind="ExternalInput").ap()
    es_in = nc.dram_tensor("es_in", [64, NCJ * 128], F32, kind="ExternalInput").ap()
    out_hbm = nc.dram_tensor("out_hbm", [TOKC, CD], F32, kind="ExternalOutput").ap()
    if dbg:
        d_xn = nc.dram_tensor("d_xn", [128, CD], F32, kind="ExternalOutput").ap()
        d_ktm = nc.dram_tensor("d_ktm", [128, CD], F32, kind="ExternalOutput").ap()
        d_vtm = nc.dram_tensor("d_vtm", [128, 64 * 33], F32, kind="ExternalOutput").ap()
        d_ccsb = nc.dram_tensor("d_ccsb", [128, 530], F32, kind="ExternalOutput").ap()
        d_swv = nc.dram_tensor("d_swv", [128, 4], F32, kind="ExternalOutput").ap()
        d_qf = nc.dram_tensor("d_qf", [128, NCJ * 512], F32, kind="ExternalOutput").ap()
        d_dent = nc.dram_tensor("d_dent", [64, 512], F32, kind="ExternalOutput").ap()
        d_att = nc.dram_tensor("d_att", [128, NCJ * 512], F32, kind="ExternalOutput").ap()
        d_xqf = nc.dram_tensor("d_xqf", [128, NCJ * 512], F32, kind="ExternalOutput").ap()
        d_hid = nc.dram_tensor("d_hid", [128, NFC * 512], F32, kind="ExternalOutput").ap()
        d_lnv = nc.dram_tensor("d_lnv", [128, 128], F32, kind="ExternalOutput").ap()
        d_wg = nc.dram_tensor("d_wg", [128, 8192], F32, kind="ExternalOutput").ap()
        d_out2a = nc.dram_tensor("d_out2a", [128, NM * CD], F32, kind="ExternalOutput").ap()
        d_osq = nc.dram_tensor("d_osq", [128, 4], F32, kind="ExternalOutput").ap()

    cc1_in = nc.dram_tensor("cc1_in", [128, 530], F32, kind="Internal").ap()
    cc1_out = nc.dram_tensor("cc1_out", [128, 530], F32, kind="Internal").ap()
    ccg_in = nc.dram_tensor("ccg_in", [2 * GUH], BF16, kind="Internal").ap()
    ccg_out = nc.dram_tensor("ccg_out", [16 * GUH], BF16, kind="Internal").ap()
    ccd_in = nc.dram_tensor("ccd_in", [704 * CD], BF16, kind="Internal").ap()
    ccd_out = nc.dram_tensor("ccd_out", [FF * CD], BF16, kind="Internal").ap()

    wgu_view = ccg_out.rearrange("(cj b p f) -> b p cj f", cj=16, b=44, p=128)
    wdn_view = ccd_out.rearrange("(f c) -> f c", c=CD)

    ident32 = nc.alloc_sbuf_tensor("ident32", [128, 128], F32)
    ones128 = nc.alloc_sbuf_tensor("ones128", [128, 128], F32)
    onescol = nc.alloc_sbuf_tensor("onescol", [128, 1], F32)
    esel = nc.alloc_sbuf_tensor("esel", [64, NCJ * 128], BF16)
    vkb = nc.alloc_sbuf_tensor("vkb", [128, 528], BF16)
    ksbig = nc.alloc_sbuf_tensor("ksbig", [128, NCJ * 64], BF16)
    ccsb = nc.alloc_sbuf_tensor("ccsb", [128, 530], F32)
    absred = nc.alloc_sbuf_tensor("absred", [128, 64], F32)
    swv = nc.alloc_sbuf_tensor("swv", [128, 4], F32)
    zpad2 = nc.alloc_sbuf_tensor("zpad2", [128, 2], F32)
    lnv = nc.alloc_sbuf_tensor("lnv", [128, 128], F32)
    gam_b = nc.alloc_sbuf_tensor("gam_b", [128, CD], F32)
    bpr_b = nc.alloc_sbuf_tensor("bpr_b", [128, CD], F32)
    szpad = nc.alloc_sbuf_tensor("szpad", [128, 512], F32)
    osq = nc.alloc_sbuf_tensor("osq", [128, 4], F32)
    cvec = nc.alloc_sbuf_tensor("cvec", [128, 4], F32)
    sh128 = nc.alloc_sbuf_tensor("sh128", [128, 128], F32)

    ARN = 179968
    arena = nc.alloc_sbuf_tensor("arena", [128, ARN], mybir.dt.uint8)
    AB = nc.lookup_mloc(arena).addr

    def at(name, shape, dtype, off):
        itb = {F32: 4, BF16: 2, I32: 4}[dtype]
        need = int(np.prod(shape[1:])) * itb
        assert off % 64 == 0 and off + need <= ARN, (name, off, need)
        return nc.alloc_sbuf_tensor_at(name, shape, dtype, offset=AB + off)

    p_tkv = at("p_tkv", [128, NCJ * 4096], BF16, 0)
    p_x = [at(f"p_x{i}", [128, CD], F32, 131072 + 8192 * i) for i in range(2)]
    p_xn = at("p_xn", [128, CD], F32, 147456)
    p_xnf = at("p_xnf", [128, CD], BF16, 155648)
    p_ktm = at("p_ktm", [128, CD], BF16, 159744)
    p_vtm = at("p_vtm", [128, 64 * 33], BF16, 163840)
    p_abs = [at(f"p_abs{i}", [128, 1408], F32, 168128 + 5632 * i) for i in range(2)]

    a_acc = at("a_acc", [128, NM * CD], F32, 0)
    a_xnf = at("a_xnf", [128, NM * CD], BF16, 32768)
    a_qf = at("a_qf", [128, NCJ * 512], BF16, 49152)
    a_att = at("a_att", [128, NCJ * 512], BF16, 65536)
    a_xn = at("a_xn", [128, CD], F32, 81920)
    a_tq = [at(f"a_tq{i}", [128, CD], BF16, 90112 + 4096 * i) for i in range(2)]
    a_tp = [at(f"a_tp{i}", [128, 8 * 512], BF16, 98304 + 8192 * i) for i in range(2)]
    a_dent = at("a_dent", [64, 512], F32, 114688)
    a_dent2 = at("a_dent2", [64, 512], F32, 116736)
    a_rden = at("a_rden", [64, 512], BF16, 118784)
    a_bcs = at("a_bcs", [128, 512], BF16, 119808)
    q_in = at("q_in", [128, 2816], F32, 120832)
    q_u = at("q_u", [128, 2816], F32, 132096)
    q_i1 = at("q_i1", [128, 2816], I32, 143360)
    q_c1 = at("q_c1", [128, 2816], F32, 154624)
    q_r = at("q_r", [128, 2816], F32, 165888)
    q_wq = at("q_wq", [128, 2816], BF16, 120832)

    b_hid = at("b_hid", [128, NFC * 512], BF16, 0)
    b_xqf = at("b_xqf", [128, NCJ * 512], BF16, 45056)
    b_x = [at(f"b_x{i}", [128, CD], F32, 61440 + 8192 * i) for i in range(2)]
    b_xn = at("b_xn", [128, CD], F32, 77824)
    b_gt = at("b_gt", [128, CD], F32, 86016)
    b_u = at("b_u", [128, CD], F32, 94208)
    b_i1 = at("b_i1", [128, CD], I32, 102400)
    b_c1 = at("b_c1", [128, CD], F32, 110592)
    b_xq32 = at("b_xq32", [128, CD], F32, 118784)
    b_wg = [at(f"b_wg{i}", [128, 2 * 4096], BF16, 126976 + 16384 * i) for i in range(2)]
    b_wd = [at(f"b_wd{i}", [128, CD], BF16, 159744 + 4096 * i) for i in range(2)]
    b_h1 = at("b_h1", [128, 512], F32, 167936)
    b_h2 = at("b_h2", [128, 512], F32, 169984)
    b_mp = at("b_mp", [128, 512], F32, 172032)
    b_mn = at("b_mn", [128, 512], F32, 174080)
    hq_i1 = at("hq_i1", [128, 512], I32, 172032)
    hq_c1 = at("hq_c1", [128, 512], F32, 174080)
    b_fs = [b_xn, b_gt]

    pb = [nc.alloc_psum_tensor(f"pb{i}", [128, 512], F32) for i in range(8)]

    from contextlib import ExitStack
    es = ExitStack()
    SEM = {k: es.enter_context(nc.semaphore(k))
           for k in ("SX", "SG", "SV", "SS", "ST", "SGC", "SCC", "SBAR")}
    cnt = {k: 0 for k in SEM}
    M = {}

    def bump(ins, key, n=1, mark=None):
        ins.then_inc(SEM[key], n)
        cnt[key] += n
        if mark is not None:
            M[mark] = (key, cnt[key])
        return cnt[key]

    def wait(eng, mark):
        key, val = M[mark]
        eng.wait_ge(SEM[key], val)

    def dma(eng, out, in_, key, mark=None, **kw):
        ins = eng.dma_start(out=out, in_=in_, **kw)
        return bump(ins, key, 16, mark)

    def col(base):
        return lnv[:, base:base + 1]

    def s_chain1(s):
        s.activation(col(8), col(0), AF.Identity, scale=LNC)
        s.drain()
        s.activation(col(12), col(4), AF.Identity, scale=LNC)
        s.drain()
        s.activation(col(16), col(8), AF.Square)
        s.drain()
        s.activation(col(20), col(16), AF.Identity, scale=-1.0, bias=col(12))
        s.drain()
        s.activation(col(24), col(20), AF.Identity, bias=cvec[:, 0:1])
        s.drain()
        s.activation(col(28), col(24), AF.Sqrt)
        s.drain()

    def s_chain2(s, rms):
        s.activation(col(36), col(32), AF.Square)
        s.drain()
        s.activation(col(40), col(36), AF.Identity, scale=col(24))
        s.drain()
        s.activation(col(44), col(40), AF.Identity, scale=-0.5, bias=cvec[:, 1:2])
        s.drain()
        s.activation(col(48), col(32), AF.Identity, scale=col(44))
        s.drain()
        s.activation(col(52), col(8), AF.Identity, scale=col(48))
        s.drain()
        s.activation(col(56), col(52), AF.Identity, scale=-1.0)
        s.drain()
        if rms:
            s.activation(col(60), col(20), AF.Identity, scale=col(48))
            s.drain()
            s.activation(col(64), col(60), AF.Identity, scale=col(48))
            s.drain()
            s.activation(col(68), col(64), AF.Identity, bias=cvec[:, 0:1])
            s.drain()
            s.activation(col(72), col(68), AF.Sqrt)
            s.drain()

    def s_chain3(s):
        s.activation(col(80), col(76), AF.Square)
        s.drain()
        s.activation(col(84), col(80), AF.Identity, scale=col(68))
        s.drain()
        s.activation(col(88), col(84), AF.Identity, scale=-0.5, bias=cvec[:, 1:2])
        s.drain()
        s.activation(col(92), col(76), AF.Identity, scale=col(88))
        s.drain()

    with nc.Block() as blk:
        @blk.sync
        def _(sy):
            dma(sy, ident32[:], id_in, "SX")
            dma(sy, gam_b[:], gam_in, "SX")
            dma(sy, bpr_b[:], bpr_in, "SX", mark="consts")

        @blk.vector
        def _(v):
            v.memset(ones128[:], 1.0)
            v.memset(onescol[:], 1.0)
            v.memset(zpad2[:], 0.0)
            v.memset(szpad[:], 0.0)
            v.memset(p_vtm[:], 1.0)
            v.memset(cvec[:, 0:1], EPS_LN)
            v.memset(cvec[:, 1:2], 1.5)
            v.memset(cvec[:, 2:3], 128.0)
            v.drain()
            bump(v.nop(), "SV", 1, "v_init")

        @blk.gpsimd
        def _(g):
            dma(g, esel[:], es_in, "SG", mark="esel")
            for cj in range(NCJ):
                dma(g, p_tkv[:, 4096 * cj:4096 * (cj + 1)],
                    tkv[128 * cj:128 * (cj + 1), :], "SG", mark=f"tkv{cj}")

        NABS = N_ABS_GU + N_ABS_DN
        agu_f = agu_in.rearrange("(p r) c -> p (r c)", r=4)
        adn_f = adn_in.rearrange("(p r) c -> p (r c)", r=11)
        for tt in range(NT1):
            @blk.sync
            def _(sy, tt=tt):
                if tt >= 2:
                    wait(sy, f"xn32_{tt - 2}")
                dma(sy, p_x[tt % 2][:], x_in[128 * tt:128 * (tt + 1), :], "SX",
                    mark=f"x_{tt}")
                for a in range(2 * tt, min(2 * tt + 2, 32)):
                    if a >= 2:
                        wait(sy, f"absr_{a - 2}")
                    if a < N_ABS_GU:
                        src_ = agu_f[:, 1408 * a:1408 * (a + 1)]
                    else:
                        d = a - N_ABS_GU
                        src_ = adn_f[:, 1408 * d:1408 * (d + 1)]
                    dma(sy, p_abs[a % 2][:], src_, "SX", mark=f"absl_{a}")

            @blk.vector
            def _(v, tt=tt):
                wait(v, f"x_{tt}")
                v.reduce_sum(col(0), p_x[tt % 2][:], axis=AX.X)
                v.drain()
                bump(v.nop(), "SV", 1, f"s1_{tt}")

            @blk.scalar
            def _(s, tt=tt):
                wait(s, f"x_{tt}")
                if tt == 0:
                    wait(s, "v_init")
                if tt > 0:
                    wait(s, f"tr_{tt - 1}_3")
                s.activation(p_xn[:], p_x[tt % 2][:], AF.Square)
                s.drain()
                bump(s.nop(), "SS", 1, f"sq_{tt}")

            @blk.vector
            def _(v, tt=tt):
                wait(v, f"sq_{tt}")
                v.reduce_sum(col(4), p_xn[:], axis=AX.X)
                v.drain()
                bump(v.nop(), "SV", 1, f"s2_{tt}")

            @blk.scalar
            def _(s, tt=tt):
                wait(s, f"s2_{tt}")
                s_chain1(s)
                bump(s.nop(), "SS", 1, f"sd_{tt}")

            @blk.vector
            def _(v, tt=tt):
                wait(v, f"sd_{tt}")
                v.reciprocal(col(32), col(28))
                v.drain()
                bump(v.nop(), "SV", 1, f"r0_{tt}")

            @blk.scalar
            def _(s, tt=tt):
                wait(s, f"r0_{tt}")
                s_chain2(s, rms=False)
                s.activation(p_xn[:], p_x[tt % 2][:], AF.Identity,
                             scale=col(48), bias=col(56))
                s.drain()
                bump(s.nop(), "SS", 1, f"xn32_{tt}")

            for half2 in range(2):
                @blk.tensor
                def _(t, tt=tt, half2=half2):
                    if half2 == 0:
                        wait(t, f"xn32_{tt}")
                        if tt == 0:
                            wait(t, "consts")
                    for r in (2 * half2, 2 * half2 + 1):
                        if r >= 2:
                            wait(t, f"trev_{tt}_{r - 2}")
                        elif tt > 0:
                            wait(t, f"trev_{tt - 1}_{r + 2}")
                        ins = None
                        for j4 in range(4):
                            cj = 4 * r + j4
                            ins = t.transpose(
                                pb[2 + (r % 2)][:, 128 * j4:128 * (j4 + 1)],
                                p_xn[:, 128 * cj:128 * (cj + 1)], ident32[:])
                        bump(ins, "ST", 1, f"tr_{tt}_{r}")

                @blk.scalar
                def _(s, tt=tt, half2=half2):
                    for r in (2 * half2, 2 * half2 + 1):
                        wait(s, f"tr_{tt}_{r}")
                        for j4 in range(4):
                            cj = 4 * r + j4
                            s.activation(p_xnf[:, 128 * cj:128 * (cj + 1)],
                                         pb[2 + (r % 2)][:, 128 * j4:128 * (j4 + 1)],
                                         AF.Identity)
                        s.drain()
                        bump(s.nop(), "SS", 1, f"trev_{tt}_{r}")

            for oc in range(8):
                @blk.tensor
                def _(t, tt=tt, oc=oc):
                    if oc == 0:
                        if tt == 0:
                            wait(t, f"tkv{NCJ - 1}")
                        wait(t, f"trev_{tt}_3")
                    po, ptt = oc - 2, tt
                    if po < 0:
                        po, ptt = po + 8, tt - 1
                    if ptt >= 0:
                        wait(t, f"kvev_{ptt}_{po}")
                    ins = None
                    for cj in range(NCJ):
                        ins = t.matmul(pb[oc % 2][:],
                                       p_xnf[:, 128 * cj:128 * (cj + 1)],
                                       p_tkv[:, 4096 * cj + 512 * oc:
                                             4096 * cj + 512 * (oc + 1)],
                                       start=(cj == 0), stop=(cj == NCJ - 1))
                    bump(ins, "ST", 1, f"kvmm_{tt}_{oc}")

                if oc < 4:
                    @blk.scalar
                    def _(s, tt=tt, oc=oc):
                        wait(s, f"kvmm_{tt}_{oc}")
                        s.activation(p_ktm[:, 512 * oc:512 * (oc + 1)],
                                     pb[oc % 2][:], AF.Relu)
                        s.drain()
                        bump(s.nop(), "SS", 1, f"kvev_{tt}_{oc}")
                else:
                    @blk.vector
                    def _(v, tt=tt, oc=oc):
                        wait(v, f"kvmm_{tt}_{oc}")
                        h0 = 16 * (oc - 4)
                        dst = p_vtm[:].rearrange("p (h e) -> p h e", e=33)[
                            :, h0:h0 + 16, 0:32]
                        v.tensor_copy(dst, pb[oc % 2][:].rearrange(
                            "p (h e) -> p h e", e=32))
                        v.drain()
                        bump(v.nop(), "SV", 1, f"kvev_{tt}_{oc}")

            @blk.vector
            def _(v, tt=tt):
                for a in range(2 * tt, min(2 * tt + 2, 32)):
                    wait(v, f"absl_{a}")
                    v.reduce_sum(absred[:, a:a + 1], p_abs[a % 2][:], axis=AX.X,
                                 apply_absolute_value=True)
                    v.drain()
                    bump(v.nop(), "SV", 1, f"absr_{a}")

            @blk.tensor
            def _(t, tt=tt):
                wait(t, f"kvev_{tt}_3")
                wait(t, f"kvev_{tt}_7")
                if tt > 0:
                    wait(t, f"vkev_{tt - 1}")
                ins = None
                for h in range(64):
                    cj, i = h // 4, h % 4
                    bank = pb[5] if cj < 8 else pb[6]
                    cjl = cj % 8
                    ins = t.matmul(bank[32 * i:32 * (i + 1), 33 * cjl:33 * (cjl + 1)],
                                   p_ktm[:, 32 * h:32 * (h + 1)],
                                   p_vtm[:, 33 * h:33 * (h + 1)],
                                   start=True, stop=True,
                                   tile_position=(0, 32 * i))
                bump(ins, "ST", 1, f"vk_{tt}")

            @blk.vector
            def _(v, tt=tt):
                wait(v, f"vk_{tt}")
                for half in range(2):
                    dst = ccsb[:, 264 * half:264 * (half + 1)]
                    pbank = pb[5 + half]
                    if tt == 0:
                        v.tensor_copy(dst, pbank[:, 0:264])
                    else:
                        v.scalar_tensor_tensor(dst, pbank[:, 0:264], 1.0, dst,
                                               op0=ALU.mult, op1=ALU.add)
                    v.drain()
                bump(v.nop(), "SV", 1, f"vkev_{tt}")

        if dbg:
            @blk.gpsimd
            def _(g):
                wait(g, f"xn32_{NT1 - 1}")
                dma(g, d_xn, p_xn[:], "SG")
                wait(g, f"kvev_{NT1 - 1}_3")
                dma(g, d_ktm, p_ktm[:], "SG")
                wait(g, f"kvev_{NT1 - 1}_7")
                dma(g, d_vtm, p_vtm[:], "SG", mark="dump_p1")

        for a in range(32, NABS):
            @blk.sync
            def _(sy, a=a):
                wait(sy, f"absr_{a - 2}")
                if a < N_ABS_GU:
                    src_ = agu_f[:, 1408 * a:1408 * (a + 1)]
                else:
                    d = a - N_ABS_GU
                    src_ = adn_f[:, 1408 * d:1408 * (d + 1)]
                dma(sy, p_abs[a % 2][:], src_, "SX", mark=f"absl_{a}")

            @blk.vector
            def _(v, a=a):
                wait(v, f"absl_{a}")
                v.reduce_sum(absred[:, a:a + 1], p_abs[a % 2][:], axis=AX.X,
                             apply_absolute_value=True)
                v.drain()
                bump(v.nop(), "SV", 1, f"absr_{a}")

        @blk.vector
        def _(v):
            wait(v, f"vkev_{NT1 - 1}")
            wait(v, f"absr_{NABS - 1}")
            v.reduce_sum(ccsb[:, 528:529], absred[:, 0:N_ABS_GU], axis=AX.X)
            v.reduce_sum(ccsb[:, 529:530], absred[:, N_ABS_GU:NABS], axis=AX.X)
            v.drain()
            bump(v.nop(), "SV", 1, "ccsb_ready")

        @blk.sync
        def _(sy):
            wait(sy, "ccsb_ready")
            dma(sy, cc1_in, ccsb[:], "SX", mark="cc1_in")

        @blk.gpsimd
        def _(g):
            wait(g, "cc1_in")
            ins = g.collective_compute(
                "AllReduce", ALU.add,
                replica_groups=[[0, 1, 2, 3], [4, 5, 6, 7]],
                ins=[cc1_in], outs=[cc1_out])
            bump(ins, "SCC", 1, "ar1")

        @blk.sync
        def _(sy):
            wait(sy, "ar1")
            dma(sy, ccsb[:], cc1_out, "SX", mark="cc1_back")

        @blk.tensor
        def _(t):
            wait(t, "cc1_back")
            wait(t, "v_init")
            t.matmul(pb[0][0:1, 0:1], ccsb[:, 528:529], onescol[:],
                     start=True, stop=True)
            ins = t.matmul(pb[1][0:1, 0:1], ccsb[:, 529:530], onescol[:],
                           start=True, stop=True)
            bump(ins, "ST", 1, "abst")

        @blk.vector
        def _(v):
            wait(v, "abst")
            v.tensor_copy(zpad2[0:1, 0:1], pb[0][0:1, 0:1])
            v.tensor_copy(zpad2[0:1, 1:2], pb[1][0:1, 0:1])
            v.drain()
            bump(v.nop(), "SV", 1, "zpad2")
            v.tensor_copy(vkb[:], ccsb[:, 0:528])
            v.memset(ksbig[:], 0.0)
            v.drain()
            for cj in range(NCJ):
                for i in range(4):
                    h = 4 * cj + i
                    v.tensor_copy(
                        ksbig[32 * i:32 * (i + 1), 64 * cj + h:64 * cj + h + 1],
                        vkb[32 * i:32 * (i + 1), 33 * cj + 32:33 * cj + 33])
            v.drain()
            bump(v.nop(), "SV", 1, "vkb_ready")

        @blk.tensor
        def _(t):
            wait(t, "zpad2")
            ins = t.matmul(pb[2][:, 0:2], ones128[:], zpad2[:],
                           start=True, stop=True)
            bump(ins, "ST", 1, "absbc")

        @blk.vector
        def _(v):
            wait(v, "absbc")
            v.tensor_scalar(swv[:, 0:1], pb[2][:, 0:1], 1.0 / GU_CNT, 1e-5,
                            op0=ALU.mult, op1=ALU.max)
            v.tensor_scalar(swv[:, 2:3], pb[2][:, 1:2], 1.0 / DN_CNT, 1e-5,
                            op0=ALU.mult, op1=ALU.max)
            v.drain()
            v.reciprocal(swv[:, 1:2], swv[:, 0:1])
            v.reciprocal(swv[:, 3:4], swv[:, 2:3])
            v.drain()
            bump(v.nop(), "SV", 1, "sw_ready")

        if dbg:
            @blk.gpsimd
            def _(g):
                wait(g, "cc1_back")
                dma(g, d_ccsb, ccsb[:], "SG")
                wait(g, "sw_ready")
                dma(g, d_swv, swv[:], "SG")
                wait(g, "dump_p1")
                g.wait_ge(SEM["SG"], cnt["SG"])

        for name in ("sync", "vector", "scalar", "tensor", "gpsimd"):
            def mk():
                def f(e):
                    bump(e.nop(), "SBAR", 1)
                return f
            getattr(blk, name)(mk())
        M["bar1"] = ("SBAR", cnt["SBAR"])

        QS = [("gu", l, bq) for l in range(2) for bq in range(4)] + \
             [("dn", d, 0) for d in range(11)]
        for qi, spec in enumerate(QS):
            if spec[0] == "gu":
                _, l, bq = spec
                srcap = qgu_in[128 * l:128 * (l + 1), 2816 * bq:2816 * (bq + 1)]
                npart, wid, swc = 128, 2816, 1
            else:
                _, d, _ = spec
                srcap = qdn_in[64 * d:64 * (d + 1), :]
                npart, wid, swc = 64, 2048, 3

            @blk.sync
            def _(sy, qi=qi, srcap=srcap, npart=npart, wid=wid):
                if qi == 0:
                    wait(sy, "bar1")
                else:
                    wait(sy, f"qs_wq_{qi - 1}")
                dma(sy, q_in[0:npart, 0:wid], srcap, "SX", mark=f"qs_in_{qi}")

            @blk.scalar
            def _(s, qi=qi, npart=npart, wid=wid, swc=swc):
                wait(s, f"qs_in_{qi}")
                if qi == 0:
                    wait(s, "sw_ready")
                s.activation(q_u[0:npart, 0:wid], q_in[0:npart, 0:wid], AF.Identity,
                             scale=swv[0:npart, swc:swc + 1])
                s.drain()
                bump(s.nop(), "SS", 1, f"qs_u_{qi}")

            @blk.vector
            def _(v, qi=qi, npart=npart, wid=wid):
                wait(v, f"qs_u_{qi}")
                v.tensor_scalar(q_c1[0:npart, 0:wid], q_u[0:npart, 0:wid],
                                -0.5, None, op0=ALU.is_le)
                v.drain()
                v.scalar_tensor_tensor(q_wq[0:npart, 0:wid], q_u[0:npart, 0:wid],
                                       0.5, q_c1[0:npart, 0:wid],
                                       op0=ALU.is_ge, op1=ALU.subtract)
                v.drain()
                bump(v.nop(), "SV", 1, f"qs_wq_{qi}")

            @blk.sync
            def _(sy, qi=qi, spec=spec):
                wait(sy, f"qs_wq_{qi}")
                if spec[0] == "gu":
                    _, l, bq = spec
                    off = l * GUH + (11 * bq) * 32768
                    dst = ccg_in[off:off + 11 * 32768].rearrange(
                        "(j p f) -> p j f", p=128, f=256)
                    src = q_wq[:].rearrange("p (j f) -> p j f", f=256)[:, 0:11, :]
                else:
                    _, d, _ = spec
                    off = 64 * d * CD
                    dst = ccd_in[off:off + 64 * CD].rearrange("(p f) -> p f", p=64)
                    src = q_wq[0:64, 0:2048]
                dma(sy, dst, src, "SX", mark=f"qs_done_{qi}")

        @blk.gpsimd
        def _(g):
            wait(g, f"qs_done_{len(QS) - 1}")
            ins = g.collective_compute(
                "AllGather", ALU.bypass,
                replica_groups=[[0, 1, 2, 3, 4, 5, 6, 7]],
                ins=[ccg_in], outs=[ccg_out])
            bump(ins, "SCC", 1, "ag_gu")
            ins = g.collective_compute(
                "AllGather", ALU.bypass,
                replica_groups=[[0, 1, 2, 3, 4, 5, 6, 7]],
                ins=[ccd_in], outs=[ccd_out])
            bump(ins, "SCC", 1, "ag_dn")

        for tl in range(NT2):
            t0 = 512 * tl

            @blk.sync
            def _(sy, tl=tl, t0=t0):
                if tl == 0:
                    wait(sy, "bar1")
                else:
                    wait(sy, f"out_{tl - 1}")
                src = x_in[t0:t0 + 512, :].rearrange("(m p) c -> p m c", p=128)
                dst = a_acc[:].rearrange("p (m c) -> p m c", c=CD)
                dma(sy, dst, src, "SX", mark=f"xa_{tl}")

            for mm in range(NM):
                @blk.vector
                def _(v, tl=tl, mm=mm):
                    wait(v, f"xa_{tl}")
                    xs = a_acc[:, CD * mm:CD * (mm + 1)]
                    v.reduce_sum(col(0), xs, axis=AX.X)
                    v.drain()
                    bump(v.nop(), "SV", 1, f"a_s1_{tl}_{mm}")

                @blk.scalar
                def _(s, tl=tl, mm=mm):
                    wait(s, f"a_s1_{tl}_{mm}")
                    if mm > 0:
                        wait(s, f"a_tr_{tl}_{mm - 1}_3")
                    elif tl > 0:
                        wait(s, f"a_tr_{tl - 1}_{NM - 1}_3")
                    xs = a_acc[:, CD * mm:CD * (mm + 1)]
                    s.activation(a_xn[:], xs, AF.Square)
                    s.drain()
                    bump(s.nop(), "SS", 1, f"a_sq_{tl}_{mm}")

                @blk.vector
                def _(v, tl=tl, mm=mm):
                    wait(v, f"a_sq_{tl}_{mm}")
                    v.reduce_sum(col(4), a_xn[:], axis=AX.X)
                    v.drain()
                    bump(v.nop(), "SV", 1, f"a_s2_{tl}_{mm}")

                @blk.scalar
                def _(s, tl=tl, mm=mm):
                    wait(s, f"a_s2_{tl}_{mm}")
                    s_chain1(s)
                    bump(s.nop(), "SS", 1, f"a_sd_{tl}_{mm}")

                @blk.vector
                def _(v, tl=tl, mm=mm):
                    wait(v, f"a_sd_{tl}_{mm}")
                    v.reciprocal(col(32), col(28))
                    v.drain()
                    bump(v.nop(), "SV", 1, f"a_r0_{tl}_{mm}")

                @blk.scalar
                def _(s, tl=tl, mm=mm):
                    wait(s, f"a_r0_{tl}_{mm}")
                    s_chain2(s, rms=False)
                    xs = a_acc[:, CD * mm:CD * (mm + 1)]
                    s.activation(a_xn[:], xs, AF.Identity, scale=col(48),
                                 bias=col(56))
                    s.drain()
                    bump(s.nop(), "SS", 1, f"a_xn_{tl}_{mm}")

                @blk.vector
                def _(v, tl=tl, mm=mm):
                    wait(v, f"a_xn_{tl}_{mm}")
                    xs = a_acc[:, CD * mm:CD * (mm + 1)]
                    v.tensor_add(xs, xs, bpr_b[:])
                    v.drain()
                    bump(v.nop(), "SV", 1, f"a_bp_{tl}_{mm}")

                for half2 in range(2):
                    @blk.tensor
                    def _(t, tl=tl, mm=mm, half2=half2):
                        if half2 == 0:
                            wait(t, f"a_xn_{tl}_{mm}")
                        for r in (2 * half2, 2 * half2 + 1):
                            prev = None
                            if r >= 2:
                                prev = f"a_trev_{tl}_{mm}_{r - 2}"
                            elif mm > 0:
                                prev = f"a_trev_{tl}_{mm - 1}_{r + 2}"
                            elif tl > 0:
                                prev = f"a_trev_{tl - 1}_{NM - 1}_{r + 2}"
                            if prev is not None:
                                wait(t, prev)
                            ins = None
                            for j4 in range(4):
                                cj = 4 * r + j4
                                ins = t.transpose(
                                    pb[2 + (r % 2)][:, 128 * j4:128 * (j4 + 1)],
                                    a_xn[:, 128 * cj:128 * (cj + 1)], ident32[:])
                            bump(ins, "ST", 1, f"a_tr_{tl}_{mm}_{r}")

                    @blk.scalar
                    def _(s, tl=tl, mm=mm, half2=half2):
                        for r in (2 * half2, 2 * half2 + 1):
                            wait(s, f"a_tr_{tl}_{mm}_{r}")
                            for j4 in range(4):
                                cj = 4 * r + j4
                                s.activation(
                                    a_xnf[:, 512 * cj + 128 * mm:
                                          512 * cj + 128 * (mm + 1)],
                                    pb[2 + (r % 2)][:, 128 * j4:128 * (j4 + 1)],
                                    AF.Identity)
                            s.drain()
                            bump(s.nop(), "SS", 1, f"a_trev_{tl}_{mm}_{r}")

            for oj in range(NCJ):
                @blk.gpsimd
                def _(g, tl=tl, oj=oj):
                    if tl == 0 and oj == 0:
                        wait(g, "bar1")
                    k = NCJ * tl + oj
                    if k >= 2:
                        wait(g, f"qmm_{(k - 2) // NCJ}_{(k - 2) % NCJ}")
                    src = tq[:, 128 * oj:128 * (oj + 1)].rearrange(
                        "(cj p) f -> p cj f", p=128)
                    dst = a_tq[oj % 2][:].rearrange("p (cj f) -> p cj f", f=128)
                    dma(g, dst, src, "SG", mark=f"tq_{tl}_{oj}")

                @blk.tensor
                def _(t, tl=tl, oj=oj):
                    wait(t, f"tq_{tl}_{oj}")
                    wait(t, f"a_trev_{tl}_{NM - 1}_3")
                    if oj >= 2:
                        wait(t, f"qev_{tl}_{oj - 2}")
                    ins = None
                    for cj in range(NCJ):
                        ins = t.matmul(pb[oj % 2][:],
                                       a_tq[oj % 2][:, 128 * cj:128 * (cj + 1)],
                                       a_xnf[:, 512 * cj:512 * (cj + 1)],
                                       start=(cj == 0), stop=(cj == NCJ - 1))
                    bump(ins, "ST", 1, f"qmm_{tl}_{oj}")

                @blk.scalar
                def _(s, tl=tl, oj=oj):
                    wait(s, f"qmm_{tl}_{oj}")
                    s.activation(a_qf[:, 512 * oj:512 * (oj + 1)], pb[oj % 2][:],
                                 AF.Relu)
                    s.drain()
                    bump(s.nop(), "SS", 1, f"qev_{tl}_{oj}")

            @blk.tensor
            def _(t, tl=tl):
                wait(t, f"qev_{tl}_{NCJ - 1}")
                if tl == 0:
                    wait(t, "vkb_ready")
                ins = None
                for cj in range(NCJ):
                    ins = t.matmul(pb[2][0:64, :], ksbig[:, 64 * cj:64 * (cj + 1)],
                                   a_qf[:, 512 * cj:512 * (cj + 1)],
                                   start=(cj == 0), stop=(cj == NCJ - 1))
                bump(ins, "ST", 1, f"den_{tl}")

            @blk.vector
            def _(v, tl=tl):
                wait(v, f"den_{tl}")
                v.tensor_scalar_add(a_dent[:], pb[2][0:64, :], EPS_ATT)
                v.drain()
                v.reciprocal(a_dent2[:], a_dent[:])
                v.drain()
                v.tensor_copy(a_rden[:], a_dent2[:])
                v.drain()
                bump(v.nop(), "SV", 1, f"rden_{tl}")

            for cj in range(NCJ):
                @blk.tensor
                def _(t, tl=tl, cj=cj):
                    if cj == 0:
                        wait(t, f"rden_{tl}")
                        if tl == 0:
                            wait(t, "esel")
                    if cj >= 2:
                        wait(t, f"att_{tl}_{cj - 2}")
                    ins = t.matmul(pb[5 + cj % 2][:],
                                   esel[:, 128 * cj:128 * (cj + 1)], a_rden[:],
                                   start=True, stop=True)
                    bump(ins, "ST", 1, f"bc_{tl}_{cj}")
                    ins = None
                    for i in range(4):
                        ins = t.matmul(
                            pb[3 + cj % 2][32 * i:32 * (i + 1), :],
                            vkb[32 * i:32 * (i + 1), 33 * cj:33 * cj + 32],
                            a_qf[32 * i:32 * (i + 1), 512 * cj:512 * (cj + 1)],
                            start=True, stop=True, tile_position=(32 * i, 32 * i))
                    bump(ins, "ST", 1, f"num_{tl}_{cj}")

                @blk.scalar
                def _(s, tl=tl, cj=cj):
                    wait(s, f"bc_{tl}_{cj}")
                    s.activation(a_bcs[:], pb[5 + cj % 2][:], AF.Identity)
                    s.drain()
                    bump(s.nop(), "SS", 1, f"bcs_{tl}_{cj}")

                @blk.vector
                def _(v, tl=tl, cj=cj):
                    wait(v, f"num_{tl}_{cj}")
                    wait(v, f"bcs_{tl}_{cj}")
                    v.tensor_mul(a_att[:, 512 * cj:512 * (cj + 1)],
                                 pb[3 + cj % 2][:], a_bcs[:])
                    v.drain()
                    bump(v.nop(), "SV", 1, f"att_{tl}_{cj}")

            if dbg and tl == 0:
                @blk.gpsimd
                def _(g):
                    wait(g, "qev_0_15")
                    dma(g, d_qf, a_qf[:], "SG")
                    wait(g, "rden_0")
                    dma(g, d_dent, a_dent[:], "SG")
                    wait(g, "att_0_15")
                    dma(g, d_att, a_att[:], "SG", mark="dump_2a")
                    g.wait_ge(SEM["SG"], cnt["SG"])

            for oc in range(4):
                for half in range(2):
                    @blk.gpsimd
                    def _(g, tl=tl, oc=oc, half=half):
                        k = 8 * tl + 2 * oc + half
                        if k >= 2:
                            kk = k - 2
                            ptl, poc = kk // 8, (kk % 8) // 2
                            wait(g, f"pmm_{ptl}_{poc}_3")
                        cj0 = 8 * half
                        src = tproj[128 * cj0:128 * (cj0 + 8),
                                    512 * oc:512 * (oc + 1)].rearrange(
                            "(cj p) f -> p cj f", p=128)
                        dst = a_tp[half][:].rearrange("p (cj f) -> p cj f", f=512)
                        dma(g, dst, src, "SG", mark=f"tp_{tl}_{oc}_{half}")

                for mm in range(NM):
                    @blk.tensor
                    def _(t, tl=tl, oc=oc, mm=mm):
                        if oc == 0 and mm == 0:
                            wait(t, f"att_{tl}_{NCJ - 1}")
                        idx = 4 * oc + mm
                        if idx >= 2:
                            pidx = idx - 2
                            wait(t, f"pev_{tl}_{pidx // 4}_{pidx % 4}")
                        elif tl > 0:
                            pidx = 14 + idx
                            wait(t, f"pev_{tl - 1}_{pidx // 4}_{pidx % 4}")
                        bank = pb[2 * (oc % 2) + mm % 2]
                        ins = None
                        for half in range(2):
                            wait(t, f"tp_{tl}_{oc}_{half}")
                            for c8 in range(8):
                                cj = 8 * half + c8
                                ins = t.matmul(
                                    bank[:],
                                    a_att[:, 512 * cj + 128 * mm:
                                          512 * cj + 128 * (mm + 1)],
                                    a_tp[half][:, 512 * c8:512 * (c8 + 1)],
                                    start=(cj == 0), stop=(cj == NCJ - 1))
                        bump(ins, "ST", 1, f"pmm_{tl}_{oc}_{mm}")

                    @blk.vector
                    def _(v, tl=tl, oc=oc, mm=mm):
                        wait(v, f"pmm_{tl}_{oc}_{mm}")
                        if oc == 0:
                            wait(v, f"a_bp_{tl}_{mm}")
                        bank = pb[2 * (oc % 2) + mm % 2]
                        dst = a_acc[:, CD * mm + 512 * oc:CD * mm + 512 * (oc + 1)]
                        v.scalar_tensor_tensor(dst, bank[:], 1.0, dst,
                                               op0=ALU.mult, op1=ALU.add)
                        v.drain()
                        bump(v.nop(), "SV", 1, f"pev_{tl}_{oc}_{mm}")

            @blk.sync
            def _(sy, tl=tl, t0=t0):
                for oc in range(4):
                    for mm in range(NM):
                        wait(sy, f"pev_{tl}_{oc}_{mm}")
                if dbg and tl == 0:
                    dma(sy, d_out2a, a_acc[:], "SX")
                dst = out_hbm[t0:t0 + 512, :].rearrange("(m p) c -> p m c", p=128)
                src = a_acc[:].rearrange("p (m c) -> p m c", c=CD)
                dma(sy, dst, src, "SX", mark=f"out_{tl}")

        @blk.sync
        def _(sy):
            wait(sy, f"out_{NT2 - 1}")
            bump(sy.nop(), "SBAR", 1)
        for name in ("vector", "scalar", "tensor", "gpsimd"):
            def mk():
                def f(e):
                    wait(e, f"out_{NT2 - 1}")
                    bump(e.nop(), "SBAR", 1)
                return f
            getattr(blk, name)(mk())
        M["bar2"] = ("SBAR", cnt["SBAR"])

        for tl in range(NT2):
            t0 = 512 * tl
            for mm in range(NM):
                @blk.sync
                def _(sy, tl=tl, mm=mm, t0=t0):
                    if tl == 0 and mm == 0:
                        wait(sy, "bar2")
                    k = NM * tl + mm
                    if k >= 2:
                        wait(sy, f"b_xnr_{(k - 2) // NM}_{(k - 2) % NM}")
                    dma(sy, b_x[mm % 2][:],
                        x_in[t0 + 128 * mm:t0 + 128 * (mm + 1), :], "SX",
                        mark=f"bx_{tl}_{mm}")

                @blk.vector
                def _(v, tl=tl, mm=mm):
                    wait(v, f"bx_{tl}_{mm}")
                    v.reduce_sum(col(0), b_x[mm % 2][:], axis=AX.X)
                    v.drain()
                    bump(v.nop(), "SV", 1, f"b_s1_{tl}_{mm}")

                @blk.scalar
                def _(s, tl=tl, mm=mm):
                    wait(s, f"b_s1_{tl}_{mm}")
                    if mm > 0:
                        wait(s, f"b_tr_{tl}_{mm - 1}_3")
                    elif tl > 0:
                        wait(s, f"b_tr_{tl - 1}_{NM - 1}_3")
                        wait(s, f"facc_{tl - 1}_1_1")
                    s.activation(b_xn[:], b_x[mm % 2][:], AF.Square)
                    s.drain()
                    bump(s.nop(), "SS", 1, f"b_sq_{tl}_{mm}")

                @blk.vector
                def _(v, tl=tl, mm=mm):
                    wait(v, f"b_sq_{tl}_{mm}")
                    v.reduce_sum(col(4), b_xn[:], axis=AX.X)
                    v.drain()
                    bump(v.nop(), "SV", 1, f"b_s2_{tl}_{mm}")

                @blk.scalar
                def _(s, tl=tl, mm=mm):
                    wait(s, f"b_s2_{tl}_{mm}")
                    s_chain1(s)
                    bump(s.nop(), "SS", 1, f"b_sd_{tl}_{mm}")

                @blk.vector
                def _(v, tl=tl, mm=mm):
                    wait(v, f"b_sd_{tl}_{mm}")
                    v.reciprocal(col(32), col(28))
                    v.drain()
                    bump(v.nop(), "SV", 1, f"b_r0_{tl}_{mm}")

                @blk.scalar
                def _(s, tl=tl, mm=mm):
                    wait(s, f"b_r0_{tl}_{mm}")
                    s_chain2(s, rms=True)
                    bump(s.nop(), "SS", 1, f"b_sd2_{tl}_{mm}")

                @blk.vector
                def _(v, tl=tl, mm=mm):
                    wait(v, f"b_sd2_{tl}_{mm}")
                    v.reciprocal(col(76), col(72))
                    v.drain()
                    bump(v.nop(), "SV", 1, f"b_r02_{tl}_{mm}")

                @blk.scalar
                def _(s, tl=tl, mm=mm):
                    wait(s, f"b_r02_{tl}_{mm}")
                    s_chain3(s)
                    s.activation(b_xn[:], b_x[mm % 2][:], AF.Identity,
                                 scale=col(48), bias=col(56))
                    s.drain()
                    bump(s.nop(), "SS", 1, f"b_xnr_{tl}_{mm}")

                @blk.vector
                def _(v, tl=tl, mm=mm):
                    wait(v, f"b_xnr_{tl}_{mm}")
                    if mm == 0 and tl > 0:
                        wait(v, f"facc_{tl - 1}_1_1")
                    v.tensor_mul(b_gt[:], b_xn[:], gam_b[:])
                    v.drain()
                    v.reduce_max(col(96), b_gt[:], axis=AX.X,
                                 apply_absolute_value=True)
                    v.drain()
                    v.tensor_scalar(col(100), col(96), col(92), 1e-5,
                                    op0=ALU.mult, op1=ALU.max)
                    v.drain()
                    v.reciprocal(col(120), col(100))
                    v.drain()
                    v.scalar_tensor_tensor(col(104), col(120), 127.0, col(92),
                                           op0=ALU.mult, op1=ALU.mult)
                    v.drain()
                    v.tensor_scalar(col(108), col(100), swv[:, 0:1], 1.0 / 127.0,
                                    op0=ALU.mult, op1=ALU.mult)
                    v.drain()
                    v.tensor_mul(col(112), col(108), col(108))
                    v.drain()
                    v.tensor_mul(col(116), col(112), col(108))
                    v.drain()
                    v.tensor_copy(lnv[:, 124 + mm:125 + mm], col(116))
                    v.drain()
                    bump(v.nop(), "SV", 1, f"b_qv_{tl}_{mm}")

                @blk.scalar
                def _(s, tl=tl, mm=mm):
                    wait(s, f"b_qv_{tl}_{mm}")
                    s.activation(b_u[:], b_gt[:], AF.Identity, scale=col(104),
                                 bias=cvec[:, 2:3])
                    s.drain()
                    bump(s.nop(), "SS", 1, f"b_u_{tl}_{mm}")

                @blk.vector
                def _(v, tl=tl, mm=mm):
                    wait(v, f"b_u_{tl}_{mm}")
                    v.tensor_copy(b_i1[:], b_u[:])
                    v.drain()
                    bump(v.nop(), "SV", 1, f"b_i1_{tl}_{mm}")

                @blk.vector
                def _(v, tl=tl, mm=mm):
                    wait(v, f"b_i1_{tl}_{mm}")
                    v.scalar_tensor_tensor(b_c1[:], b_i1[:], 0.5, b_u[:],
                                           op0=ALU.add, op1=ALU.is_le)
                    v.drain()
                    bump(v.nop(), "SV", 1, f"b_c1_{tl}_{mm}")

                @blk.vector
                def _(v, tl=tl, mm=mm):
                    wait(v, f"b_c1_{tl}_{mm}")
                    if mm > 0:
                        wait(v, f"b_tr_{tl}_{mm - 1}_3")
                    elif tl > 0:
                        wait(v, f"b_tr_{tl - 1}_{NM - 1}_3")
                    v.scalar_tensor_tensor(b_xq32[:], b_i1[:], -128.0, b_c1[:],
                                           op0=ALU.add, op1=ALU.add)
                    v.drain()
                    bump(v.nop(), "SV", 1, f"b_xq_{tl}_{mm}")

                for half2 in range(2):
                    @blk.tensor
                    def _(t, tl=tl, mm=mm, half2=half2):
                        if half2 == 0:
                            wait(t, f"b_xq_{tl}_{mm}")
                            if mm == 0 and tl > 0:
                                wait(t, f"dev_{tl - 1}_1_7")
                        for r in (2 * half2, 2 * half2 + 1):
                            prev = None
                            if r >= 2:
                                prev = f"b_trev_{tl}_{mm}_{r - 2}"
                            elif mm > 0:
                                prev = f"b_trev_{tl}_{mm - 1}_{r + 2}"
                            elif tl > 0:
                                prev = f"b_trev_{tl - 1}_{NM - 1}_{r + 2}"
                            if prev is not None:
                                wait(t, prev)
                            ins = None
                            for j4 in range(4):
                                cj = 4 * r + j4
                                ins = t.transpose(
                                    pb[2 + (r % 2)][:, 128 * j4:128 * (j4 + 1)],
                                    b_xq32[:, 128 * cj:128 * (cj + 1)], ident32[:])
                            bump(ins, "ST", 1, f"b_tr_{tl}_{mm}_{r}")

                    @blk.scalar
                    def _(s, tl=tl, mm=mm, half2=half2):
                        for r in (2 * half2, 2 * half2 + 1):
                            wait(s, f"b_tr_{tl}_{mm}_{r}")
                            for j4 in range(4):
                                cj = 4 * r + j4
                                s.activation(
                                    b_xqf[:, 512 * cj + 128 * mm:
                                          512 * cj + 128 * (mm + 1)],
                                    pb[2 + (r % 2)][:, 128 * j4:128 * (j4 + 1)],
                                    AF.Identity)
                            s.drain()
                            bump(s.nop(), "SS", 1, f"b_trev_{tl}_{mm}_{r}")

            @blk.vector
            def _(v, tl=tl):
                if tl > 0:
                    wait(v, f"hq_{tl - 1}_{NFC - 1}")
                v.memset(b_mp[:], 0.0)
                v.drain()
                bump(v.nop(), "SV", 1, f"mpinit_{tl}")

            for pi in range(NFB2):
                if dbg and tl == 0 and pi == 1:
                    @blk.gpsimd
                    def _(g):
                        wait(g, "wg_0_0_1")
                        dma(g, d_wg, b_wg[0][:], "SG", mark="dump_wg")
                        g.wait_ge(SEM["SG"], cnt["SG"])

                @blk.sync
                def _(sy, tl=tl, pi=pi):
                    if tl == 0 and pi == 0:
                        wait(sy, "ag_gu")
                    if dbg and tl == 0 and pi == 2:
                        wait(sy, "dump_wg")
                    k = NFB2 * tl + pi
                    if k >= 2:
                        kk = k - 2
                        wait(sy, f"gu_done_{kk // NFB2}_{kk % NFB2}")
                    slot = b_wg[pi % 2]
                    for h in range(2):
                        dst = slot[:, 2048 * h:2048 * (h + 1)].rearrange(
                            "p (cj f) -> p cj f", f=256)
                        dma(sy, dst, wgu_view[pi][:, 8 * h:8 * (h + 1), :],
                            "SX", mark=f"wg_{tl}_{pi}_0" if h == 1 else None)

                @blk.gpsimd
                def _(g, tl=tl, pi=pi):
                    if tl == 0 and pi == 0:
                        wait(g, "ag_gu")
                    k = NFB2 * tl + pi
                    if k >= 2:
                        kk = k - 2
                        wait(g, f"gu_done_{kk // NFB2}_{kk % NFB2}")
                    slot = b_wg[pi % 2]
                    for h in range(2):
                        dst = slot[:, 4096 + 2048 * h:
                                   4096 + 2048 * (h + 1)].rearrange(
                            "p (cj f) -> p cj f", f=256)
                        dma(g, dst, wgu_view[22 + pi][:, 8 * h:8 * (h + 1), :],
                            "SG", mark=f"wg_{tl}_{pi}_1" if h == 1 else None)

                @blk.tensor
                def _(t, tl=tl, pi=pi):
                    wait(t, f"wg_{tl}_{pi}_0")
                    wait(t, f"wg_{tl}_{pi}_1")
                    wait(t, f"b_trev_{tl}_{NM - 1}_3")
                    if pi >= 2:
                        wait(t, f"hid_{tl}_{pi - 2}")
                    elif tl > 0:
                        wait(t, f"dev_{tl - 1}_1_7")
                    slot = b_wg[pi % 2]
                    bk = 4 * (pi % 2)
                    for part in range(4):
                        gi, fh = part // 2, part % 2
                        ins = None
                        for cj in range(NCJ):
                            lhs = slot[:, 4096 * gi + 256 * cj + 128 * fh:
                                       4096 * gi + 256 * cj + 128 * (fh + 1)]
                            ins = t.matmul(pb[bk + part][:], lhs,
                                           b_xqf[:, 512 * cj:512 * (cj + 1)],
                                           start=(cj == 0), stop=(cj == NCJ - 1))
                        bump(ins, "ST", 1, f"gumm_{tl}_{pi}_{part}")

                for fh in range(2):
                    @blk.scalar
                    def _(s, tl=tl, pi=pi, fh=fh):
                        bk = 4 * (pi % 2)
                        wait(s, f"gumm_{tl}_{pi}_{fh}")
                        if (pi, fh) != (0, 0):
                            pm = (tl, pi, 0) if fh == 1 else (tl, pi - 1, 1)
                            wait(s, f"hch_{pm[0]}_{pm[1]}_{pm[2]}")
                        elif tl > 0:
                            wait(s, f"hch_{tl - 1}_{NFB2 - 1}_1")
                        s.activation(b_h1[:], pb[bk + fh][:], AF.Relu)
                        s.drain()
                        s.activation(b_h2[:], b_h1[:], AF.Square)
                        s.drain()
                        bump(s.nop(), "SS", 1, f"h2_{tl}_{pi}_{fh}")

                    @blk.vector
                    def _(v, tl=tl, pi=pi, fh=fh):
                        bk = 4 * (pi % 2)
                        wait(v, f"h2_{tl}_{pi}_{fh}")
                        wait(v, f"gumm_{tl}_{pi}_{2 + fh}")
                        fc = 2 * pi + fh
                        v.tensor_mul(b_hid[:, 512 * fc:512 * (fc + 1)],
                                     b_h2[:], pb[bk + 2 + fh][:])
                        v.drain()
                        bump(v.nop(), "SV", 1, f"hch_{tl}_{pi}_{fh}")

                @blk.vector
                def _(v, tl=tl, pi=pi):
                    bump(v.nop(), "SV", 1, f"hid_{tl}_{pi}")

                for fh in range(2):
                    @blk.scalar
                    def _(s, tl=tl, pi=pi, fh=fh):
                        wait(s, f"hch_{tl}_{pi}_{fh}")
                        if (pi, fh) != (0, 0):
                            pfh = (tl, pi, 0) if fh == 1 else (tl, pi - 1, 1)
                            wait(s, f"hmax_{pfh[0]}_{pfh[1]}_{pfh[2]}")
                        elif tl > 0:
                            wait(s, f"hmax_{tl - 1}_{NFB2 - 1}_1")
                        fc = 2 * pi + fh
                        s.activation(b_mn[:], b_hid[:, 512 * fc:512 * (fc + 1)],
                                     AF.Abs)
                        s.drain()
                        bump(s.nop(), "SS", 1, f"habs_{tl}_{pi}_{fh}")

                    @blk.vector
                    def _(v, tl=tl, pi=pi, fh=fh):
                        if pi == 0 and fh == 0:
                            wait(v, f"mpinit_{tl}")
                        wait(v, f"habs_{tl}_{pi}_{fh}")
                        v.tensor_max(b_mp[:], b_mp[:], b_mn[:])
                        v.drain()
                        bump(v.nop(), "SV", 1, f"hmax_{tl}_{pi}_{fh}")

                @blk.vector
                def _(v, tl=tl, pi=pi):
                    bump(v.nop(), "SV", 1, f"mm_{tl}_{pi}")
                    bump(v.nop(), "SV", 1, f"gu_done_{tl}_{pi}")

            @blk.vector
            def _(v, tl=tl):
                wait(v, f"mm_{tl}_{NFB2 - 1}")
                bump(v.nop(), "SV", 1, f"mabs_{tl}")

            @blk.tensor
            def _(t, tl=tl):
                wait(t, f"mabs_{tl}")
                wait(t, f"gumm_{tl}_{NFB2 - 1}_3")
                ins = None
                for bq in range(4):
                    ins = t.transpose(pb[4][:, 128 * bq:128 * (bq + 1)],
                                      b_mp[:, 128 * bq:128 * (bq + 1)], ident32[:])
                bump(ins, "ST", 1, f"mtr_{tl}")

            @blk.vector
            def _(v, tl=tl):
                wait(v, f"mtr_{tl}")
                v.reduce_max(lnv[:, 96:100], pb[4][:].rearrange(
                    "p (b c) -> p b c", c=128), axis=AX.X)
                v.drain()
                v.tensor_mul(lnv[:, 100:104], lnv[:, 96:100], lnv[:, 124:128])
                v.drain()
                v.tensor_scalar_max(lnv[:, 100:104], lnv[:, 100:104], 1e-5)
                v.drain()
                v.reciprocal(lnv[:, 104:108], lnv[:, 100:104])
                v.drain()
                v.scalar_tensor_tensor(lnv[:, 108:112], lnv[:, 104:108], 127.0,
                                       lnv[:, 124:128], op0=ALU.mult, op1=ALU.mult)
                v.drain()
                v.tensor_scalar(osq[:], lnv[:, 100:104], swv[:, 2:3],
                                1.0 / 127.0, op0=ALU.mult, op1=ALU.mult)
                v.drain()
                v.tensor_copy(sh128[:].rearrange("p (b c) -> p b c", c=32)[
                    :, :, 0:1].rearrange("p b c -> p (b c)"), lnv[:, 108:112])
                v.drain()
                bump(v.nop(), "SV", 1, f"shv_{tl}")

            @blk.tensor
            def _(t, tl=tl):
                wait(t, f"shv_{tl}")
                ins = t.transpose(pb[5][:, 0:128], sh128[:], ident32[:])
                bump(ins, "ST", 1, f"shtr_{tl}")

            @blk.vector
            def _(v, tl=tl):
                wait(v, f"shtr_{tl}")
                for bq in range(4):
                    v.tensor_copy(szpad[0:1, 128 * bq:128 * (bq + 1)],
                                  pb[5][32 * bq:32 * bq + 1, 0:128])
                v.drain()
                bump(v.nop(), "SV", 1, f"szp_{tl}")

            if dbg and tl == 0:
                @blk.gpsimd
                def _(g):
                    wait(g, "b_trev_0_3_3")
                    dma(g, d_xqf, b_xqf[:], "SG")
                    wait(g, "mm_0_21")
                    dma(g, d_hid, b_hid[:], "SG", mark="dump_hid")
                    g.wait_ge(SEM["SG"], cnt["SG"])

            @blk.tensor
            def _(t, tl=tl):
                wait(t, f"szp_{tl}")
                if tl == 0:
                    wait(t, "v_init")
                    if dbg:
                        wait(t, "dump_hid")
                ins = t.matmul(pb[6][:], ones128[:], szpad[:], start=True, stop=True)
                bump(ins, "ST", 1, f"bcq_{tl}")

            for fc in range(NFC):
                @blk.vector
                def _(v, tl=tl, fc=fc):
                    if fc == 0:
                        wait(v, f"bcq_{tl}")
                    hc = b_hid[:, 512 * fc:512 * (fc + 1)]
                    v.tensor_mul(b_h1[:], hc, pb[6][:])
                    v.drain()
                    bump(v.nop(), "SV", 1, f"hb_{tl}_{fc}")

                @blk.scalar
                def _(s, tl=tl, fc=fc):
                    wait(s, f"hb_{tl}_{fc}")
                    s.activation(b_h2[:], b_h1[:], AF.Identity, bias=cvec[:, 2:3])
                    s.drain()
                    bump(s.nop(), "SS", 1, f"hu_{tl}_{fc}")

                @blk.vector
                def _(v, tl=tl, fc=fc):
                    wait(v, f"hu_{tl}_{fc}")
                    v.tensor_copy(hq_i1[:], b_h2[:])
                    v.drain()
                    bump(v.nop(), "SV", 1, f"hi1_{tl}_{fc}")

                @blk.vector
                def _(v, tl=tl, fc=fc):
                    wait(v, f"hi1_{tl}_{fc}")
                    v.scalar_tensor_tensor(hq_c1[:], hq_i1[:], 0.5, b_h2[:],
                                           op0=ALU.add, op1=ALU.is_le)
                    v.drain()
                    bump(v.nop(), "SV", 1, f"hc1_{tl}_{fc}")

                @blk.vector
                def _(v, tl=tl, fc=fc):
                    wait(v, f"hc1_{tl}_{fc}")
                    hc = b_hid[:, 512 * fc:512 * (fc + 1)]
                    v.scalar_tensor_tensor(hc, hq_i1[:], -128.0, hq_c1[:],
                                           op0=ALU.add, op1=ALU.add)
                    v.drain()
                    bump(v.nop(), "SV", 1, f"hq_{tl}_{fc}")

            if dbg and tl == 0:
                @blk.gpsimd
                def _(g):
                    wait(g, "shv_0")
                    dma(g, d_lnv, lnv[:], "SG")
                    dma(g, d_osq, osq[:], "SG")
                    g.wait_ge(SEM["SG"], cnt["SG"])

            for mp in range(2):
                for fc in range(NFC):
                    def wd_body(e, tl, mp, fc, key):
                        if tl == 0 and mp == 0 and fc <= 1:
                            wait(e, "ag_dn")
                        k = (2 * tl + mp) * NFC + fc
                        if k >= 2:
                            kk = k - 2
                            wait(e, f"dmm_{kk // (2 * NFC)}_"
                                    f"{(kk % (2 * NFC)) // NFC}_{kk % NFC}")
                        for h in range(2):
                            dma(e, b_wd[fc % 2][:, 1024 * h:1024 * (h + 1)],
                                wdn_view[128 * fc:128 * (fc + 1),
                                         1024 * h:1024 * (h + 1)], key,
                                mark=f"wd_{tl}_{mp}_{fc}" if h == 1 else None)

                    if fc % 2 == 0:
                        @blk.sync
                        def _(sy, tl=tl, mp=mp, fc=fc):
                            wd_body(sy, tl, mp, fc, "SX")
                    else:
                        @blk.gpsimd
                        def _(g, tl=tl, mp=mp, fc=fc):
                            wd_body(g, tl, mp, fc, "SG")

                    @blk.tensor
                    def _(t, tl=tl, mp=mp, fc=fc):
                        wait(t, f"wd_{tl}_{mp}_{fc}")
                        if fc == 0:
                            wait(t, f"hq_{tl}_{NFC - 1}")
                            if (tl, mp) != (0, 0):
                                ptl, pmp = (tl, 0) if mp == 1 else (tl - 1, 1)
                                wait(t, f"dev_{ptl}_{pmp}_7")
                        ins = None
                        for mi in range(2):
                            mmx = 2 * mp + mi
                            lhs = b_hid[:, 512 * fc + 128 * mmx:
                                        512 * fc + 128 * (mmx + 1)]
                            for oc in range(4):
                                ins = t.matmul(pb[4 * mi + oc][:], lhs,
                                               b_wd[fc % 2][:, 512 * oc:512 * (oc + 1)],
                                               start=(fc == 0), stop=(fc == NFC - 1))
                        bump(ins, "ST", 1, f"dmm_{tl}_{mp}_{fc}")

                for mi in range(2):
                    @blk.scalar
                    def _(s, tl=tl, mp=mp, mi=mi):
                        wait(s, f"dmm_{tl}_{mp}_{NFC - 1}")
                        mmx = 2 * mp + mi
                        for oc in range(4):
                            s.activation(b_fs[mi][:, 512 * oc:512 * (oc + 1)],
                                         pb[4 * mi + oc][:], AF.Identity,
                                         scale=osq[:, mmx:mmx + 1])
                            s.drain()
                            bump(s.nop(), "SS", 1, f"dev_{tl}_{mp}_{4 * mi + oc}")
                        bump(s.nop(), "SS", 1, f"ffn_{tl}_{mp}_{mi}")

                    @blk.gpsimd
                    def _(g, tl=tl, mp=mp, mi=mi, t0=t0):
                        wait(g, f"ffn_{tl}_{mp}_{mi}")
                        wait(g, f"out_{tl}")
                        mmx = 2 * mp + mi
                        dma(g, out_hbm[t0 + 128 * mmx:t0 + 128 * (mmx + 1), :],
                            b_fs[mi][:], "SG", accum_op=ALU.add,
                            mark=f"facc_{tl}_{mp}_{mi}")

        @blk.sync
        def _(sy):
            sy.wait_ge(SEM["SX"], cnt["SX"])
            sy.wait_ge(SEM["SG"], cnt["SG"])

        @blk.gpsimd
        def _(g):
            g.wait_ge(SEM["SG"], cnt["SG"])

    return nc, es


_CACHE = {}


def kernel(L0_attn_x, L0_attn_w_qkv, L0_attn_w_proj, L0_attn_b_proj,
           L0_ffn_w_gate_up, L0_ffn_w_down, L0_ffn_norm_weight):
    x = np.ascontiguousarray(np.asarray(L0_attn_x, dtype=np.float32))
    w_qkv = np.asarray(L0_attn_w_qkv, dtype=np.float32)
    w_proj = np.asarray(L0_attn_w_proj, dtype=np.float32)
    b_proj = np.asarray(L0_attn_b_proj, dtype=np.float32)
    w_gu = np.asarray(L0_ffn_w_gate_up, dtype=np.float32)
    w_dn = np.asarray(L0_ffn_w_down, dtype=np.float32)
    gam = np.asarray(L0_ffn_norm_weight, dtype=np.float32)

    tkv_h = np.ascontiguousarray(w_qkv[CD:3 * CD, :].T)
    tq_h = np.ascontiguousarray(w_qkv[0:CD, :].T)
    tproj_h = np.ascontiguousarray(w_proj.T)
    tgu_h = np.ascontiguousarray(w_gu.T)
    tdown_h = np.ascontiguousarray(w_dn.T)
    gam_b_h = np.ascontiguousarray(np.broadcast_to(gam[None, :], (128, CD)))
    bpr_b_h = np.ascontiguousarray(np.broadcast_to(b_proj[None, :], (128, CD)))
    id_h = np.eye(128, dtype=np.float32)
    es_h = np.zeros((64, NCJ * 128), dtype=np.float32)
    for cj in range(NCJ):
        for i in range(4):
            es_h[4 * cj + i, 128 * cj + 32 * i:128 * cj + 32 * (i + 1)] = 1.0

    in_maps = []
    for c in range(8):
        b, s0, q = c // 4, (c % 4) * TOKC, c % 4
        in_maps.append({
            "x_in": np.ascontiguousarray(x[b, s0:s0 + TOKC, :]),
            "tkv": tkv_h, "tq": tq_h, "tproj": tproj_h,
            "agu_in": np.ascontiguousarray(tgu_h[512 * q:512 * (q + 1), :]),
            "adn_in": np.ascontiguousarray(tdown_h[1408 * q:1408 * (q + 1), :]),
            "qgu_in": np.ascontiguousarray(tgu_h[256 * c:256 * (c + 1), :]),
            "qdn_in": np.ascontiguousarray(tdown_h[704 * c:704 * (c + 1), :]),
            "gam_in": gam_b_h, "bpr_in": bpr_b_h,
            "id_in": id_h, "es_in": es_h,
        })

    if "nc" not in _CACHE:
        _CACHE["nc"] = build_program()
    nc, es = _CACHE["nc"]

    res = run_bass_kernel_spmd(nc, in_maps, core_ids=list(range(8)))
    out = np.empty((B, NSEQ, CD), dtype=np.float32)
    for c in range(8):
        b, s0 = c // 4, (c % 4) * TOKC
        out[b, s0:s0 + TOKC, :] = res.results[c]["out_hbm"]
    return out


# version 07
# speedup vs baseline: 1.0310x; 1.0310x over previous
import numpy as np
import ml_dtypes
import concourse.bass as bass
import concourse.mybir as mybir
from concourse.bass_utils import run_bass_kernel_spmd

F32 = mybir.dt.float32
BF16 = mybir.dt.bfloat16
I32 = mybir.dt.int32
AF = mybir.ActivationFunctionType
ALU = mybir.AluOpType
AX = mybir.AxisListType

B, NSEQ, CD = 2, 8192, 2048
TOKC = 2048
FF = 5632
NT1 = 16
NT2 = 4
NM = 4
NCJ = 16
NFB2 = 22
NFC = 44
EPS_LN = 1e-5
EPS_RMS = 1e-5
EPS_ATT = 1e-15
GU_CNT = float(2 * FF * CD)
DN_CNT = float(CD * FF)
GUH = 44 * 128 * 256
N_ABS_GU = 32
N_ABS_DN = 16
LNC = 1.0 / CD


def build_program(dbg=False):
    nc = bass.Bass("TRN2", target_bir_lowering=False, detect_race_conditions=False)

    x_in = nc.dram_tensor("x_in", [TOKC, CD], F32, kind="ExternalInput").ap()
    tkv = nc.dram_tensor("tkv", [CD, 2 * CD], F32, kind="ExternalInput").ap()
    tq = nc.dram_tensor("tq", [CD, CD], F32, kind="ExternalInput").ap()
    tproj = nc.dram_tensor("tproj", [CD, CD], F32, kind="ExternalInput").ap()
    agu_in = nc.dram_tensor("agu_in", [512, 2 * FF], F32, kind="ExternalInput").ap()
    adn_in = nc.dram_tensor("adn_in", [1408, CD], F32, kind="ExternalInput").ap()
    qgu_in = nc.dram_tensor("qgu_in", [256, 2 * FF], F32, kind="ExternalInput").ap()
    qdn_in = nc.dram_tensor("qdn_in", [704, CD], F32, kind="ExternalInput").ap()
    gam_in = nc.dram_tensor("gam_in", [128, CD], F32, kind="ExternalInput").ap()
    bpr_in = nc.dram_tensor("bpr_in", [128, CD], F32, kind="ExternalInput").ap()
    id_in = nc.dram_tensor("id_in", [128, 128], F32, kind="ExternalInput").ap()
    es_in = nc.dram_tensor("es_in", [64, NCJ * 128], F32, kind="ExternalInput").ap()
    out_hbm = nc.dram_tensor("out_hbm", [TOKC, CD], F32, kind="ExternalOutput").ap()
    if dbg:
        d_xn = nc.dram_tensor("d_xn", [128, CD], F32, kind="ExternalOutput").ap()
        d_ktm = nc.dram_tensor("d_ktm", [128, CD], F32, kind="ExternalOutput").ap()
        d_vtm = nc.dram_tensor("d_vtm", [128, 64 * 33], F32, kind="ExternalOutput").ap()
        d_ccsb = nc.dram_tensor("d_ccsb", [128, 530], F32, kind="ExternalOutput").ap()
        d_swv = nc.dram_tensor("d_swv", [128, 4], F32, kind="ExternalOutput").ap()
        d_qf = nc.dram_tensor("d_qf", [128, NCJ * 512], F32, kind="ExternalOutput").ap()
        d_dent = nc.dram_tensor("d_dent", [64, 512], F32, kind="ExternalOutput").ap()
        d_att = nc.dram_tensor("d_att", [128, NCJ * 512], F32, kind="ExternalOutput").ap()
        d_xqf = nc.dram_tensor("d_xqf", [128, NCJ * 512], F32, kind="ExternalOutput").ap()
        d_hid = nc.dram_tensor("d_hid", [128, NFC * 512], F32, kind="ExternalOutput").ap()
        d_lnv = nc.dram_tensor("d_lnv", [128, 128], F32, kind="ExternalOutput").ap()
        d_wg = nc.dram_tensor("d_wg", [128, 8192], F32, kind="ExternalOutput").ap()
        d_out2a = nc.dram_tensor("d_out2a", [128, NM * CD], F32, kind="ExternalOutput").ap()
        d_osq = nc.dram_tensor("d_osq", [128, 4], F32, kind="ExternalOutput").ap()

    cc1_in = nc.dram_tensor("cc1_in", [128, 530], F32, kind="Internal").ap()
    cc1_out = nc.dram_tensor("cc1_out", [128, 530], F32, kind="Internal").ap()
    ccg_in = nc.dram_tensor("ccg_in", [2 * GUH], BF16, kind="Internal").ap()
    ccg_out = nc.dram_tensor("ccg_out", [16 * GUH], BF16, kind="Internal").ap()
    ccd_in = nc.dram_tensor("ccd_in", [704 * CD], BF16, kind="Internal").ap()
    ccd_out = nc.dram_tensor("ccd_out", [FF * CD], BF16, kind="Internal").ap()

    wgu_view = ccg_out.rearrange("(cj b p f) -> b p cj f", cj=16, b=44, p=128)
    wdn_view = ccd_out.rearrange("(f c) -> f c", c=CD)

    ident32 = nc.alloc_sbuf_tensor("ident32", [128, 128], F32)
    ones128 = nc.alloc_sbuf_tensor("ones128", [128, 128], F32)
    onescol = nc.alloc_sbuf_tensor("onescol", [128, 1], F32)
    esel = nc.alloc_sbuf_tensor("esel", [64, NCJ * 128], BF16)
    vkb = nc.alloc_sbuf_tensor("vkb", [128, 528], BF16)
    ksbig = nc.alloc_sbuf_tensor("ksbig", [128, NCJ * 64], BF16)
    ccsb = nc.alloc_sbuf_tensor("ccsb", [128, 530], F32)
    absred = nc.alloc_sbuf_tensor("absred", [128, 64], F32)
    swv = nc.alloc_sbuf_tensor("swv", [128, 4], F32)
    zpad2 = nc.alloc_sbuf_tensor("zpad2", [128, 2], F32)
    lnv = nc.alloc_sbuf_tensor("lnv", [128, 128], F32)
    gam_b = nc.alloc_sbuf_tensor("gam_b", [128, CD], F32)
    bpr_b = nc.alloc_sbuf_tensor("bpr_b", [128, CD], F32)
    szpad = nc.alloc_sbuf_tensor("szpad", [128, 512], F32)
    osq = nc.alloc_sbuf_tensor("osq", [128, 4], F32)
    cvec = nc.alloc_sbuf_tensor("cvec", [128, 4], F32)
    sh128 = nc.alloc_sbuf_tensor("sh128", [128, 128], F32)

    ARN = 179968
    arena = nc.alloc_sbuf_tensor("arena", [128, ARN], mybir.dt.uint8)
    AB = nc.lookup_mloc(arena).addr

    def at(name, shape, dtype, off):
        itb = {F32: 4, BF16: 2, I32: 4}[dtype]
        need = int(np.prod(shape[1:])) * itb
        assert off % 64 == 0 and off + need <= ARN, (name, off, need)
        return nc.alloc_sbuf_tensor_at(name, shape, dtype, offset=AB + off)

    p_tkv = at("p_tkv", [128, NCJ * 4096], BF16, 0)
    p_x = [at(f"p_x{i}", [128, CD], F32, 131072 + 8192 * i) for i in range(2)]
    p_xn = at("p_xn", [128, CD], F32, 147456)
    p_xnf = at("p_xnf", [128, CD], BF16, 155648)
    p_ktm = at("p_ktm", [128, CD], BF16, 159744)
    p_vtm = at("p_vtm", [128, 64 * 33], BF16, 163840)
    p_abs = [at(f"p_abs{i}", [128, 1408], F32, 168128 + 5632 * i) for i in range(2)]

    a_acc = at("a_acc", [128, NM * CD], F32, 0)
    a_xnf = at("a_xnf", [128, NM * CD], BF16, 32768)
    a_qf = at("a_qf", [128, NCJ * 512], BF16, 49152)
    a_att = at("a_att", [128, NCJ * 512], BF16, 65536)
    a_xn = at("a_xn", [128, CD], F32, 81920)
    a_tq = [at(f"a_tq{i}", [128, CD], BF16, 90112 + 4096 * i) for i in range(2)]
    a_tp = [at(f"a_tp{i}", [128, 8 * 512], BF16, 98304 + 8192 * i) for i in range(2)]
    a_dent = at("a_dent", [64, 512], F32, 114688)
    a_dent2 = at("a_dent2", [64, 512], F32, 116736)
    a_rden = at("a_rden", [64, 512], BF16, 118784)
    a_bcs = at("a_bcs", [128, 512], BF16, 119808)
    q_in = at("q_in", [128, 2816], F32, 120832)
    q_u = at("q_u", [128, 2816], F32, 132096)
    q_i1 = at("q_i1", [128, 2816], I32, 143360)
    q_c1 = at("q_c1", [128, 2816], F32, 154624)
    q_r = at("q_r", [128, 2816], F32, 165888)
    q_wq = at("q_wq", [128, 2816], BF16, 120832)

    b_hid = at("b_hid", [128, NFC * 512], BF16, 0)
    b_xqf = at("b_xqf", [128, NCJ * 512], BF16, 45056)
    b_x = [at(f"b_x{i}", [128, CD], F32, 61440 + 8192 * i) for i in range(2)]
    b_xn = at("b_xn", [128, CD], F32, 77824)
    b_gt = at("b_gt", [128, CD], F32, 86016)
    b_u = at("b_u", [128, CD], F32, 94208)
    b_i1 = at("b_i1", [128, CD], I32, 102400)
    b_c1 = at("b_c1", [128, CD], F32, 110592)
    b_xq32 = at("b_xq32", [128, CD], F32, 118784)
    b_wg = [at(f"b_wg{i}", [128, 2 * 4096], BF16, 126976 + 16384 * i) for i in range(2)]
    b_wd = [at(f"b_wd{i}", [128, CD], BF16, 159744 + 4096 * i) for i in range(2)]
    b_h1 = at("b_h1", [128, 512], F32, 167936)
    b_h2 = at("b_h2", [128, 512], F32, 169984)
    b_mp = at("b_mp", [128, 512], F32, 172032)
    b_mn = at("b_mn", [128, 512], F32, 174080)
    hq_i1 = at("hq_i1", [128, 512], I32, 172032)
    hq_c1 = at("hq_c1", [128, 512], F32, 174080)
    b_fs = [b_xn, b_gt]

    pb = [nc.alloc_psum_tensor(f"pb{i}", [128, 512], F32) for i in range(8)]

    from contextlib import ExitStack
    es = ExitStack()
    SEM = {k: es.enter_context(nc.semaphore(k))
           for k in ("SX", "SG", "SV", "SS", "ST", "SGC", "SCC", "SBAR")}
    cnt = {k: 0 for k in SEM}
    M = {}

    def bump(ins, key, n=1, mark=None):
        ins.then_inc(SEM[key], n)
        cnt[key] += n
        if mark is not None:
            M[mark] = (key, cnt[key])
        return cnt[key]

    def wait(eng, mark):
        key, val = M[mark]
        eng.wait_ge(SEM[key], val)

    def dma(eng, out, in_, key, mark=None, **kw):
        ins = eng.dma_start(out=out, in_=in_, **kw)
        return bump(ins, key, 16, mark)

    def col(base):
        return lnv[:, base:base + 1]

    def s_chain1(s):
        s.activation(col(8), col(0), AF.Identity, scale=LNC)
        s.drain()
        s.activation(col(12), col(4), AF.Identity, scale=LNC)
        s.drain()
        s.activation(col(16), col(8), AF.Square)
        s.drain()
        s.activation(col(20), col(16), AF.Identity, scale=-1.0, bias=col(12))
        s.drain()
        s.activation(col(24), col(20), AF.Identity, bias=cvec[:, 0:1])
        s.drain()
        s.activation(col(28), col(24), AF.Sqrt)
        s.drain()

    def s_chain2(s, rms):
        s.activation(col(36), col(32), AF.Square)
        s.drain()
        s.activation(col(40), col(36), AF.Identity, scale=col(24))
        s.drain()
        s.activation(col(44), col(40), AF.Identity, scale=-0.5, bias=cvec[:, 1:2])
        s.drain()
        s.activation(col(48), col(32), AF.Identity, scale=col(44))
        s.drain()
        s.activation(col(52), col(8), AF.Identity, scale=col(48))
        s.drain()
        s.activation(col(56), col(52), AF.Identity, scale=-1.0)
        s.drain()
        if rms:
            s.activation(col(60), col(20), AF.Identity, scale=col(48))
            s.drain()
            s.activation(col(64), col(60), AF.Identity, scale=col(48))
            s.drain()
            s.activation(col(68), col(64), AF.Identity, bias=cvec[:, 0:1])
            s.drain()
            s.activation(col(72), col(68), AF.Sqrt)
            s.drain()

    def s_chain3(s):
        s.activation(col(80), col(76), AF.Square)
        s.drain()
        s.activation(col(84), col(80), AF.Identity, scale=col(68))
        s.drain()
        s.activation(col(88), col(84), AF.Identity, scale=-0.5, bias=cvec[:, 1:2])
        s.drain()
        s.activation(col(92), col(76), AF.Identity, scale=col(88))
        s.drain()

    with nc.Block() as blk:
        @blk.sync
        def _(sy):
            dma(sy, ident32[:], id_in, "SX")
            dma(sy, gam_b[:], gam_in, "SX")
            dma(sy, bpr_b[:], bpr_in, "SX", mark="consts")

        @blk.vector
        def _(v):
            v.memset(ones128[:], 1.0)
            v.memset(onescol[:], 1.0)
            v.memset(zpad2[:], 0.0)
            v.memset(szpad[:], 0.0)
            v.memset(p_vtm[:], 1.0)
            v.memset(cvec[:, 0:1], EPS_LN)
            v.memset(cvec[:, 1:2], 1.5)
            v.memset(cvec[:, 2:3], 128.0)
            v.drain()
            bump(v.nop(), "SV", 1, "v_init")

        @blk.gpsimd
        def _(g):
            dma(g, esel[:], es_in, "SG", mark="esel")
            for cj in range(NCJ):
                dma(g, p_tkv[:, 4096 * cj:4096 * (cj + 1)],
                    tkv[128 * cj:128 * (cj + 1), :], "SG", mark=f"tkv{cj}")

        NABS = N_ABS_GU + N_ABS_DN
        agu_f = agu_in.rearrange("(p r) c -> p (r c)", r=4)
        adn_f = adn_in.rearrange("(p r) c -> p (r c)", r=11)
        for tt in range(NT1):
            @blk.sync
            def _(sy, tt=tt):
                if tt >= 2:
                    wait(sy, f"xn32_{tt - 2}")
                dma(sy, p_x[tt % 2][:], x_in[128 * tt:128 * (tt + 1), :], "SX",
                    mark=f"x_{tt}")
                for a in range(2 * tt, min(2 * tt + 2, 32)):
                    if a >= 2:
                        wait(sy, f"absr_{a - 2}")
                    if a < N_ABS_GU:
                        src_ = agu_f[:, 1408 * a:1408 * (a + 1)]
                    else:
                        d = a - N_ABS_GU
                        src_ = adn_f[:, 1408 * d:1408 * (d + 1)]
                    dma(sy, p_abs[a % 2][:], src_, "SX", mark=f"absl_{a}")

            @blk.vector
            def _(v, tt=tt):
                wait(v, f"x_{tt}")
                v.reduce_sum(col(0), p_x[tt % 2][:], axis=AX.X)
                v.drain()
                bump(v.nop(), "SV", 1, f"s1_{tt}")

            @blk.scalar
            def _(s, tt=tt):
                wait(s, f"x_{tt}")
                if tt == 0:
                    wait(s, "v_init")
                if tt > 0:
                    wait(s, f"tr_{tt - 1}_3")
                s.activation(p_xn[:], p_x[tt % 2][:], AF.Square)
                s.drain()
                bump(s.nop(), "SS", 1, f"sq_{tt}")

            @blk.vector
            def _(v, tt=tt):
                wait(v, f"sq_{tt}")
                v.reduce_sum(col(4), p_xn[:], axis=AX.X)
                v.drain()
                bump(v.nop(), "SV", 1, f"s2_{tt}")

            @blk.scalar
            def _(s, tt=tt):
                wait(s, f"s2_{tt}")
                s_chain1(s)
                bump(s.nop(), "SS", 1, f"sd_{tt}")

            @blk.vector
            def _(v, tt=tt):
                wait(v, f"sd_{tt}")
                v.reciprocal(col(32), col(28))
                v.drain()
                bump(v.nop(), "SV", 1, f"r0_{tt}")

            @blk.scalar
            def _(s, tt=tt):
                wait(s, f"r0_{tt}")
                s_chain2(s, rms=False)
                s.activation(p_xn[:], p_x[tt % 2][:], AF.Identity,
                             scale=col(48), bias=col(56))
                s.drain()
                bump(s.nop(), "SS", 1, f"xn32_{tt}")

            for half2 in range(2):
                @blk.tensor
                def _(t, tt=tt, half2=half2):
                    if half2 == 0:
                        wait(t, f"xn32_{tt}")
                        if tt == 0:
                            wait(t, "consts")
                    for r in (2 * half2, 2 * half2 + 1):
                        if r >= 2:
                            wait(t, f"trev_{tt}_{r - 2}")
                        elif tt > 0:
                            wait(t, f"trev_{tt - 1}_{r + 2}")
                        ins = None
                        for j4 in range(4):
                            cj = 4 * r + j4
                            ins = t.transpose(
                                pb[2 + (r % 2)][:, 128 * j4:128 * (j4 + 1)],
                                p_xn[:, 128 * cj:128 * (cj + 1)], ident32[:])
                        bump(ins, "ST", 1, f"tr_{tt}_{r}")

                @blk.scalar
                def _(s, tt=tt, half2=half2):
                    for r in (2 * half2, 2 * half2 + 1):
                        wait(s, f"tr_{tt}_{r}")
                        for j4 in range(4):
                            cj = 4 * r + j4
                            s.activation(p_xnf[:, 128 * cj:128 * (cj + 1)],
                                         pb[2 + (r % 2)][:, 128 * j4:128 * (j4 + 1)],
                                         AF.Identity)
                        s.drain()
                        bump(s.nop(), "SS", 1, f"trev_{tt}_{r}")

            for oc in range(8):
                @blk.tensor
                def _(t, tt=tt, oc=oc):
                    if oc == 0:
                        if tt == 0:
                            wait(t, f"tkv{NCJ - 1}")
                        wait(t, f"trev_{tt}_3")
                    po, ptt = oc - 2, tt
                    if po < 0:
                        po, ptt = po + 8, tt - 1
                    if ptt >= 0:
                        wait(t, f"kvev_{ptt}_{po}")
                    ins = None
                    for cj in range(NCJ):
                        ins = t.matmul(pb[oc % 2][:],
                                       p_xnf[:, 128 * cj:128 * (cj + 1)],
                                       p_tkv[:, 4096 * cj + 512 * oc:
                                             4096 * cj + 512 * (oc + 1)],
                                       start=(cj == 0), stop=(cj == NCJ - 1))
                    bump(ins, "ST", 1, f"kvmm_{tt}_{oc}")

                if oc < 4:
                    @blk.scalar
                    def _(s, tt=tt, oc=oc):
                        wait(s, f"kvmm_{tt}_{oc}")
                        s.activation(p_ktm[:, 512 * oc:512 * (oc + 1)],
                                     pb[oc % 2][:], AF.Relu)
                        s.drain()
                        bump(s.nop(), "SS", 1, f"kvev_{tt}_{oc}")
                else:
                    @blk.vector
                    def _(v, tt=tt, oc=oc):
                        wait(v, f"kvmm_{tt}_{oc}")
                        h0 = 16 * (oc - 4)
                        dst = p_vtm[:].rearrange("p (h e) -> p h e", e=33)[
                            :, h0:h0 + 16, 0:32]
                        v.tensor_copy(dst, pb[oc % 2][:].rearrange(
                            "p (h e) -> p h e", e=32))
                        v.drain()
                        bump(v.nop(), "SV", 1, f"kvev_{tt}_{oc}")

            @blk.vector
            def _(v, tt=tt):
                for a in range(2 * tt, min(2 * tt + 2, 32)):
                    wait(v, f"absl_{a}")
                    v.reduce_sum(absred[:, a:a + 1], p_abs[a % 2][:], axis=AX.X,
                                 apply_absolute_value=True)
                    v.drain()
                    bump(v.nop(), "SV", 1, f"absr_{a}")

            @blk.tensor
            def _(t, tt=tt):
                wait(t, f"kvev_{tt}_3")
                wait(t, f"kvev_{tt}_7")
                if tt > 0:
                    wait(t, f"vkev_{tt - 1}")
                ins = None
                for h in range(64):
                    cj, i = h // 4, h % 4
                    bank = pb[5] if cj < 8 else pb[6]
                    cjl = cj % 8
                    ins = t.matmul(bank[32 * i:32 * (i + 1), 33 * cjl:33 * (cjl + 1)],
                                   p_ktm[:, 32 * h:32 * (h + 1)],
                                   p_vtm[:, 33 * h:33 * (h + 1)],
                                   start=True, stop=True,
                                   tile_position=(0, 32 * i))
                bump(ins, "ST", 1, f"vk_{tt}")

            @blk.vector
            def _(v, tt=tt):
                wait(v, f"vk_{tt}")
                for half in range(2):
                    dst = ccsb[:, 264 * half:264 * (half + 1)]
                    pbank = pb[5 + half]
                    if tt == 0:
                        v.tensor_copy(dst, pbank[:, 0:264])
                    else:
                        v.scalar_tensor_tensor(dst, pbank[:, 0:264], 1.0, dst,
                                               op0=ALU.mult, op1=ALU.add)
                    v.drain()
                bump(v.nop(), "SV", 1, f"vkev_{tt}")

        if dbg:
            @blk.gpsimd
            def _(g):
                wait(g, f"xn32_{NT1 - 1}")
                dma(g, d_xn, p_xn[:], "SG")
                wait(g, f"kvev_{NT1 - 1}_3")
                dma(g, d_ktm, p_ktm[:], "SG")
                wait(g, f"kvev_{NT1 - 1}_7")
                dma(g, d_vtm, p_vtm[:], "SG", mark="dump_p1")

        for a in range(32, NABS):
            @blk.sync
            def _(sy, a=a):
                wait(sy, f"absr_{a - 2}")
                if a < N_ABS_GU:
                    src_ = agu_f[:, 1408 * a:1408 * (a + 1)]
                else:
                    d = a - N_ABS_GU
                    src_ = adn_f[:, 1408 * d:1408 * (d + 1)]
                dma(sy, p_abs[a % 2][:], src_, "SX", mark=f"absl_{a}")

            @blk.vector
            def _(v, a=a):
                wait(v, f"absl_{a}")
                v.reduce_sum(absred[:, a:a + 1], p_abs[a % 2][:], axis=AX.X,
                             apply_absolute_value=True)
                v.drain()
                bump(v.nop(), "SV", 1, f"absr_{a}")

        @blk.vector
        def _(v):
            wait(v, f"vkev_{NT1 - 1}")
            wait(v, f"absr_{NABS - 1}")
            v.reduce_sum(ccsb[:, 528:529], absred[:, 0:N_ABS_GU], axis=AX.X)
            v.reduce_sum(ccsb[:, 529:530], absred[:, N_ABS_GU:NABS], axis=AX.X)
            v.drain()
            bump(v.nop(), "SV", 1, "ccsb_ready")

        @blk.sync
        def _(sy):
            wait(sy, "ccsb_ready")
            dma(sy, cc1_in, ccsb[:], "SX", mark="cc1_in")

        @blk.gpsimd
        def _(g):
            wait(g, "cc1_in")
            ins = g.collective_compute(
                "AllReduce", ALU.add,
                replica_groups=[[0, 1, 2, 3], [4, 5, 6, 7]],
                ins=[cc1_in], outs=[cc1_out])
            bump(ins, "SCC", 1, "ar1")

        @blk.sync
        def _(sy):
            wait(sy, "ar1")
            dma(sy, ccsb[:], cc1_out, "SX", mark="cc1_back")

        @blk.tensor
        def _(t):
            wait(t, "cc1_back")
            wait(t, "v_init")
            t.matmul(pb[0][0:1, 0:1], ccsb[:, 528:529], onescol[:],
                     start=True, stop=True)
            ins = t.matmul(pb[1][0:1, 0:1], ccsb[:, 529:530], onescol[:],
                           start=True, stop=True)
            bump(ins, "ST", 1, "abst")

        @blk.vector
        def _(v):
            wait(v, "abst")
            v.tensor_copy(zpad2[0:1, 0:1], pb[0][0:1, 0:1])
            v.tensor_copy(zpad2[0:1, 1:2], pb[1][0:1, 0:1])
            v.drain()
            bump(v.nop(), "SV", 1, "zpad2")
            v.tensor_copy(vkb[:], ccsb[:, 0:528])
            v.memset(ksbig[:], 0.0)
            v.drain()
            for cj in range(NCJ):
                for i in range(4):
                    h = 4 * cj + i
                    v.tensor_copy(
                        ksbig[32 * i:32 * (i + 1), 64 * cj + h:64 * cj + h + 1],
                        vkb[32 * i:32 * (i + 1), 33 * cj + 32:33 * cj + 33])
            v.drain()
            bump(v.nop(), "SV", 1, "vkb_ready")

        @blk.tensor
        def _(t):
            wait(t, "zpad2")
            ins = t.matmul(pb[2][:, 0:2], ones128[:], zpad2[:],
                           start=True, stop=True)
            bump(ins, "ST", 1, "absbc")

        @blk.vector
        def _(v):
            wait(v, "absbc")
            v.tensor_scalar(swv[:, 0:1], pb[2][:, 0:1], 1.0 / GU_CNT, 1e-5,
                            op0=ALU.mult, op1=ALU.max)
            v.tensor_scalar(swv[:, 2:3], pb[2][:, 1:2], 1.0 / DN_CNT, 1e-5,
                            op0=ALU.mult, op1=ALU.max)
            v.drain()
            v.reciprocal(swv[:, 1:2], swv[:, 0:1])
            v.reciprocal(swv[:, 3:4], swv[:, 2:3])
            v.drain()
            bump(v.nop(), "SV", 1, "sw_ready")

        if dbg:
            @blk.gpsimd
            def _(g):
                wait(g, "cc1_back")
                dma(g, d_ccsb, ccsb[:], "SG")
                wait(g, "sw_ready")
                dma(g, d_swv, swv[:], "SG")
                wait(g, "dump_p1")
                g.wait_ge(SEM["SG"], cnt["SG"])

        for name in ("sync", "vector", "scalar", "tensor", "gpsimd"):
            def mk():
                def f(e):
                    bump(e.nop(), "SBAR", 1)
                return f
            getattr(blk, name)(mk())
        M["bar1"] = ("SBAR", cnt["SBAR"])

        QS = [("gu", l, bq) for l in range(2) for bq in range(4)] + \
             [("dn", d, 0) for d in range(11)]
        for qi, spec in enumerate(QS):
            if spec[0] == "gu":
                _, l, bq = spec
                srcap = qgu_in[128 * l:128 * (l + 1), 2816 * bq:2816 * (bq + 1)]
                npart, wid, swc = 128, 2816, 1
            else:
                _, d, _ = spec
                srcap = qdn_in[64 * d:64 * (d + 1), :]
                npart, wid, swc = 64, 2048, 3

            @blk.sync
            def _(sy, qi=qi, srcap=srcap, npart=npart, wid=wid):
                if qi == 0:
                    wait(sy, "bar1")
                else:
                    wait(sy, f"qs_wq_{qi - 1}")
                dma(sy, q_in[0:npart, 0:wid], srcap, "SX", mark=f"qs_in_{qi}")

            @blk.scalar
            def _(s, qi=qi, npart=npart, wid=wid, swc=swc):
                wait(s, f"qs_in_{qi}")
                if qi == 0:
                    wait(s, "sw_ready")
                s.activation(q_u[0:npart, 0:wid], q_in[0:npart, 0:wid], AF.Identity,
                             scale=swv[0:npart, swc:swc + 1])
                s.drain()
                bump(s.nop(), "SS", 1, f"qs_u_{qi}")

            @blk.vector
            def _(v, qi=qi, npart=npart, wid=wid):
                wait(v, f"qs_u_{qi}")
                v.tensor_scalar(q_c1[0:npart, 0:wid], q_u[0:npart, 0:wid],
                                -0.5, None, op0=ALU.is_le)
                v.drain()
                v.scalar_tensor_tensor(q_wq[0:npart, 0:wid], q_u[0:npart, 0:wid],
                                       0.5, q_c1[0:npart, 0:wid],
                                       op0=ALU.is_ge, op1=ALU.subtract)
                v.drain()
                bump(v.nop(), "SV", 1, f"qs_wq_{qi}")

            @blk.sync
            def _(sy, qi=qi, spec=spec):
                wait(sy, f"qs_wq_{qi}")
                if spec[0] == "gu":
                    _, l, bq = spec
                    off = l * GUH + (11 * bq) * 32768
                    dst = ccg_in[off:off + 11 * 32768].rearrange(
                        "(j p f) -> p j f", p=128, f=256)
                    src = q_wq[:].rearrange("p (j f) -> p j f", f=256)[:, 0:11, :]
                else:
                    _, d, _ = spec
                    off = 64 * d * CD
                    dst = ccd_in[off:off + 64 * CD].rearrange("(p f) -> p f", p=64)
                    src = q_wq[0:64, 0:2048]
                dma(sy, dst, src, "SX", mark=f"qs_done_{qi}")

        @blk.gpsimd
        def _(g):
            wait(g, f"qs_done_{len(QS) - 1}")
            ins = g.collective_compute(
                "AllGather", ALU.bypass,
                replica_groups=[[0, 1, 2, 3, 4, 5, 6, 7]],
                ins=[ccg_in], outs=[ccg_out])
            bump(ins, "SCC", 1, "ag_gu")
            ins = g.collective_compute(
                "AllGather", ALU.bypass,
                replica_groups=[[0, 1, 2, 3, 4, 5, 6, 7]],
                ins=[ccd_in], outs=[ccd_out])
            bump(ins, "SCC", 1, "ag_dn")

        for tl in range(NT2):
            t0 = 512 * tl

            @blk.sync
            def _(sy, tl=tl, t0=t0):
                if tl == 0:
                    wait(sy, "bar1")
                else:
                    wait(sy, f"out_{tl - 1}")
                src = x_in[t0:t0 + 512, :].rearrange("(m p) c -> p m c", p=128)
                dst = a_acc[:].rearrange("p (m c) -> p m c", c=CD)
                dma(sy, dst, src, "SX", mark=f"xa_{tl}")

            for mm in range(NM):
                @blk.vector
                def _(v, tl=tl, mm=mm):
                    wait(v, f"xa_{tl}")
                    xs = a_acc[:, CD * mm:CD * (mm + 1)]
                    v.reduce_sum(col(0), xs, axis=AX.X)
                    v.drain()
                    bump(v.nop(), "SV", 1, f"a_s1_{tl}_{mm}")

                @blk.scalar
                def _(s, tl=tl, mm=mm):
                    wait(s, f"a_s1_{tl}_{mm}")
                    if mm > 0:
                        wait(s, f"a_tr_{tl}_{mm - 1}_3")
                    elif tl > 0:
                        wait(s, f"a_tr_{tl - 1}_{NM - 1}_3")
                    xs = a_acc[:, CD * mm:CD * (mm + 1)]
                    s.activation(a_xn[:], xs, AF.Square)
                    s.drain()
                    bump(s.nop(), "SS", 1, f"a_sq_{tl}_{mm}")

                @blk.vector
                def _(v, tl=tl, mm=mm):
                    wait(v, f"a_sq_{tl}_{mm}")
                    v.reduce_sum(col(4), a_xn[:], axis=AX.X)
                    v.drain()
                    bump(v.nop(), "SV", 1, f"a_s2_{tl}_{mm}")

                @blk.scalar
                def _(s, tl=tl, mm=mm):
                    wait(s, f"a_s2_{tl}_{mm}")
                    s_chain1(s)
                    bump(s.nop(), "SS", 1, f"a_sd_{tl}_{mm}")

                @blk.vector
                def _(v, tl=tl, mm=mm):
                    wait(v, f"a_sd_{tl}_{mm}")
                    v.reciprocal(col(32), col(28))
                    v.drain()
                    bump(v.nop(), "SV", 1, f"a_r0_{tl}_{mm}")

                @blk.scalar
                def _(s, tl=tl, mm=mm):
                    wait(s, f"a_r0_{tl}_{mm}")
                    s_chain2(s, rms=False)
                    xs = a_acc[:, CD * mm:CD * (mm + 1)]
                    s.activation(a_xn[:], xs, AF.Identity, scale=col(48),
                                 bias=col(56))
                    s.drain()
                    bump(s.nop(), "SS", 1, f"a_xn_{tl}_{mm}")

                @blk.vector
                def _(v, tl=tl, mm=mm):
                    wait(v, f"a_xn_{tl}_{mm}")
                    xs = a_acc[:, CD * mm:CD * (mm + 1)]
                    v.tensor_add(xs, xs, bpr_b[:])
                    v.drain()
                    bump(v.nop(), "SV", 1, f"a_bp_{tl}_{mm}")

                for half2 in range(2):
                    @blk.tensor
                    def _(t, tl=tl, mm=mm, half2=half2):
                        if half2 == 0:
                            wait(t, f"a_xn_{tl}_{mm}")
                        for r in (2 * half2, 2 * half2 + 1):
                            prev = None
                            if r >= 2:
                                prev = f"a_trev_{tl}_{mm}_{r - 2}"
                            elif mm > 0:
                                prev = f"a_trev_{tl}_{mm - 1}_{r + 2}"
                            elif tl > 0:
                                prev = f"a_trev_{tl - 1}_{NM - 1}_{r + 2}"
                            if prev is not None:
                                wait(t, prev)
                            ins = None
                            for j4 in range(4):
                                cj = 4 * r + j4
                                ins = t.transpose(
                                    pb[2 + (r % 2)][:, 128 * j4:128 * (j4 + 1)],
                                    a_xn[:, 128 * cj:128 * (cj + 1)], ident32[:])
                            bump(ins, "ST", 1, f"a_tr_{tl}_{mm}_{r}")

                    @blk.scalar
                    def _(s, tl=tl, mm=mm, half2=half2):
                        for r in (2 * half2, 2 * half2 + 1):
                            wait(s, f"a_tr_{tl}_{mm}_{r}")
                            for j4 in range(4):
                                cj = 4 * r + j4
                                s.activation(
                                    a_xnf[:, 512 * cj + 128 * mm:
                                          512 * cj + 128 * (mm + 1)],
                                    pb[2 + (r % 2)][:, 128 * j4:128 * (j4 + 1)],
                                    AF.Identity)
                            s.drain()
                            bump(s.nop(), "SS", 1, f"a_trev_{tl}_{mm}_{r}")

            for oj in range(NCJ):
                @blk.gpsimd
                def _(g, tl=tl, oj=oj):
                    if tl == 0 and oj == 0:
                        wait(g, "bar1")
                    k = NCJ * tl + oj
                    if k >= 2:
                        wait(g, f"qmm_{(k - 2) // NCJ}_{(k - 2) % NCJ}")
                    src = tq[:, 128 * oj:128 * (oj + 1)].rearrange(
                        "(cj p) f -> p cj f", p=128)
                    dst = a_tq[oj % 2][:].rearrange("p (cj f) -> p cj f", f=128)
                    dma(g, dst, src, "SG", mark=f"tq_{tl}_{oj}")

                @blk.tensor
                def _(t, tl=tl, oj=oj):
                    wait(t, f"tq_{tl}_{oj}")
                    wait(t, f"a_trev_{tl}_{NM - 1}_3")
                    if oj >= 2:
                        wait(t, f"qev_{tl}_{oj - 2}")
                    ins = None
                    for cj in range(NCJ):
                        ins = t.matmul(pb[oj % 2][:],
                                       a_tq[oj % 2][:, 128 * cj:128 * (cj + 1)],
                                       a_xnf[:, 512 * cj:512 * (cj + 1)],
                                       start=(cj == 0), stop=(cj == NCJ - 1))
                    bump(ins, "ST", 1, f"qmm_{tl}_{oj}")

                @blk.scalar
                def _(s, tl=tl, oj=oj):
                    wait(s, f"qmm_{tl}_{oj}")
                    s.activation(a_qf[:, 512 * oj:512 * (oj + 1)], pb[oj % 2][:],
                                 AF.Relu)
                    s.drain()
                    bump(s.nop(), "SS", 1, f"qev_{tl}_{oj}")

            @blk.tensor
            def _(t, tl=tl):
                wait(t, f"qev_{tl}_{NCJ - 1}")
                if tl == 0:
                    wait(t, "vkb_ready")
                ins = None
                for cj in range(NCJ):
                    ins = t.matmul(pb[2][0:64, :], ksbig[:, 64 * cj:64 * (cj + 1)],
                                   a_qf[:, 512 * cj:512 * (cj + 1)],
                                   start=(cj == 0), stop=(cj == NCJ - 1))
                bump(ins, "ST", 1, f"den_{tl}")

            @blk.vector
            def _(v, tl=tl):
                wait(v, f"den_{tl}")
                v.tensor_scalar_add(a_dent[:], pb[2][0:64, :], EPS_ATT)
                v.drain()
                v.reciprocal(a_dent2[:], a_dent[:])
                v.drain()
                v.tensor_copy(a_rden[:], a_dent2[:])
                v.drain()
                bump(v.nop(), "SV", 1, f"rden_{tl}")

            for cj in range(NCJ):
                @blk.tensor
                def _(t, tl=tl, cj=cj):
                    if cj == 0:
                        wait(t, f"rden_{tl}")
                        if tl == 0:
                            wait(t, "esel")
                    if cj >= 2:
                        wait(t, f"att_{tl}_{cj - 2}")
                    ins = t.matmul(pb[5 + cj % 2][:],
                                   esel[:, 128 * cj:128 * (cj + 1)], a_rden[:],
                                   start=True, stop=True)
                    bump(ins, "ST", 1, f"bc_{tl}_{cj}")
                    ins = None
                    for i in range(4):
                        ins = t.matmul(
                            pb[3 + cj % 2][32 * i:32 * (i + 1), :],
                            vkb[32 * i:32 * (i + 1), 33 * cj:33 * cj + 32],
                            a_qf[32 * i:32 * (i + 1), 512 * cj:512 * (cj + 1)],
                            start=True, stop=True, tile_position=(32 * i, 32 * i))
                    bump(ins, "ST", 1, f"num_{tl}_{cj}")

                @blk.scalar
                def _(s, tl=tl, cj=cj):
                    wait(s, f"bc_{tl}_{cj}")
                    s.activation(a_bcs[:], pb[5 + cj % 2][:], AF.Identity)
                    s.drain()
                    bump(s.nop(), "SS", 1, f"bcs_{tl}_{cj}")

                @blk.vector
                def _(v, tl=tl, cj=cj):
                    wait(v, f"num_{tl}_{cj}")
                    wait(v, f"bcs_{tl}_{cj}")
                    v.tensor_mul(a_att[:, 512 * cj:512 * (cj + 1)],
                                 pb[3 + cj % 2][:], a_bcs[:])
                    v.drain()
                    bump(v.nop(), "SV", 1, f"att_{tl}_{cj}")

            if dbg and tl == 0:
                @blk.gpsimd
                def _(g):
                    wait(g, "qev_0_15")
                    dma(g, d_qf, a_qf[:], "SG")
                    wait(g, "rden_0")
                    dma(g, d_dent, a_dent[:], "SG")
                    wait(g, "att_0_15")
                    dma(g, d_att, a_att[:], "SG", mark="dump_2a")
                    g.wait_ge(SEM["SG"], cnt["SG"])

            for oc in range(4):
                for half in range(2):
                    @blk.gpsimd
                    def _(g, tl=tl, oc=oc, half=half):
                        k = 8 * tl + 2 * oc + half
                        if k >= 2:
                            kk = k - 2
                            ptl, poc = kk // 8, (kk % 8) // 2
                            wait(g, f"pmm_{ptl}_{poc}_3")
                        cj0 = 8 * half
                        src = tproj[128 * cj0:128 * (cj0 + 8),
                                    512 * oc:512 * (oc + 1)].rearrange(
                            "(cj p) f -> p cj f", p=128)
                        dst = a_tp[half][:].rearrange("p (cj f) -> p cj f", f=512)
                        dma(g, dst, src, "SG", mark=f"tp_{tl}_{oc}_{half}")

                for mm in range(NM):
                    @blk.tensor
                    def _(t, tl=tl, oc=oc, mm=mm):
                        if oc == 0 and mm == 0:
                            wait(t, f"att_{tl}_{NCJ - 1}")
                        idx = 4 * oc + mm
                        if idx >= 2:
                            pidx = idx - 2
                            wait(t, f"pev_{tl}_{pidx // 4}_{pidx % 4}")
                        elif tl > 0:
                            pidx = 14 + idx
                            wait(t, f"pev_{tl - 1}_{pidx // 4}_{pidx % 4}")
                        bank = pb[2 * (oc % 2) + mm % 2]
                        ins = None
                        for half in range(2):
                            wait(t, f"tp_{tl}_{oc}_{half}")
                            for c8 in range(8):
                                cj = 8 * half + c8
                                ins = t.matmul(
                                    bank[:],
                                    a_att[:, 512 * cj + 128 * mm:
                                          512 * cj + 128 * (mm + 1)],
                                    a_tp[half][:, 512 * c8:512 * (c8 + 1)],
                                    start=(cj == 0), stop=(cj == NCJ - 1))
                        bump(ins, "ST", 1, f"pmm_{tl}_{oc}_{mm}")

                    @blk.vector
                    def _(v, tl=tl, oc=oc, mm=mm):
                        wait(v, f"pmm_{tl}_{oc}_{mm}")
                        if oc == 0:
                            wait(v, f"a_bp_{tl}_{mm}")
                        bank = pb[2 * (oc % 2) + mm % 2]
                        dst = a_acc[:, CD * mm + 512 * oc:CD * mm + 512 * (oc + 1)]
                        v.scalar_tensor_tensor(dst, bank[:], 1.0, dst,
                                               op0=ALU.mult, op1=ALU.add)
                        v.drain()
                        bump(v.nop(), "SV", 1, f"pev_{tl}_{oc}_{mm}")

            @blk.sync
            def _(sy, tl=tl, t0=t0):
                for oc in range(4):
                    for mm in range(NM):
                        wait(sy, f"pev_{tl}_{oc}_{mm}")
                if dbg and tl == 0:
                    dma(sy, d_out2a, a_acc[:], "SX")
                dst = out_hbm[t0:t0 + 512, :].rearrange("(m p) c -> p m c", p=128)
                src = a_acc[:].rearrange("p (m c) -> p m c", c=CD)
                dma(sy, dst, src, "SX", mark=f"out_{tl}")

        @blk.sync
        def _(sy):
            wait(sy, f"out_{NT2 - 1}")
            bump(sy.nop(), "SBAR", 1)
        for name in ("vector", "scalar", "tensor", "gpsimd"):
            def mk():
                def f(e):
                    wait(e, f"out_{NT2 - 1}")
                    bump(e.nop(), "SBAR", 1)
                return f
            getattr(blk, name)(mk())
        M["bar2"] = ("SBAR", cnt["SBAR"])

        for tl in range(NT2):
            t0 = 512 * tl
            for mm in range(NM):
                @blk.sync
                def _(sy, tl=tl, mm=mm, t0=t0):
                    if tl == 0 and mm == 0:
                        wait(sy, "bar2")
                    k = NM * tl + mm
                    if k >= 2:
                        wait(sy, f"b_xnr_{(k - 2) // NM}_{(k - 2) % NM}")
                    dma(sy, b_x[mm % 2][:],
                        x_in[t0 + 128 * mm:t0 + 128 * (mm + 1), :], "SX",
                        mark=f"bx_{tl}_{mm}")

                @blk.vector
                def _(v, tl=tl, mm=mm):
                    wait(v, f"bx_{tl}_{mm}")
                    v.reduce_sum(col(0), b_x[mm % 2][:], axis=AX.X)
                    v.drain()
                    bump(v.nop(), "SV", 1, f"b_s1_{tl}_{mm}")

                @blk.scalar
                def _(s, tl=tl, mm=mm):
                    wait(s, f"b_s1_{tl}_{mm}")
                    if mm > 0:
                        wait(s, f"b_tr_{tl}_{mm - 1}_3")
                    elif tl > 0:
                        wait(s, f"b_tr_{tl - 1}_{NM - 1}_3")
                        wait(s, f"facc_{tl - 1}_1_1")
                    s.activation(b_xn[:], b_x[mm % 2][:], AF.Square)
                    s.drain()
                    bump(s.nop(), "SS", 1, f"b_sq_{tl}_{mm}")

                @blk.vector
                def _(v, tl=tl, mm=mm):
                    wait(v, f"b_sq_{tl}_{mm}")
                    v.reduce_sum(col(4), b_xn[:], axis=AX.X)
                    v.drain()
                    bump(v.nop(), "SV", 1, f"b_s2_{tl}_{mm}")

                @blk.scalar
                def _(s, tl=tl, mm=mm):
                    wait(s, f"b_s2_{tl}_{mm}")
                    s_chain1(s)
                    bump(s.nop(), "SS", 1, f"b_sd_{tl}_{mm}")

                @blk.vector
                def _(v, tl=tl, mm=mm):
                    wait(v, f"b_sd_{tl}_{mm}")
                    v.reciprocal(col(32), col(28))
                    v.drain()
                    bump(v.nop(), "SV", 1, f"b_r0_{tl}_{mm}")

                @blk.scalar
                def _(s, tl=tl, mm=mm):
                    wait(s, f"b_r0_{tl}_{mm}")
                    s_chain2(s, rms=True)
                    bump(s.nop(), "SS", 1, f"b_sd2_{tl}_{mm}")

                @blk.vector
                def _(v, tl=tl, mm=mm):
                    wait(v, f"b_sd2_{tl}_{mm}")
                    v.reciprocal(col(76), col(72))
                    v.drain()
                    bump(v.nop(), "SV", 1, f"b_r02_{tl}_{mm}")

                @blk.scalar
                def _(s, tl=tl, mm=mm):
                    wait(s, f"b_r02_{tl}_{mm}")
                    s_chain3(s)
                    s.activation(b_xn[:], b_x[mm % 2][:], AF.Identity,
                                 scale=col(48), bias=col(56))
                    s.drain()
                    bump(s.nop(), "SS", 1, f"b_xnr_{tl}_{mm}")

                @blk.vector
                def _(v, tl=tl, mm=mm):
                    wait(v, f"b_xnr_{tl}_{mm}")
                    if mm == 0 and tl > 0:
                        wait(v, f"facc_{tl - 1}_1_1")
                    v.tensor_mul(b_gt[:], b_xn[:], gam_b[:])
                    v.drain()
                    v.reduce_max(col(96), b_gt[:], axis=AX.X,
                                 apply_absolute_value=True)
                    v.drain()
                    v.tensor_scalar(col(100), col(96), col(92), 1e-5,
                                    op0=ALU.mult, op1=ALU.max)
                    v.drain()
                    v.reciprocal(col(120), col(100))
                    v.drain()
                    v.scalar_tensor_tensor(col(104), col(120), 127.0, col(92),
                                           op0=ALU.mult, op1=ALU.mult)
                    v.drain()
                    v.tensor_scalar(col(108), col(100), swv[:, 0:1], 1.0 / 127.0,
                                    op0=ALU.mult, op1=ALU.mult)
                    v.drain()
                    v.tensor_mul(col(112), col(108), col(108))
                    v.drain()
                    v.tensor_mul(col(116), col(112), col(108))
                    v.drain()
                    v.tensor_copy(lnv[:, 124 + mm:125 + mm], col(116))
                    v.drain()
                    bump(v.nop(), "SV", 1, f"b_qv_{tl}_{mm}")

                @blk.scalar
                def _(s, tl=tl, mm=mm):
                    wait(s, f"b_qv_{tl}_{mm}")
                    s.activation(b_u[:], b_gt[:], AF.Identity, scale=col(104),
                                 bias=cvec[:, 2:3])
                    s.drain()
                    bump(s.nop(), "SS", 1, f"b_u_{tl}_{mm}")

                @blk.vector
                def _(v, tl=tl, mm=mm):
                    wait(v, f"b_u_{tl}_{mm}")
                    v.tensor_copy(b_i1[:], b_u[:])
                    v.drain()
                    bump(v.nop(), "SV", 1, f"b_i1_{tl}_{mm}")

                @blk.vector
                def _(v, tl=tl, mm=mm):
                    wait(v, f"b_i1_{tl}_{mm}")
                    v.scalar_tensor_tensor(b_c1[:], b_i1[:], 0.5, b_u[:],
                                           op0=ALU.add, op1=ALU.is_le)
                    v.drain()
                    bump(v.nop(), "SV", 1, f"b_c1_{tl}_{mm}")

                @blk.vector
                def _(v, tl=tl, mm=mm):
                    wait(v, f"b_c1_{tl}_{mm}")
                    if mm > 0:
                        wait(v, f"b_tr_{tl}_{mm - 1}_3")
                    elif tl > 0:
                        wait(v, f"b_tr_{tl - 1}_{NM - 1}_3")
                    v.scalar_tensor_tensor(b_xq32[:], b_i1[:], -128.0, b_c1[:],
                                           op0=ALU.add, op1=ALU.add)
                    v.drain()
                    bump(v.nop(), "SV", 1, f"b_xq_{tl}_{mm}")

                for half2 in range(2):
                    @blk.tensor
                    def _(t, tl=tl, mm=mm, half2=half2):
                        if half2 == 0:
                            wait(t, f"b_xq_{tl}_{mm}")
                            if mm == 0 and tl > 0:
                                wait(t, f"dev_{tl - 1}_1_7")
                        for r in (2 * half2, 2 * half2 + 1):
                            prev = None
                            if r >= 2:
                                prev = f"b_trev_{tl}_{mm}_{r - 2}"
                            elif mm > 0:
                                prev = f"b_trev_{tl}_{mm - 1}_{r + 2}"
                            elif tl > 0:
                                prev = f"b_trev_{tl - 1}_{NM - 1}_{r + 2}"
                            if prev is not None:
                                wait(t, prev)
                            ins = None
                            for j4 in range(4):
                                cj = 4 * r + j4
                                ins = t.transpose(
                                    pb[2 + (r % 2)][:, 128 * j4:128 * (j4 + 1)],
                                    b_xq32[:, 128 * cj:128 * (cj + 1)], ident32[:])
                            bump(ins, "ST", 1, f"b_tr_{tl}_{mm}_{r}")

                    @blk.scalar
                    def _(s, tl=tl, mm=mm, half2=half2):
                        for r in (2 * half2, 2 * half2 + 1):
                            wait(s, f"b_tr_{tl}_{mm}_{r}")
                            for j4 in range(4):
                                cj = 4 * r + j4
                                s.activation(
                                    b_xqf[:, 512 * cj + 128 * mm:
                                          512 * cj + 128 * (mm + 1)],
                                    pb[2 + (r % 2)][:, 128 * j4:128 * (j4 + 1)],
                                    AF.Identity)
                            s.drain()
                            bump(s.nop(), "SS", 1, f"b_trev_{tl}_{mm}_{r}")

            @blk.vector
            def _(v, tl=tl):
                if tl > 0:
                    wait(v, f"hq_{tl - 1}_{NFC - 1}")
                v.memset(b_mp[:], 0.0)
                v.drain()
                v.memset(b_mn[:], 0.0)
                v.drain()
                bump(v.nop(), "SV", 1, f"mpinit_{tl}")

            for pi in range(NFB2):
                if dbg and tl == 0 and pi == 1:
                    @blk.gpsimd
                    def _(g):
                        wait(g, "wg_0_0_1")
                        dma(g, d_wg, b_wg[0][:], "SG", mark="dump_wg")
                        g.wait_ge(SEM["SG"], cnt["SG"])

                @blk.sync
                def _(sy, tl=tl, pi=pi):
                    if tl == 0 and pi == 0:
                        wait(sy, "ag_gu")
                    if dbg and tl == 0 and pi == 2:
                        wait(sy, "dump_wg")
                    k = NFB2 * tl + pi
                    if k >= 2:
                        kk = k - 2
                        wait(sy, f"gu_done_{kk // NFB2}_{kk % NFB2}")
                    slot = b_wg[pi % 2]
                    for gi, fb2 in enumerate((pi, 22 + pi)):
                        dst = slot[:, 4096 * gi:4096 * (gi + 1)].rearrange(
                            "p (cj f) -> p cj f", f=256)
                        dma(sy, dst, wgu_view[fb2], "SX", mark=f"wg_{tl}_{pi}_{gi}")

                @blk.tensor
                def _(t, tl=tl, pi=pi):
                    wait(t, f"wg_{tl}_{pi}_1")
                    wait(t, f"b_trev_{tl}_{NM - 1}_3")
                    if pi >= 2:
                        wait(t, f"hid_{tl}_{pi - 2}")
                    elif tl > 0:
                        wait(t, f"dev_{tl - 1}_1_7")
                    slot = b_wg[pi % 2]
                    bk = 4 * (pi % 2)
                    for part in range(4):
                        gi, fh = part // 2, part % 2
                        ins = None
                        for cj in range(NCJ):
                            lhs = slot[:, 4096 * gi + 256 * cj + 128 * fh:
                                       4096 * gi + 256 * cj + 128 * (fh + 1)]
                            ins = t.matmul(pb[bk + part][:], lhs,
                                           b_xqf[:, 512 * cj:512 * (cj + 1)],
                                           start=(cj == 0), stop=(cj == NCJ - 1))
                        bump(ins, "ST", 1, f"gumm_{tl}_{pi}_{part}")

                for fh in range(2):
                    @blk.vector
                    def _(v, tl=tl, pi=pi, fh=fh):
                        bk = 4 * (pi % 2)
                        wait(v, f"gumm_{tl}_{pi}_{fh}")
                        wait(v, f"gumm_{tl}_{pi}_{2 + fh}")
                        fc = 2 * pi + fh
                        v.tensor_scalar(b_h1[:], pb[bk + fh][:], 0.0, None,
                                        op0=ALU.max)
                        v.drain()
                        v.tensor_mul(b_h2[:], b_h1[:], pb[bk + 2 + fh][:])
                        v.drain()
                        v.tensor_mul(b_hid[:, 512 * fc:512 * (fc + 1)],
                                     b_h2[:], b_h1[:])
                        v.drain()
                        bump(v.nop(), "SV", 1, f"hch_{tl}_{pi}_{fh}")

                @blk.vector
                def _(v, tl=tl, pi=pi):
                    bump(v.nop(), "SV", 1, f"hid_{tl}_{pi}")

                for fh in range(2):
                    @blk.vector
                    def _(v, tl=tl, pi=pi, fh=fh):
                        if pi == 0 and fh == 0:
                            wait(v, f"mpinit_{tl}")
                        fc = 2 * pi + fh
                        hc = b_hid[:, 512 * fc:512 * (fc + 1)]
                        v.tensor_max(b_mp[:], b_mp[:], hc)
                        v.drain()
                        v.scalar_tensor_tensor(b_mn[:], hc, -1.0, b_mn[:],
                                               op0=ALU.mult, op1=ALU.max)
                        v.drain()
                        bump(v.nop(), "SV", 1, f"hmax_{tl}_{pi}_{fh}")

                @blk.vector
                def _(v, tl=tl, pi=pi):
                    bump(v.nop(), "SV", 1, f"mm_{tl}_{pi}")
                    bump(v.nop(), "SV", 1, f"gu_done_{tl}_{pi}")

            @blk.vector
            def _(v, tl=tl):
                wait(v, f"mm_{tl}_{NFB2 - 1}")
                v.tensor_max(b_mp[:], b_mp[:], b_mn[:])
                v.drain()
                bump(v.nop(), "SV", 1, f"mabs_{tl}")

            @blk.tensor
            def _(t, tl=tl):
                wait(t, f"mabs_{tl}")
                wait(t, f"gumm_{tl}_{NFB2 - 1}_3")
                ins = None
                for bq in range(4):
                    ins = t.transpose(pb[4][:, 128 * bq:128 * (bq + 1)],
                                      b_mp[:, 128 * bq:128 * (bq + 1)], ident32[:])
                bump(ins, "ST", 1, f"mtr_{tl}")

            @blk.vector
            def _(v, tl=tl):
                wait(v, f"mtr_{tl}")
                v.reduce_max(lnv[:, 96:100], pb[4][:].rearrange(
                    "p (b c) -> p b c", c=128), axis=AX.X)
                v.drain()
                v.tensor_mul(lnv[:, 100:104], lnv[:, 96:100], lnv[:, 124:128])
                v.drain()
                v.tensor_scalar_max(lnv[:, 100:104], lnv[:, 100:104], 1e-5)
                v.drain()
                v.reciprocal(lnv[:, 104:108], lnv[:, 100:104])
                v.drain()
                v.scalar_tensor_tensor(lnv[:, 108:112], lnv[:, 104:108], 127.0,
                                       lnv[:, 124:128], op0=ALU.mult, op1=ALU.mult)
                v.drain()
                v.tensor_scalar(osq[:], lnv[:, 100:104], swv[:, 2:3],
                                1.0 / 127.0, op0=ALU.mult, op1=ALU.mult)
                v.drain()
                v.tensor_copy(sh128[:].rearrange("p (b c) -> p b c", c=32)[
                    :, :, 0:1].rearrange("p b c -> p (b c)"), lnv[:, 108:112])
                v.drain()
                bump(v.nop(), "SV", 1, f"shv_{tl}")

            @blk.tensor
            def _(t, tl=tl):
                wait(t, f"shv_{tl}")
                ins = t.transpose(pb[5][:, 0:128], sh128[:], ident32[:])
                bump(ins, "ST", 1, f"shtr_{tl}")

            @blk.vector
            def _(v, tl=tl):
                wait(v, f"shtr_{tl}")
                for bq in range(4):
                    v.tensor_copy(szpad[0:1, 128 * bq:128 * (bq + 1)],
                                  pb[5][32 * bq:32 * bq + 1, 0:128])
                v.drain()
                bump(v.nop(), "SV", 1, f"szp_{tl}")

            if dbg and tl == 0:
                @blk.gpsimd
                def _(g):
                    wait(g, "b_trev_0_3_3")
                    dma(g, d_xqf, b_xqf[:], "SG")
                    wait(g, "mm_0_21")
                    dma(g, d_hid, b_hid[:], "SG", mark="dump_hid")
                    g.wait_ge(SEM["SG"], cnt["SG"])

            @blk.tensor
            def _(t, tl=tl):
                wait(t, f"szp_{tl}")
                if tl == 0:
                    wait(t, "v_init")
                    if dbg:
                        wait(t, "dump_hid")
                ins = t.matmul(pb[6][:], ones128[:], szpad[:], start=True, stop=True)
                bump(ins, "ST", 1, f"bcq_{tl}")

            for fc in range(NFC):
                @blk.vector
                def _(v, tl=tl, fc=fc):
                    if fc == 0:
                        wait(v, f"bcq_{tl}")
                    hc = b_hid[:, 512 * fc:512 * (fc + 1)]
                    v.tensor_mul(b_h1[:], hc, pb[6][:])
                    v.drain()
                    bump(v.nop(), "SV", 1, f"hb_{tl}_{fc}")

                @blk.scalar
                def _(s, tl=tl, fc=fc):
                    wait(s, f"hb_{tl}_{fc}")
                    s.activation(b_h2[:], b_h1[:], AF.Identity, bias=cvec[:, 2:3])
                    s.drain()
                    bump(s.nop(), "SS", 1, f"hu_{tl}_{fc}")

                @blk.vector
                def _(v, tl=tl, fc=fc):
                    wait(v, f"hu_{tl}_{fc}")
                    v.tensor_copy(hq_i1[:], b_h2[:])
                    v.drain()
                    bump(v.nop(), "SV", 1, f"hi1_{tl}_{fc}")

                @blk.vector
                def _(v, tl=tl, fc=fc):
                    wait(v, f"hi1_{tl}_{fc}")
                    v.scalar_tensor_tensor(hq_c1[:], hq_i1[:], 0.5, b_h2[:],
                                           op0=ALU.add, op1=ALU.is_le)
                    v.drain()
                    bump(v.nop(), "SV", 1, f"hc1_{tl}_{fc}")

                @blk.vector
                def _(v, tl=tl, fc=fc):
                    wait(v, f"hc1_{tl}_{fc}")
                    hc = b_hid[:, 512 * fc:512 * (fc + 1)]
                    v.scalar_tensor_tensor(hc, hq_i1[:], -128.0, hq_c1[:],
                                           op0=ALU.add, op1=ALU.add)
                    v.drain()
                    bump(v.nop(), "SV", 1, f"hq_{tl}_{fc}")

            if dbg and tl == 0:
                @blk.gpsimd
                def _(g):
                    wait(g, "shv_0")
                    dma(g, d_lnv, lnv[:], "SG")
                    dma(g, d_osq, osq[:], "SG")
                    g.wait_ge(SEM["SG"], cnt["SG"])

            for mp in range(2):
                for fc in range(NFC):
                    @blk.sync
                    def _(sy, tl=tl, mp=mp, fc=fc):
                        if tl == 0 and mp == 0 and fc == 0:
                            wait(sy, "ag_dn")
                        k = (2 * tl + mp) * NFC + fc
                        if k >= 2:
                            kk = k - 2
                            wait(sy, f"dmm_{kk // (2 * NFC)}_"
                                     f"{(kk % (2 * NFC)) // NFC}_{kk % NFC}")
                        dma(sy, b_wd[fc % 2][:],
                            wdn_view[128 * fc:128 * (fc + 1), :], "SX",
                            mark=f"wd_{tl}_{mp}_{fc}")

                    @blk.tensor
                    def _(t, tl=tl, mp=mp, fc=fc):
                        wait(t, f"wd_{tl}_{mp}_{fc}")
                        if fc == 0:
                            wait(t, f"hq_{tl}_{NFC - 1}")
                            if (tl, mp) != (0, 0):
                                ptl, pmp = (tl, 0) if mp == 1 else (tl - 1, 1)
                                wait(t, f"dev_{ptl}_{pmp}_7")
                        ins = None
                        for mi in range(2):
                            mmx = 2 * mp + mi
                            lhs = b_hid[:, 512 * fc + 128 * mmx:
                                        512 * fc + 128 * (mmx + 1)]
                            for oc in range(4):
                                ins = t.matmul(pb[4 * mi + oc][:], lhs,
                                               b_wd[fc % 2][:, 512 * oc:512 * (oc + 1)],
                                               start=(fc == 0), stop=(fc == NFC - 1))
                        bump(ins, "ST", 1, f"dmm_{tl}_{mp}_{fc}")

                for mi in range(2):
                    @blk.scalar
                    def _(s, tl=tl, mp=mp, mi=mi):
                        wait(s, f"dmm_{tl}_{mp}_{NFC - 1}")
                        mmx = 2 * mp + mi
                        for oc in range(4):
                            s.activation(b_fs[mi][:, 512 * oc:512 * (oc + 1)],
                                         pb[4 * mi + oc][:], AF.Identity,
                                         scale=osq[:, mmx:mmx + 1])
                            s.drain()
                            bump(s.nop(), "SS", 1, f"dev_{tl}_{mp}_{4 * mi + oc}")
                        bump(s.nop(), "SS", 1, f"ffn_{tl}_{mp}_{mi}")

                    @blk.gpsimd
                    def _(g, tl=tl, mp=mp, mi=mi, t0=t0):
                        wait(g, f"ffn_{tl}_{mp}_{mi}")
                        wait(g, f"out_{tl}")
                        mmx = 2 * mp + mi
                        dma(g, out_hbm[t0 + 128 * mmx:t0 + 128 * (mmx + 1), :],
                            b_fs[mi][:], "SG", accum_op=ALU.add,
                            mark=f"facc_{tl}_{mp}_{mi}")

        @blk.sync
        def _(sy):
            sy.wait_ge(SEM["SX"], cnt["SX"])
            sy.wait_ge(SEM["SG"], cnt["SG"])

        @blk.gpsimd
        def _(g):
            g.wait_ge(SEM["SG"], cnt["SG"])

    return nc, es


_CACHE = {}


def kernel(L0_attn_x, L0_attn_w_qkv, L0_attn_w_proj, L0_attn_b_proj,
           L0_ffn_w_gate_up, L0_ffn_w_down, L0_ffn_norm_weight):
    x = np.ascontiguousarray(np.asarray(L0_attn_x, dtype=np.float32))
    w_qkv = np.asarray(L0_attn_w_qkv, dtype=np.float32)
    w_proj = np.asarray(L0_attn_w_proj, dtype=np.float32)
    b_proj = np.asarray(L0_attn_b_proj, dtype=np.float32)
    w_gu = np.asarray(L0_ffn_w_gate_up, dtype=np.float32)
    w_dn = np.asarray(L0_ffn_w_down, dtype=np.float32)
    gam = np.asarray(L0_ffn_norm_weight, dtype=np.float32)

    tkv_h = np.ascontiguousarray(w_qkv[CD:3 * CD, :].T)
    tq_h = np.ascontiguousarray(w_qkv[0:CD, :].T)
    tproj_h = np.ascontiguousarray(w_proj.T)
    tgu_h = np.ascontiguousarray(w_gu.T)
    tdown_h = np.ascontiguousarray(w_dn.T)
    gam_b_h = np.ascontiguousarray(np.broadcast_to(gam[None, :], (128, CD)))
    bpr_b_h = np.ascontiguousarray(np.broadcast_to(b_proj[None, :], (128, CD)))
    id_h = np.eye(128, dtype=np.float32)
    es_h = np.zeros((64, NCJ * 128), dtype=np.float32)
    for cj in range(NCJ):
        for i in range(4):
            es_h[4 * cj + i, 128 * cj + 32 * i:128 * cj + 32 * (i + 1)] = 1.0

    in_maps = []
    for c in range(8):
        b, s0, q = c // 4, (c % 4) * TOKC, c % 4
        in_maps.append({
            "x_in": np.ascontiguousarray(x[b, s0:s0 + TOKC, :]),
            "tkv": tkv_h, "tq": tq_h, "tproj": tproj_h,
            "agu_in": np.ascontiguousarray(tgu_h[512 * q:512 * (q + 1), :]),
            "adn_in": np.ascontiguousarray(tdown_h[1408 * q:1408 * (q + 1), :]),
            "qgu_in": np.ascontiguousarray(tgu_h[256 * c:256 * (c + 1), :]),
            "qdn_in": np.ascontiguousarray(tdown_h[704 * c:704 * (c + 1), :]),
            "gam_in": gam_b_h, "bpr_in": bpr_b_h,
            "id_in": id_h, "es_in": es_h,
        })

    if "nc" not in _CACHE:
        _CACHE["nc"] = build_program()
    nc, es = _CACHE["nc"]

    res = run_bass_kernel_spmd(nc, in_maps, core_ids=list(range(8)))
    out = np.empty((B, NSEQ, CD), dtype=np.float32)
    for c in range(8):
        b, s0 = c // 4, (c % 4) * TOKC
        out[b, s0:s0 + TOKC, :] = res.results[c]["out_hbm"]
    return out


# version 08 (final)
# speedup vs baseline: 1.0771x; 1.0447x over previous
import numpy as np
import ml_dtypes
import concourse.bass as bass
import concourse.mybir as mybir
from concourse.bass_utils import run_bass_kernel_spmd

F32 = mybir.dt.float32
BF16 = mybir.dt.bfloat16
I32 = mybir.dt.int32
AF = mybir.ActivationFunctionType
ALU = mybir.AluOpType
AX = mybir.AxisListType

B, NSEQ, CD = 2, 8192, 2048
TOKC = 2048
FF = 5632
NT1 = 16
NT2 = 4
NM = 4
NCJ = 16
NFB2 = 22
NFC = 44
EPS_LN = 1e-5
EPS_RMS = 1e-5
EPS_ATT = 1e-15
GU_CNT = float(2 * FF * CD)
DN_CNT = float(CD * FF)
GUH = 44 * 128 * 256
N_ABS_GU = 32
N_ABS_DN = 16
LNC = 1.0 / CD


def build_program(dbg=False):
    nc = bass.Bass("TRN2", target_bir_lowering=False, detect_race_conditions=False)

    x_in = nc.dram_tensor("x_in", [TOKC, CD], F32, kind="ExternalInput").ap()
    tkv = nc.dram_tensor("tkv", [CD, 2 * CD], F32, kind="ExternalInput").ap()
    tq = nc.dram_tensor("tq", [CD, CD], F32, kind="ExternalInput").ap()
    tproj = nc.dram_tensor("tproj", [CD, CD], F32, kind="ExternalInput").ap()
    agu_in = nc.dram_tensor("agu_in", [512, 2 * FF], F32, kind="ExternalInput").ap()
    adn_in = nc.dram_tensor("adn_in", [1408, CD], F32, kind="ExternalInput").ap()
    qgu_in = nc.dram_tensor("qgu_in", [256, 2 * FF], F32, kind="ExternalInput").ap()
    qdn_in = nc.dram_tensor("qdn_in", [704, CD], F32, kind="ExternalInput").ap()
    gam_in = nc.dram_tensor("gam_in", [128, CD], F32, kind="ExternalInput").ap()
    bpr_in = nc.dram_tensor("bpr_in", [128, CD], F32, kind="ExternalInput").ap()
    id_in = nc.dram_tensor("id_in", [128, 128], F32, kind="ExternalInput").ap()
    es_in = nc.dram_tensor("es_in", [64, NCJ * 128], F32, kind="ExternalInput").ap()
    out_hbm = nc.dram_tensor("out_hbm", [TOKC, CD], F32, kind="ExternalOutput").ap()
    if dbg:
        d_xn = nc.dram_tensor("d_xn", [128, CD], F32, kind="ExternalOutput").ap()
        d_ktm = nc.dram_tensor("d_ktm", [128, CD], F32, kind="ExternalOutput").ap()
        d_vtm = nc.dram_tensor("d_vtm", [128, 64 * 33], F32, kind="ExternalOutput").ap()
        d_ccsb = nc.dram_tensor("d_ccsb", [128, 530], F32, kind="ExternalOutput").ap()
        d_swv = nc.dram_tensor("d_swv", [128, 4], F32, kind="ExternalOutput").ap()
        d_qf = nc.dram_tensor("d_qf", [128, NCJ * 512], F32, kind="ExternalOutput").ap()
        d_dent = nc.dram_tensor("d_dent", [64, 512], F32, kind="ExternalOutput").ap()
        d_att = nc.dram_tensor("d_att", [128, NCJ * 512], F32, kind="ExternalOutput").ap()
        d_xqf = nc.dram_tensor("d_xqf", [128, NCJ * 512], F32, kind="ExternalOutput").ap()
        d_hid = nc.dram_tensor("d_hid", [128, NFC * 512], F32, kind="ExternalOutput").ap()
        d_lnv = nc.dram_tensor("d_lnv", [128, 128], F32, kind="ExternalOutput").ap()
        d_wg = nc.dram_tensor("d_wg", [128, 8192], F32, kind="ExternalOutput").ap()
        d_out2a = nc.dram_tensor("d_out2a", [128, NM * CD], F32, kind="ExternalOutput").ap()
        d_osq = nc.dram_tensor("d_osq", [128, 4], F32, kind="ExternalOutput").ap()

    cc1_in = nc.dram_tensor("cc1_in", [128, 530], F32, kind="Internal").ap()
    cc1_out = nc.dram_tensor("cc1_out", [128, 530], F32, kind="Internal").ap()
    ccg_in = nc.dram_tensor("ccg_in", [2 * GUH], BF16, kind="Internal").ap()
    ccg_out = nc.dram_tensor("ccg_out", [16 * GUH], BF16, kind="Internal").ap()
    ccd_in = nc.dram_tensor("ccd_in", [704 * CD], BF16, kind="Internal").ap()
    ccd_out = nc.dram_tensor("ccd_out", [FF * CD], BF16, kind="Internal").ap()

    wgu_view = ccg_out.rearrange("(cj b p f) -> b p cj f", cj=16, b=44, p=128)
    wdn_view = ccd_out.rearrange("(f c) -> f c", c=CD)

    ident32 = nc.alloc_sbuf_tensor("ident32", [128, 128], F32)
    ones128 = nc.alloc_sbuf_tensor("ones128", [128, 128], F32)
    onescol = nc.alloc_sbuf_tensor("onescol", [128, 1], F32)
    esel = nc.alloc_sbuf_tensor("esel", [64, NCJ * 128], BF16)
    vkb = nc.alloc_sbuf_tensor("vkb", [128, 528], BF16)
    ksbig = nc.alloc_sbuf_tensor("ksbig", [128, NCJ * 64], BF16)
    ccsb = nc.alloc_sbuf_tensor("ccsb", [128, 530], F32)
    absred = nc.alloc_sbuf_tensor("absred", [128, 64], F32)
    swv = nc.alloc_sbuf_tensor("swv", [128, 4], F32)
    zpad2 = nc.alloc_sbuf_tensor("zpad2", [128, 2], F32)
    lnv = nc.alloc_sbuf_tensor("lnv", [128, 128], F32)
    gam_b = nc.alloc_sbuf_tensor("gam_b", [128, CD], F32)
    bpr_b = nc.alloc_sbuf_tensor("bpr_b", [128, CD], F32)
    szpad = nc.alloc_sbuf_tensor("szpad", [128, 512], F32)
    osq = nc.alloc_sbuf_tensor("osq", [128, 4], F32)
    cvec = nc.alloc_sbuf_tensor("cvec", [128, 4], F32)
    sh128 = nc.alloc_sbuf_tensor("sh128", [128, 128], F32)

    ARN = 179968
    arena = nc.alloc_sbuf_tensor("arena", [128, ARN], mybir.dt.uint8)
    AB = nc.lookup_mloc(arena).addr

    def at(name, shape, dtype, off):
        itb = {F32: 4, BF16: 2, I32: 4}[dtype]
        need = int(np.prod(shape[1:])) * itb
        assert off % 64 == 0 and off + need <= ARN, (name, off, need)
        return nc.alloc_sbuf_tensor_at(name, shape, dtype, offset=AB + off)

    p_tkv = at("p_tkv", [128, NCJ * 4096], BF16, 0)
    p_x = [at(f"p_x{i}", [128, CD], F32, 131072 + 8192 * i) for i in range(2)]
    p_xn = at("p_xn", [128, CD], F32, 147456)
    p_xnf = at("p_xnf", [128, CD], BF16, 155648)
    p_ktm = at("p_ktm", [128, CD], BF16, 159744)
    p_vtm = at("p_vtm", [128, 64 * 33], BF16, 163840)
    p_abs = [at(f"p_abs{i}", [128, 1408], F32, 168128 + 5632 * i) for i in range(2)]

    a_acc = at("a_acc", [128, NM * CD], F32, 0)
    a_xnf = at("a_xnf", [128, NM * CD], BF16, 32768)
    a_qf = at("a_qf", [128, NCJ * 512], BF16, 49152)
    a_att = at("a_att", [128, NCJ * 512], BF16, 65536)
    a_xn = at("a_xn", [128, CD], F32, 81920)
    a_tq = [at(f"a_tq{i}", [128, CD], BF16, 90112 + 4096 * i) for i in range(2)]
    a_tp = [at(f"a_tp{i}", [128, 8 * 512], BF16, 98304 + 8192 * i) for i in range(2)]
    a_dent = at("a_dent", [64, 512], F32, 114688)
    a_dent2 = at("a_dent2", [64, 512], F32, 116736)
    a_rden = at("a_rden", [64, 512], BF16, 118784)
    a_bcs = at("a_bcs", [128, 512], BF16, 119808)
    q_in = at("q_in", [128, 2816], F32, 120832)
    q_u = at("q_u", [128, 2816], F32, 132096)
    q_i1 = at("q_i1", [128, 2816], I32, 143360)
    q_c1 = at("q_c1", [128, 2816], F32, 154624)
    q_r = at("q_r", [128, 2816], F32, 165888)
    q_wq = at("q_wq", [128, 2816], BF16, 120832)

    b_hid = at("b_hid", [128, NFC * 512], BF16, 0)
    b_xqf = at("b_xqf", [128, NCJ * 512], BF16, 45056)
    b_x = [at(f"b_x{i}", [128, CD], F32, 61440 + 8192 * i) for i in range(2)]
    b_xn = at("b_xn", [128, CD], F32, 77824)
    b_gt = at("b_gt", [128, CD], F32, 86016)
    b_u = at("b_u", [128, CD], F32, 94208)
    b_i1 = at("b_i1", [128, CD], I32, 102400)
    b_c1 = at("b_c1", [128, CD], F32, 110592)
    b_xq32 = at("b_xq32", [128, CD], F32, 118784)
    b_wg = [at(f"b_wg{i}", [128, 2 * 4096], BF16, 126976 + 16384 * i) for i in range(2)]
    b_wd = [at(f"b_wd{i}", [128, CD], BF16, 159744 + 4096 * i) for i in range(2)]
    b_h1 = at("b_h1", [128, 512], F32, 167936)
    b_h2 = at("b_h2", [128, 512], F32, 169984)
    b_mp = at("b_mp", [128, 512], F32, 172032)
    b_mn = at("b_mn", [128, 512], F32, 174080)
    hq_i1 = at("hq_i1", [128, 512], I32, 172032)
    hq_c1 = at("hq_c1", [128, 512], F32, 174080)
    b_fs = [b_xn, b_gt]

    pb = [nc.alloc_psum_tensor(f"pb{i}", [128, 512], F32) for i in range(8)]

    from contextlib import ExitStack
    es = ExitStack()
    SEM = {k: es.enter_context(nc.semaphore(k))
           for k in ("SX", "SG", "SV", "SS", "ST", "SGC", "SCC", "SBAR")}
    cnt = {k: 0 for k in SEM}
    M = {}

    def bump(ins, key, n=1, mark=None):
        ins.then_inc(SEM[key], n)
        cnt[key] += n
        if mark is not None:
            M[mark] = (key, cnt[key])
        return cnt[key]

    def wait(eng, mark):
        key, val = M[mark]
        eng.wait_ge(SEM[key], val)

    def dma(eng, out, in_, key, mark=None, **kw):
        ins = eng.dma_start(out=out, in_=in_, **kw)
        return bump(ins, key, 16, mark)

    def col(base):
        return lnv[:, base:base + 1]

    def s_chain1(s):
        s.activation(col(8), col(0), AF.Identity, scale=LNC)
        s.drain()
        s.activation(col(12), col(4), AF.Identity, scale=LNC)
        s.drain()
        s.activation(col(16), col(8), AF.Square)
        s.drain()
        s.activation(col(20), col(16), AF.Identity, scale=-1.0, bias=col(12))
        s.drain()
        s.activation(col(24), col(20), AF.Identity, bias=cvec[:, 0:1])
        s.drain()
        s.activation(col(28), col(24), AF.Sqrt)
        s.drain()

    def s_chain2(s, rms):
        s.activation(col(36), col(32), AF.Square)
        s.drain()
        s.activation(col(40), col(36), AF.Identity, scale=col(24))
        s.drain()
        s.activation(col(44), col(40), AF.Identity, scale=-0.5, bias=cvec[:, 1:2])
        s.drain()
        s.activation(col(48), col(32), AF.Identity, scale=col(44))
        s.drain()
        s.activation(col(52), col(8), AF.Identity, scale=col(48))
        s.drain()
        s.activation(col(56), col(52), AF.Identity, scale=-1.0)
        s.drain()
        if rms:
            s.activation(col(60), col(20), AF.Identity, scale=col(48))
            s.drain()
            s.activation(col(64), col(60), AF.Identity, scale=col(48))
            s.drain()
            s.activation(col(68), col(64), AF.Identity, bias=cvec[:, 0:1])
            s.drain()
            s.activation(col(72), col(68), AF.Sqrt)
            s.drain()

    def s_chain3(s):
        s.activation(col(80), col(76), AF.Square)
        s.drain()
        s.activation(col(84), col(80), AF.Identity, scale=col(68))
        s.drain()
        s.activation(col(88), col(84), AF.Identity, scale=-0.5, bias=cvec[:, 1:2])
        s.drain()
        s.activation(col(92), col(76), AF.Identity, scale=col(88))
        s.drain()

    with nc.Block() as blk:
        @blk.sync
        def _(sy):
            dma(sy, ident32[:], id_in, "SX")
            dma(sy, gam_b[:], gam_in, "SX")
            dma(sy, bpr_b[:], bpr_in, "SX", mark="consts")

        @blk.vector
        def _(v):
            v.memset(ones128[:], 1.0)
            v.memset(onescol[:], 1.0)
            v.memset(zpad2[:], 0.0)
            v.memset(szpad[:], 0.0)
            v.memset(p_vtm[:], 1.0)
            v.memset(cvec[:, 0:1], EPS_LN)
            v.memset(cvec[:, 1:2], 1.5)
            v.memset(cvec[:, 2:3], 128.0)
            v.drain()
            bump(v.nop(), "SV", 1, "v_init")

        @blk.gpsimd
        def _(g):
            dma(g, esel[:], es_in, "SG", mark="esel")
            for cj in range(NCJ):
                dma(g, p_tkv[:, 4096 * cj:4096 * (cj + 1)],
                    tkv[128 * cj:128 * (cj + 1), :], "SG", mark=f"tkv{cj}")

        NABS = N_ABS_GU + N_ABS_DN
        agu_f = agu_in.rearrange("(p r) c -> p (r c)", r=4)
        adn_f = adn_in.rearrange("(p r) c -> p (r c)", r=11)
        for tt in range(NT1):
            @blk.sync
            def _(sy, tt=tt):
                if tt >= 2:
                    wait(sy, f"xn32_{tt - 2}")
                dma(sy, p_x[tt % 2][:], x_in[128 * tt:128 * (tt + 1), :], "SX",
                    mark=f"x_{tt}")
                for a in range(2 * tt, min(2 * tt + 2, 32)):
                    if a >= 2:
                        wait(sy, f"absr_{a - 2}")
                    if a < N_ABS_GU:
                        src_ = agu_f[:, 1408 * a:1408 * (a + 1)]
                    else:
                        d = a - N_ABS_GU
                        src_ = adn_f[:, 1408 * d:1408 * (d + 1)]
                    dma(sy, p_abs[a % 2][:], src_, "SX", mark=f"absl_{a}")

            @blk.vector
            def _(v, tt=tt):
                wait(v, f"x_{tt}")
                v.reduce_sum(col(0), p_x[tt % 2][:], axis=AX.X)
                v.drain()
                bump(v.nop(), "SV", 1, f"s1_{tt}")

            @blk.scalar
            def _(s, tt=tt):
                wait(s, f"x_{tt}")
                if tt == 0:
                    wait(s, "v_init")
                if tt > 0:
                    wait(s, f"tr_{tt - 1}_3")
                s.activation(p_xn[:], p_x[tt % 2][:], AF.Square)
                s.drain()
                bump(s.nop(), "SS", 1, f"sq_{tt}")

            @blk.vector
            def _(v, tt=tt):
                wait(v, f"sq_{tt}")
                v.reduce_sum(col(4), p_xn[:], axis=AX.X)
                v.drain()
                bump(v.nop(), "SV", 1, f"s2_{tt}")

            @blk.scalar
            def _(s, tt=tt):
                wait(s, f"s2_{tt}")
                s_chain1(s)
                bump(s.nop(), "SS", 1, f"sd_{tt}")

            @blk.vector
            def _(v, tt=tt):
                wait(v, f"sd_{tt}")
                v.reciprocal(col(32), col(28))
                v.drain()
                bump(v.nop(), "SV", 1, f"r0_{tt}")

            @blk.scalar
            def _(s, tt=tt):
                wait(s, f"r0_{tt}")
                s_chain2(s, rms=False)
                s.activation(p_xn[:], p_x[tt % 2][:], AF.Identity,
                             scale=col(48), bias=col(56))
                s.drain()
                bump(s.nop(), "SS", 1, f"xn32_{tt}")

            for half2 in range(2):
                @blk.tensor
                def _(t, tt=tt, half2=half2):
                    if half2 == 0:
                        wait(t, f"xn32_{tt}")
                        if tt == 0:
                            wait(t, "consts")
                    for r in (2 * half2, 2 * half2 + 1):
                        if r >= 2:
                            wait(t, f"trev_{tt}_{r - 2}")
                        elif tt > 0:
                            wait(t, f"trev_{tt - 1}_{r + 2}")
                        ins = None
                        for j4 in range(4):
                            cj = 4 * r + j4
                            ins = t.transpose(
                                pb[2 + (r % 2)][:, 128 * j4:128 * (j4 + 1)],
                                p_xn[:, 128 * cj:128 * (cj + 1)], ident32[:])
                        bump(ins, "ST", 1, f"tr_{tt}_{r}")

                @blk.scalar
                def _(s, tt=tt, half2=half2):
                    for r in (2 * half2, 2 * half2 + 1):
                        wait(s, f"tr_{tt}_{r}")
                        for j4 in range(4):
                            cj = 4 * r + j4
                            s.activation(p_xnf[:, 128 * cj:128 * (cj + 1)],
                                         pb[2 + (r % 2)][:, 128 * j4:128 * (j4 + 1)],
                                         AF.Identity)
                        s.drain()
                        bump(s.nop(), "SS", 1, f"trev_{tt}_{r}")

            for oc in range(8):
                @blk.tensor
                def _(t, tt=tt, oc=oc):
                    if oc == 0:
                        if tt == 0:
                            wait(t, f"tkv{NCJ - 1}")
                        wait(t, f"trev_{tt}_3")
                    po, ptt = oc - 2, tt
                    if po < 0:
                        po, ptt = po + 8, tt - 1
                    if ptt >= 0:
                        wait(t, f"kvev_{ptt}_{po}")
                    ins = None
                    for cj in range(NCJ):
                        ins = t.matmul(pb[oc % 2][:],
                                       p_xnf[:, 128 * cj:128 * (cj + 1)],
                                       p_tkv[:, 4096 * cj + 512 * oc:
                                             4096 * cj + 512 * (oc + 1)],
                                       start=(cj == 0), stop=(cj == NCJ - 1))
                    bump(ins, "ST", 1, f"kvmm_{tt}_{oc}")

                if oc < 4:
                    @blk.scalar
                    def _(s, tt=tt, oc=oc):
                        wait(s, f"kvmm_{tt}_{oc}")
                        s.activation(p_ktm[:, 512 * oc:512 * (oc + 1)],
                                     pb[oc % 2][:], AF.Relu)
                        s.drain()
                        bump(s.nop(), "SS", 1, f"kvev_{tt}_{oc}")
                else:
                    @blk.vector
                    def _(v, tt=tt, oc=oc):
                        wait(v, f"kvmm_{tt}_{oc}")
                        h0 = 16 * (oc - 4)
                        dst = p_vtm[:].rearrange("p (h e) -> p h e", e=33)[
                            :, h0:h0 + 16, 0:32]
                        v.tensor_copy(dst, pb[oc % 2][:].rearrange(
                            "p (h e) -> p h e", e=32))
                        v.drain()
                        bump(v.nop(), "SV", 1, f"kvev_{tt}_{oc}")

            @blk.vector
            def _(v, tt=tt):
                for a in range(2 * tt, min(2 * tt + 2, 32)):
                    wait(v, f"absl_{a}")
                    v.reduce_sum(absred[:, a:a + 1], p_abs[a % 2][:], axis=AX.X,
                                 apply_absolute_value=True)
                    v.drain()
                    bump(v.nop(), "SV", 1, f"absr_{a}")

            @blk.tensor
            def _(t, tt=tt):
                wait(t, f"kvev_{tt}_3")
                wait(t, f"kvev_{tt}_7")
                if tt > 0:
                    wait(t, f"vkev_{tt - 1}")
                ins = None
                for h in range(64):
                    cj, i = h // 4, h % 4
                    bank = pb[5] if cj < 8 else pb[6]
                    cjl = cj % 8
                    ins = t.matmul(bank[32 * i:32 * (i + 1), 33 * cjl:33 * (cjl + 1)],
                                   p_ktm[:, 32 * h:32 * (h + 1)],
                                   p_vtm[:, 33 * h:33 * (h + 1)],
                                   start=True, stop=True,
                                   tile_position=(0, 32 * i))
                bump(ins, "ST", 1, f"vk_{tt}")

            @blk.vector
            def _(v, tt=tt):
                wait(v, f"vk_{tt}")
                for half in range(2):
                    dst = ccsb[:, 264 * half:264 * (half + 1)]
                    pbank = pb[5 + half]
                    if tt == 0:
                        v.tensor_copy(dst, pbank[:, 0:264])
                    else:
                        v.scalar_tensor_tensor(dst, pbank[:, 0:264], 1.0, dst,
                                               op0=ALU.mult, op1=ALU.add)
                    v.drain()
                bump(v.nop(), "SV", 1, f"vkev_{tt}")

        if dbg:
            @blk.gpsimd
            def _(g):
                wait(g, f"xn32_{NT1 - 1}")
                dma(g, d_xn, p_xn[:], "SG")
                wait(g, f"kvev_{NT1 - 1}_3")
                dma(g, d_ktm, p_ktm[:], "SG")
                wait(g, f"kvev_{NT1 - 1}_7")
                dma(g, d_vtm, p_vtm[:], "SG", mark="dump_p1")

        for a in range(32, NABS):
            @blk.sync
            def _(sy, a=a):
                wait(sy, f"absr_{a - 2}")
                if a < N_ABS_GU:
                    src_ = agu_f[:, 1408 * a:1408 * (a + 1)]
                else:
                    d = a - N_ABS_GU
                    src_ = adn_f[:, 1408 * d:1408 * (d + 1)]
                dma(sy, p_abs[a % 2][:], src_, "SX", mark=f"absl_{a}")

            @blk.vector
            def _(v, a=a):
                wait(v, f"absl_{a}")
                v.reduce_sum(absred[:, a:a + 1], p_abs[a % 2][:], axis=AX.X,
                             apply_absolute_value=True)
                v.drain()
                bump(v.nop(), "SV", 1, f"absr_{a}")

        @blk.vector
        def _(v):
            wait(v, f"vkev_{NT1 - 1}")
            wait(v, f"absr_{NABS - 1}")
            v.reduce_sum(ccsb[:, 528:529], absred[:, 0:N_ABS_GU], axis=AX.X)
            v.reduce_sum(ccsb[:, 529:530], absred[:, N_ABS_GU:NABS], axis=AX.X)
            v.drain()
            bump(v.nop(), "SV", 1, "ccsb_ready")

        @blk.sync
        def _(sy):
            wait(sy, "ccsb_ready")
            dma(sy, cc1_in, ccsb[:], "SX", mark="cc1_in")

        @blk.gpsimd
        def _(g):
            wait(g, "cc1_in")
            ins = g.collective_compute(
                "AllReduce", ALU.add,
                replica_groups=[[0, 1, 2, 3], [4, 5, 6, 7]],
                ins=[cc1_in], outs=[cc1_out])
            bump(ins, "SCC", 1, "ar1")

        @blk.sync
        def _(sy):
            wait(sy, "ar1")
            dma(sy, ccsb[:], cc1_out, "SX", mark="cc1_back")

        @blk.tensor
        def _(t):
            wait(t, "cc1_back")
            wait(t, "v_init")
            t.matmul(pb[0][0:1, 0:1], ccsb[:, 528:529], onescol[:],
                     start=True, stop=True)
            ins = t.matmul(pb[1][0:1, 0:1], ccsb[:, 529:530], onescol[:],
                           start=True, stop=True)
            bump(ins, "ST", 1, "abst")

        @blk.vector
        def _(v):
            wait(v, "abst")
            v.tensor_copy(zpad2[0:1, 0:1], pb[0][0:1, 0:1])
            v.tensor_copy(zpad2[0:1, 1:2], pb[1][0:1, 0:1])
            v.drain()
            bump(v.nop(), "SV", 1, "zpad2")
            v.tensor_copy(vkb[:], ccsb[:, 0:528])
            v.memset(ksbig[:], 0.0)
            v.drain()
            for cj in range(NCJ):
                for i in range(4):
                    h = 4 * cj + i
                    v.tensor_copy(
                        ksbig[32 * i:32 * (i + 1), 64 * cj + h:64 * cj + h + 1],
                        vkb[32 * i:32 * (i + 1), 33 * cj + 32:33 * cj + 33])
            v.drain()
            bump(v.nop(), "SV", 1, "vkb_ready")

        @blk.tensor
        def _(t):
            wait(t, "zpad2")
            ins = t.matmul(pb[2][:, 0:2], ones128[:], zpad2[:],
                           start=True, stop=True)
            bump(ins, "ST", 1, "absbc")

        @blk.vector
        def _(v):
            wait(v, "absbc")
            v.tensor_scalar(swv[:, 0:1], pb[2][:, 0:1], 1.0 / GU_CNT, 1e-5,
                            op0=ALU.mult, op1=ALU.max)
            v.tensor_scalar(swv[:, 2:3], pb[2][:, 1:2], 1.0 / DN_CNT, 1e-5,
                            op0=ALU.mult, op1=ALU.max)
            v.drain()
            v.reciprocal(swv[:, 1:2], swv[:, 0:1])
            v.reciprocal(swv[:, 3:4], swv[:, 2:3])
            v.drain()
            bump(v.nop(), "SV", 1, "sw_ready")

        if dbg:
            @blk.gpsimd
            def _(g):
                wait(g, "cc1_back")
                dma(g, d_ccsb, ccsb[:], "SG")
                wait(g, "sw_ready")
                dma(g, d_swv, swv[:], "SG")
                wait(g, "dump_p1")
                g.wait_ge(SEM["SG"], cnt["SG"])

        for name in ("sync", "vector", "scalar", "tensor", "gpsimd"):
            def mk():
                def f(e):
                    bump(e.nop(), "SBAR", 1)
                return f
            getattr(blk, name)(mk())
        M["bar1"] = ("SBAR", cnt["SBAR"])

        QS = [("gu", l, bq) for l in range(2) for bq in range(4)] + \
             [("dn", d, 0) for d in range(11)]
        for qi, spec in enumerate(QS):
            if spec[0] == "gu":
                _, l, bq = spec
                srcap = qgu_in[128 * l:128 * (l + 1), 2816 * bq:2816 * (bq + 1)]
                npart, wid, swc = 128, 2816, 1
            else:
                _, d, _ = spec
                srcap = qdn_in[64 * d:64 * (d + 1), :]
                npart, wid, swc = 64, 2048, 3

            @blk.sync
            def _(sy, qi=qi, srcap=srcap, npart=npart, wid=wid):
                if qi == 0:
                    wait(sy, "bar1")
                else:
                    wait(sy, f"qs_wq_{qi - 1}")
                dma(sy, q_in[0:npart, 0:wid], srcap, "SX", mark=f"qs_in_{qi}")

            @blk.scalar
            def _(s, qi=qi, npart=npart, wid=wid, swc=swc):
                wait(s, f"qs_in_{qi}")
                if qi == 0:
                    wait(s, "sw_ready")
                s.activation(q_u[0:npart, 0:wid], q_in[0:npart, 0:wid], AF.Identity,
                             scale=swv[0:npart, swc:swc + 1])
                s.drain()
                bump(s.nop(), "SS", 1, f"qs_u_{qi}")

            @blk.vector
            def _(v, qi=qi, npart=npart, wid=wid):
                wait(v, f"qs_u_{qi}")
                v.tensor_scalar(q_c1[0:npart, 0:wid], q_u[0:npart, 0:wid],
                                -0.5, None, op0=ALU.is_le)
                v.drain()
                v.scalar_tensor_tensor(q_wq[0:npart, 0:wid], q_u[0:npart, 0:wid],
                                       0.5, q_c1[0:npart, 0:wid],
                                       op0=ALU.is_ge, op1=ALU.subtract)
                v.drain()
                bump(v.nop(), "SV", 1, f"qs_wq_{qi}")

            @blk.sync
            def _(sy, qi=qi, spec=spec):
                wait(sy, f"qs_wq_{qi}")
                if spec[0] == "gu":
                    _, l, bq = spec
                    off = l * GUH + (11 * bq) * 32768
                    dst = ccg_in[off:off + 11 * 32768].rearrange(
                        "(j p f) -> p j f", p=128, f=256)
                    src = q_wq[:].rearrange("p (j f) -> p j f", f=256)[:, 0:11, :]
                else:
                    _, d, _ = spec
                    off = 64 * d * CD
                    dst = ccd_in[off:off + 64 * CD].rearrange("(p f) -> p f", p=64)
                    src = q_wq[0:64, 0:2048]
                dma(sy, dst, src, "SX", mark=f"qs_done_{qi}")

        @blk.gpsimd
        def _(g):
            wait(g, f"qs_done_{len(QS) - 1}")
            ins = g.collective_compute(
                "AllGather", ALU.bypass,
                replica_groups=[[0, 1, 2, 3, 4, 5, 6, 7]],
                ins=[ccg_in], outs=[ccg_out])
            bump(ins, "SCC", 1, "ag_gu")
            ins = g.collective_compute(
                "AllGather", ALU.bypass,
                replica_groups=[[0, 1, 2, 3, 4, 5, 6, 7]],
                ins=[ccd_in], outs=[ccd_out])
            bump(ins, "SCC", 1, "ag_dn")

        for tl in range(NT2):
            t0 = 512 * tl

            @blk.sync
            def _(sy, tl=tl, t0=t0):
                if tl == 0:
                    wait(sy, "bar1")
                else:
                    wait(sy, f"out_{tl - 1}")
                src = x_in[t0:t0 + 512, :].rearrange("(m p) c -> p m c", p=128)
                dst = a_acc[:].rearrange("p (m c) -> p m c", c=CD)
                dma(sy, dst, src, "SX", mark=f"xa_{tl}")

            for mm in range(NM):
                @blk.vector
                def _(v, tl=tl, mm=mm):
                    wait(v, f"xa_{tl}")
                    xs = a_acc[:, CD * mm:CD * (mm + 1)]
                    v.reduce_sum(col(0), xs, axis=AX.X)
                    v.drain()
                    bump(v.nop(), "SV", 1, f"a_s1_{tl}_{mm}")

                @blk.scalar
                def _(s, tl=tl, mm=mm):
                    wait(s, f"a_s1_{tl}_{mm}")
                    if mm > 0:
                        wait(s, f"a_tr_{tl}_{mm - 1}_3")
                    elif tl > 0:
                        wait(s, f"a_tr_{tl - 1}_{NM - 1}_3")
                    xs = a_acc[:, CD * mm:CD * (mm + 1)]
                    s.activation(a_xn[:], xs, AF.Square)
                    s.drain()
                    bump(s.nop(), "SS", 1, f"a_sq_{tl}_{mm}")

                @blk.vector
                def _(v, tl=tl, mm=mm):
                    wait(v, f"a_sq_{tl}_{mm}")
                    v.reduce_sum(col(4), a_xn[:], axis=AX.X)
                    v.drain()
                    bump(v.nop(), "SV", 1, f"a_s2_{tl}_{mm}")

                @blk.scalar
                def _(s, tl=tl, mm=mm):
                    wait(s, f"a_s2_{tl}_{mm}")
                    s_chain1(s)
                    bump(s.nop(), "SS", 1, f"a_sd_{tl}_{mm}")

                @blk.vector
                def _(v, tl=tl, mm=mm):
                    wait(v, f"a_sd_{tl}_{mm}")
                    v.reciprocal(col(32), col(28))
                    v.drain()
                    bump(v.nop(), "SV", 1, f"a_r0_{tl}_{mm}")

                @blk.scalar
                def _(s, tl=tl, mm=mm):
                    wait(s, f"a_r0_{tl}_{mm}")
                    s_chain2(s, rms=False)
                    xs = a_acc[:, CD * mm:CD * (mm + 1)]
                    s.activation(a_xn[:], xs, AF.Identity, scale=col(48),
                                 bias=col(56))
                    s.drain()
                    bump(s.nop(), "SS", 1, f"a_xn_{tl}_{mm}")

                @blk.vector
                def _(v, tl=tl, mm=mm):
                    wait(v, f"a_xn_{tl}_{mm}")
                    xs = a_acc[:, CD * mm:CD * (mm + 1)]
                    v.tensor_add(xs, xs, bpr_b[:])
                    v.drain()
                    bump(v.nop(), "SV", 1, f"a_bp_{tl}_{mm}")

                for half2 in range(2):
                    @blk.tensor
                    def _(t, tl=tl, mm=mm, half2=half2):
                        if half2 == 0:
                            wait(t, f"a_xn_{tl}_{mm}")
                        for r in (2 * half2, 2 * half2 + 1):
                            prev = None
                            if r >= 2:
                                prev = f"a_trev_{tl}_{mm}_{r - 2}"
                            elif mm > 0:
                                prev = f"a_trev_{tl}_{mm - 1}_{r + 2}"
                            elif tl > 0:
                                prev = f"a_trev_{tl - 1}_{NM - 1}_{r + 2}"
                            if prev is not None:
                                wait(t, prev)
                            ins = None
                            for j4 in range(4):
                                cj = 4 * r + j4
                                ins = t.transpose(
                                    pb[2 + (r % 2)][:, 128 * j4:128 * (j4 + 1)],
                                    a_xn[:, 128 * cj:128 * (cj + 1)], ident32[:])
                            bump(ins, "ST", 1, f"a_tr_{tl}_{mm}_{r}")

                    @blk.scalar
                    def _(s, tl=tl, mm=mm, half2=half2):
                        for r in (2 * half2, 2 * half2 + 1):
                            wait(s, f"a_tr_{tl}_{mm}_{r}")
                            for j4 in range(4):
                                cj = 4 * r + j4
                                s.activation(
                                    a_xnf[:, 512 * cj + 128 * mm:
                                          512 * cj + 128 * (mm + 1)],
                                    pb[2 + (r % 2)][:, 128 * j4:128 * (j4 + 1)],
                                    AF.Identity)
                            s.drain()
                            bump(s.nop(), "SS", 1, f"a_trev_{tl}_{mm}_{r}")

            for oj in range(NCJ):
                @blk.gpsimd
                def _(g, tl=tl, oj=oj):
                    if tl == 0 and oj == 0:
                        wait(g, "bar1")
                    k = NCJ * tl + oj
                    if k >= 2:
                        wait(g, f"qmm_{(k - 2) // NCJ}_{(k - 2) % NCJ}")
                    src = tq[:, 128 * oj:128 * (oj + 1)].rearrange(
                        "(cj p) f -> p cj f", p=128)
                    dst = a_tq[oj % 2][:].rearrange("p (cj f) -> p cj f", f=128)
                    dma(g, dst, src, "SG", mark=f"tq_{tl}_{oj}")

                @blk.tensor
                def _(t, tl=tl, oj=oj):
                    wait(t, f"tq_{tl}_{oj}")
                    wait(t, f"a_trev_{tl}_{NM - 1}_3")
                    if oj >= 2:
                        wait(t, f"qev_{tl}_{oj - 2}")
                    ins = None
                    for cj in range(NCJ):
                        ins = t.matmul(pb[oj % 2][:],
                                       a_tq[oj % 2][:, 128 * cj:128 * (cj + 1)],
                                       a_xnf[:, 512 * cj:512 * (cj + 1)],
                                       start=(cj == 0), stop=(cj == NCJ - 1))
                    bump(ins, "ST", 1, f"qmm_{tl}_{oj}")

                @blk.scalar
                def _(s, tl=tl, oj=oj):
                    wait(s, f"qmm_{tl}_{oj}")
                    s.activation(a_qf[:, 512 * oj:512 * (oj + 1)], pb[oj % 2][:],
                                 AF.Relu)
                    s.drain()
                    bump(s.nop(), "SS", 1, f"qev_{tl}_{oj}")

            @blk.tensor
            def _(t, tl=tl):
                wait(t, f"qev_{tl}_{NCJ - 1}")
                if tl == 0:
                    wait(t, "vkb_ready")
                ins = None
                for cj in range(NCJ):
                    ins = t.matmul(pb[2][0:64, :], ksbig[:, 64 * cj:64 * (cj + 1)],
                                   a_qf[:, 512 * cj:512 * (cj + 1)],
                                   start=(cj == 0), stop=(cj == NCJ - 1))
                bump(ins, "ST", 1, f"den_{tl}")

            @blk.vector
            def _(v, tl=tl):
                wait(v, f"den_{tl}")
                v.tensor_scalar_add(a_dent[:], pb[2][0:64, :], EPS_ATT)
                v.drain()
                v.reciprocal(a_dent2[:], a_dent[:])
                v.drain()
                v.tensor_copy(a_rden[:], a_dent2[:])
                v.drain()
                bump(v.nop(), "SV", 1, f"rden_{tl}")

            for cj in range(NCJ):
                @blk.tensor
                def _(t, tl=tl, cj=cj):
                    if cj == 0:
                        wait(t, f"rden_{tl}")
                        if tl == 0:
                            wait(t, "esel")
                    if cj >= 2:
                        wait(t, f"att_{tl}_{cj - 2}")
                    ins = t.matmul(pb[5 + cj % 2][:],
                                   esel[:, 128 * cj:128 * (cj + 1)], a_rden[:],
                                   start=True, stop=True)
                    bump(ins, "ST", 1, f"bc_{tl}_{cj}")
                    ins = None
                    for i in range(4):
                        ins = t.matmul(
                            pb[3 + cj % 2][32 * i:32 * (i + 1), :],
                            vkb[32 * i:32 * (i + 1), 33 * cj:33 * cj + 32],
                            a_qf[32 * i:32 * (i + 1), 512 * cj:512 * (cj + 1)],
                            start=True, stop=True, tile_position=(32 * i, 32 * i))
                    bump(ins, "ST", 1, f"num_{tl}_{cj}")

                @blk.scalar
                def _(s, tl=tl, cj=cj):
                    wait(s, f"bc_{tl}_{cj}")
                    s.activation(a_bcs[:], pb[5 + cj % 2][:], AF.Identity)
                    s.drain()
                    bump(s.nop(), "SS", 1, f"bcs_{tl}_{cj}")

                @blk.vector
                def _(v, tl=tl, cj=cj):
                    wait(v, f"num_{tl}_{cj}")
                    wait(v, f"bcs_{tl}_{cj}")
                    v.tensor_mul(a_att[:, 512 * cj:512 * (cj + 1)],
                                 pb[3 + cj % 2][:], a_bcs[:])
                    v.drain()
                    bump(v.nop(), "SV", 1, f"att_{tl}_{cj}")

            if dbg and tl == 0:
                @blk.gpsimd
                def _(g):
                    wait(g, "qev_0_15")
                    dma(g, d_qf, a_qf[:], "SG")
                    wait(g, "rden_0")
                    dma(g, d_dent, a_dent[:], "SG")
                    wait(g, "att_0_15")
                    dma(g, d_att, a_att[:], "SG", mark="dump_2a")
                    g.wait_ge(SEM["SG"], cnt["SG"])

            for oc in range(4):
                for half in range(2):
                    @blk.gpsimd
                    def _(g, tl=tl, oc=oc, half=half):
                        k = 8 * tl + 2 * oc + half
                        if k >= 2:
                            kk = k - 2
                            ptl, poc = kk // 8, (kk % 8) // 2
                            wait(g, f"pmm_{ptl}_{poc}_3")
                        cj0 = 8 * half
                        src = tproj[128 * cj0:128 * (cj0 + 8),
                                    512 * oc:512 * (oc + 1)].rearrange(
                            "(cj p) f -> p cj f", p=128)
                        dst = a_tp[half][:].rearrange("p (cj f) -> p cj f", f=512)
                        dma(g, dst, src, "SG", mark=f"tp_{tl}_{oc}_{half}")

                for mm in range(NM):
                    @blk.tensor
                    def _(t, tl=tl, oc=oc, mm=mm):
                        if oc == 0 and mm == 0:
                            wait(t, f"att_{tl}_{NCJ - 1}")
                        idx = 4 * oc + mm
                        if idx >= 2:
                            pidx = idx - 2
                            wait(t, f"pev_{tl}_{pidx // 4}_{pidx % 4}")
                        elif tl > 0:
                            pidx = 14 + idx
                            wait(t, f"pev_{tl - 1}_{pidx // 4}_{pidx % 4}")
                        bank = pb[2 * (oc % 2) + mm % 2]
                        ins = None
                        for half in range(2):
                            wait(t, f"tp_{tl}_{oc}_{half}")
                            for c8 in range(8):
                                cj = 8 * half + c8
                                ins = t.matmul(
                                    bank[:],
                                    a_att[:, 512 * cj + 128 * mm:
                                          512 * cj + 128 * (mm + 1)],
                                    a_tp[half][:, 512 * c8:512 * (c8 + 1)],
                                    start=(cj == 0), stop=(cj == NCJ - 1))
                        bump(ins, "ST", 1, f"pmm_{tl}_{oc}_{mm}")

                    @blk.vector
                    def _(v, tl=tl, oc=oc, mm=mm):
                        wait(v, f"pmm_{tl}_{oc}_{mm}")
                        if oc == 0:
                            wait(v, f"a_bp_{tl}_{mm}")
                        bank = pb[2 * (oc % 2) + mm % 2]
                        dst = a_acc[:, CD * mm + 512 * oc:CD * mm + 512 * (oc + 1)]
                        v.scalar_tensor_tensor(dst, bank[:], 1.0, dst,
                                               op0=ALU.mult, op1=ALU.add)
                        v.drain()
                        bump(v.nop(), "SV", 1, f"pev_{tl}_{oc}_{mm}")

            @blk.sync
            def _(sy, tl=tl, t0=t0):
                for oc in range(4):
                    for mm in range(NM):
                        wait(sy, f"pev_{tl}_{oc}_{mm}")
                if dbg and tl == 0:
                    dma(sy, d_out2a, a_acc[:], "SX")
                dst = out_hbm[t0:t0 + 512, :].rearrange("(m p) c -> p m c", p=128)
                src = a_acc[:].rearrange("p (m c) -> p m c", c=CD)
                dma(sy, dst, src, "SX", mark=f"out_{tl}")

        @blk.sync
        def _(sy):
            wait(sy, f"out_{NT2 - 1}")
            bump(sy.nop(), "SBAR", 1)
        for name in ("vector", "scalar", "tensor", "gpsimd"):
            def mk():
                def f(e):
                    wait(e, f"out_{NT2 - 1}")
                    bump(e.nop(), "SBAR", 1)
                return f
            getattr(blk, name)(mk())
        M["bar2"] = ("SBAR", cnt["SBAR"])

        for tl in range(NT2):
            t0 = 512 * tl
            for mm in range(NM):
                @blk.sync
                def _(sy, tl=tl, mm=mm, t0=t0):
                    if tl == 0 and mm == 0:
                        wait(sy, "bar2")
                    k = NM * tl + mm
                    if k >= 2:
                        wait(sy, f"b_xnr_{(k - 2) // NM}_{(k - 2) % NM}")
                    dma(sy, b_x[mm % 2][:],
                        x_in[t0 + 128 * mm:t0 + 128 * (mm + 1), :], "SX",
                        mark=f"bx_{tl}_{mm}")

                @blk.vector
                def _(v, tl=tl, mm=mm):
                    wait(v, f"bx_{tl}_{mm}")
                    v.reduce_sum(col(0), b_x[mm % 2][:], axis=AX.X)
                    v.drain()
                    bump(v.nop(), "SV", 1, f"b_s1_{tl}_{mm}")

                @blk.scalar
                def _(s, tl=tl, mm=mm):
                    wait(s, f"b_s1_{tl}_{mm}")
                    if mm > 0:
                        wait(s, f"b_tr_{tl}_{mm - 1}_3")
                    elif tl > 0:
                        wait(s, f"b_tr_{tl - 1}_{NM - 1}_3")
                        wait(s, f"facc_{tl - 1}_1_1")
                    s.activation(b_xn[:], b_x[mm % 2][:], AF.Square)
                    s.drain()
                    bump(s.nop(), "SS", 1, f"b_sq_{tl}_{mm}")

                @blk.vector
                def _(v, tl=tl, mm=mm):
                    wait(v, f"b_sq_{tl}_{mm}")
                    v.reduce_sum(col(4), b_xn[:], axis=AX.X)
                    v.drain()
                    bump(v.nop(), "SV", 1, f"b_s2_{tl}_{mm}")

                @blk.scalar
                def _(s, tl=tl, mm=mm):
                    wait(s, f"b_s2_{tl}_{mm}")
                    s_chain1(s)
                    bump(s.nop(), "SS", 1, f"b_sd_{tl}_{mm}")

                @blk.vector
                def _(v, tl=tl, mm=mm):
                    wait(v, f"b_sd_{tl}_{mm}")
                    v.reciprocal(col(32), col(28))
                    v.drain()
                    bump(v.nop(), "SV", 1, f"b_r0_{tl}_{mm}")

                @blk.scalar
                def _(s, tl=tl, mm=mm):
                    wait(s, f"b_r0_{tl}_{mm}")
                    s_chain2(s, rms=True)
                    bump(s.nop(), "SS", 1, f"b_sd2_{tl}_{mm}")

                @blk.vector
                def _(v, tl=tl, mm=mm):
                    wait(v, f"b_sd2_{tl}_{mm}")
                    v.reciprocal(col(76), col(72))
                    v.drain()
                    bump(v.nop(), "SV", 1, f"b_r02_{tl}_{mm}")

                @blk.scalar
                def _(s, tl=tl, mm=mm):
                    wait(s, f"b_r02_{tl}_{mm}")
                    s_chain3(s)
                    s.activation(b_xn[:], b_x[mm % 2][:], AF.Identity,
                                 scale=col(48), bias=col(56))
                    s.drain()
                    bump(s.nop(), "SS", 1, f"b_xnr_{tl}_{mm}")

                @blk.vector
                def _(v, tl=tl, mm=mm):
                    wait(v, f"b_xnr_{tl}_{mm}")
                    if mm == 0 and tl > 0:
                        wait(v, f"facc_{tl - 1}_1_1")
                    v.tensor_mul(b_gt[:], b_xn[:], gam_b[:])
                    v.drain()
                    v.reduce_max(col(96), b_gt[:], axis=AX.X,
                                 apply_absolute_value=True)
                    v.drain()
                    v.tensor_scalar(col(100), col(96), col(92), 1e-5,
                                    op0=ALU.mult, op1=ALU.max)
                    v.drain()
                    v.reciprocal(col(120), col(100))
                    v.drain()
                    v.scalar_tensor_tensor(col(104), col(120), 127.0, col(92),
                                           op0=ALU.mult, op1=ALU.mult)
                    v.drain()
                    v.tensor_scalar(col(108), col(100), swv[:, 0:1], 1.0 / 127.0,
                                    op0=ALU.mult, op1=ALU.mult)
                    v.drain()
                    v.tensor_mul(col(112), col(108), col(108))
                    v.drain()
                    v.tensor_mul(col(116), col(112), col(108))
                    v.drain()
                    v.tensor_copy(lnv[:, 124 + mm:125 + mm], col(116))
                    v.drain()
                    bump(v.nop(), "SV", 1, f"b_qv_{tl}_{mm}")

                @blk.scalar
                def _(s, tl=tl, mm=mm):
                    wait(s, f"b_qv_{tl}_{mm}")
                    s.activation(b_u[:], b_gt[:], AF.Identity, scale=col(104),
                                 bias=cvec[:, 2:3])
                    s.drain()
                    bump(s.nop(), "SS", 1, f"b_u_{tl}_{mm}")

                @blk.vector
                def _(v, tl=tl, mm=mm):
                    wait(v, f"b_u_{tl}_{mm}")
                    v.tensor_copy(b_i1[:], b_u[:])
                    v.drain()
                    bump(v.nop(), "SV", 1, f"b_i1_{tl}_{mm}")

                @blk.vector
                def _(v, tl=tl, mm=mm):
                    wait(v, f"b_i1_{tl}_{mm}")
                    v.scalar_tensor_tensor(b_c1[:], b_i1[:], 0.5, b_u[:],
                                           op0=ALU.add, op1=ALU.is_le)
                    v.drain()
                    bump(v.nop(), "SV", 1, f"b_c1_{tl}_{mm}")

                @blk.vector
                def _(v, tl=tl, mm=mm):
                    wait(v, f"b_c1_{tl}_{mm}")
                    if mm > 0:
                        wait(v, f"b_tr_{tl}_{mm - 1}_3")
                    elif tl > 0:
                        wait(v, f"b_tr_{tl - 1}_{NM - 1}_3")
                    v.scalar_tensor_tensor(b_xq32[:], b_i1[:], -128.0, b_c1[:],
                                           op0=ALU.add, op1=ALU.add)
                    v.drain()
                    bump(v.nop(), "SV", 1, f"b_xq_{tl}_{mm}")

                for half2 in range(2):
                    @blk.tensor
                    def _(t, tl=tl, mm=mm, half2=half2):
                        if half2 == 0:
                            wait(t, f"b_xq_{tl}_{mm}")
                            if mm == 0 and tl > 0:
                                wait(t, f"dev_{tl - 1}_1_7")
                        for r in (2 * half2, 2 * half2 + 1):
                            prev = None
                            if r >= 2:
                                prev = f"b_trev_{tl}_{mm}_{r - 2}"
                            elif mm > 0:
                                prev = f"b_trev_{tl}_{mm - 1}_{r + 2}"
                            elif tl > 0:
                                prev = f"b_trev_{tl - 1}_{NM - 1}_{r + 2}"
                            if prev is not None:
                                wait(t, prev)
                            ins = None
                            for j4 in range(4):
                                cj = 4 * r + j4
                                ins = t.transpose(
                                    pb[2 + (r % 2)][:, 128 * j4:128 * (j4 + 1)],
                                    b_xq32[:, 128 * cj:128 * (cj + 1)], ident32[:])
                            bump(ins, "ST", 1, f"b_tr_{tl}_{mm}_{r}")

                    @blk.scalar
                    def _(s, tl=tl, mm=mm, half2=half2):
                        for r in (2 * half2, 2 * half2 + 1):
                            wait(s, f"b_tr_{tl}_{mm}_{r}")
                            for j4 in range(4):
                                cj = 4 * r + j4
                                s.activation(
                                    b_xqf[:, 512 * cj + 128 * mm:
                                          512 * cj + 128 * (mm + 1)],
                                    pb[2 + (r % 2)][:, 128 * j4:128 * (j4 + 1)],
                                    AF.Identity)
                            s.drain()
                            bump(s.nop(), "SS", 1, f"b_trev_{tl}_{mm}_{r}")

            @blk.vector
            def _(v, tl=tl):
                if tl > 0:
                    wait(v, f"hq_{tl - 1}_{NFC - 1}")
                v.memset(b_mp[:], 0.0)
                v.drain()
                bump(v.nop(), "SV", 1, f"mpinit_{tl}")

            for pi in range(NFB2):
                if dbg and tl == 0 and pi == 1:
                    @blk.gpsimd
                    def _(g):
                        wait(g, "wg_0_0_1")
                        dma(g, d_wg, b_wg[0][:], "SG", mark="dump_wg")
                        g.wait_ge(SEM["SG"], cnt["SG"])

                @blk.sync
                def _(sy, tl=tl, pi=pi):
                    if tl == 0 and pi == 0:
                        wait(sy, "ag_gu")
                    if dbg and tl == 0 and pi == 2:
                        wait(sy, "dump_wg")
                    k = NFB2 * tl + pi
                    if k >= 2:
                        kk = k - 2
                        wait(sy, f"gu_done_{kk // NFB2}_{kk % NFB2}")
                    slot = b_wg[pi % 2]
                    for gi, fb2 in enumerate((pi, 22 + pi)):
                        dst = slot[:, 4096 * gi:4096 * (gi + 1)].rearrange(
                            "p (cj f) -> p cj f", f=256)
                        dma(sy, dst, wgu_view[fb2], "SX", mark=f"wg_{tl}_{pi}_{gi}")

                @blk.tensor
                def _(t, tl=tl, pi=pi):
                    wait(t, f"wg_{tl}_{pi}_0")
                    wait(t, f"b_trev_{tl}_{NM - 1}_3")
                    if pi >= 2:
                        wait(t, f"hid_{tl}_{pi - 2}")
                    elif tl > 0:
                        wait(t, f"dev_{tl - 1}_1_7")
                    slot = b_wg[pi % 2]
                    bk = 4 * (pi % 2)
                    for part in range(4):
                        gi, fh = part // 2, part % 2
                        if part == 2:
                            wait(t, f"wg_{tl}_{pi}_1")
                        ins = None
                        for cj in range(NCJ):
                            lhs = slot[:, 4096 * gi + 256 * cj + 128 * fh:
                                       4096 * gi + 256 * cj + 128 * (fh + 1)]
                            ins = t.matmul(pb[bk + part][:], lhs,
                                           b_xqf[:, 512 * cj:512 * (cj + 1)],
                                           start=(cj == 0), stop=(cj == NCJ - 1))
                        bump(ins, "ST", 1, f"gumm_{tl}_{pi}_{part}")

                for fh in range(2):
                    @blk.scalar
                    def _(s, tl=tl, pi=pi, fh=fh):
                        bk = 4 * (pi % 2)
                        wait(s, f"gumm_{tl}_{pi}_{fh}")
                        if (pi, fh) != (0, 0):
                            pm = (tl, pi, 0) if fh == 1 else (tl, pi - 1, 1)
                            wait(s, f"hch_{pm[0]}_{pm[1]}_{pm[2]}")
                        elif tl > 0:
                            wait(s, f"hch_{tl - 1}_{NFB2 - 1}_1")
                        s.activation(b_h1[:], pb[bk + fh][:], AF.Relu)
                        s.drain()
                        s.activation(b_h2[:], b_h1[:], AF.Square)
                        s.drain()
                        bump(s.nop(), "SS", 1, f"h2_{tl}_{pi}_{fh}")

                    @blk.vector
                    def _(v, tl=tl, pi=pi, fh=fh):
                        bk = 4 * (pi % 2)
                        wait(v, f"h2_{tl}_{pi}_{fh}")
                        wait(v, f"gumm_{tl}_{pi}_{2 + fh}")
                        fc = 2 * pi + fh
                        v.tensor_mul(b_hid[:, 512 * fc:512 * (fc + 1)],
                                     b_h2[:], pb[bk + 2 + fh][:])
                        v.drain()
                        bump(v.nop(), "SV", 1, f"hch_{tl}_{pi}_{fh}")

                @blk.vector
                def _(v, tl=tl, pi=pi):
                    bump(v.nop(), "SV", 1, f"hid_{tl}_{pi}")

                for fh in range(2):
                    @blk.scalar
                    def _(s, tl=tl, pi=pi, fh=fh):
                        wait(s, f"hch_{tl}_{pi}_{fh}")
                        if (pi, fh) != (0, 0):
                            pfh = (tl, pi, 0) if fh == 1 else (tl, pi - 1, 1)
                            wait(s, f"hmax_{pfh[0]}_{pfh[1]}_{pfh[2]}")
                        elif tl > 0:
                            wait(s, f"hmax_{tl - 1}_{NFB2 - 1}_1")
                        fc = 2 * pi + fh
                        s.activation(b_mn[:], b_hid[:, 512 * fc:512 * (fc + 1)],
                                     AF.Abs)
                        s.drain()
                        bump(s.nop(), "SS", 1, f"habs_{tl}_{pi}_{fh}")

                    @blk.vector
                    def _(v, tl=tl, pi=pi, fh=fh):
                        if pi == 0 and fh == 0:
                            wait(v, f"mpinit_{tl}")
                        wait(v, f"habs_{tl}_{pi}_{fh}")
                        v.tensor_max(b_mp[:], b_mp[:], b_mn[:])
                        v.drain()
                        bump(v.nop(), "SV", 1, f"hmax_{tl}_{pi}_{fh}")

                @blk.vector
                def _(v, tl=tl, pi=pi):
                    bump(v.nop(), "SV", 1, f"mm_{tl}_{pi}")
                    bump(v.nop(), "SV", 1, f"gu_done_{tl}_{pi}")

            @blk.vector
            def _(v, tl=tl):
                wait(v, f"mm_{tl}_{NFB2 - 1}")
                bump(v.nop(), "SV", 1, f"mabs_{tl}")

            @blk.tensor
            def _(t, tl=tl):
                wait(t, f"mabs_{tl}")
                wait(t, f"gumm_{tl}_{NFB2 - 1}_3")
                ins = None
                for bq in range(4):
                    ins = t.transpose(pb[4][:, 128 * bq:128 * (bq + 1)],
                                      b_mp[:, 128 * bq:128 * (bq + 1)], ident32[:])
                bump(ins, "ST", 1, f"mtr_{tl}")

            @blk.vector
            def _(v, tl=tl):
                wait(v, f"mtr_{tl}")
                v.reduce_max(lnv[:, 96:100], pb[4][:].rearrange(
                    "p (b c) -> p b c", c=128), axis=AX.X)
                v.drain()
                v.tensor_mul(lnv[:, 100:104], lnv[:, 96:100], lnv[:, 124:128])
                v.drain()
                v.tensor_scalar_max(lnv[:, 100:104], lnv[:, 100:104], 1e-5)
                v.drain()
                v.reciprocal(lnv[:, 104:108], lnv[:, 100:104])
                v.drain()
                v.scalar_tensor_tensor(lnv[:, 108:112], lnv[:, 104:108], 127.0,
                                       lnv[:, 124:128], op0=ALU.mult, op1=ALU.mult)
                v.drain()
                v.tensor_scalar(osq[:], lnv[:, 100:104], swv[:, 2:3],
                                1.0 / 127.0, op0=ALU.mult, op1=ALU.mult)
                v.drain()
                v.tensor_copy(sh128[:].rearrange("p (b c) -> p b c", c=32)[
                    :, :, 0:1].rearrange("p b c -> p (b c)"), lnv[:, 108:112])
                v.drain()
                bump(v.nop(), "SV", 1, f"shv_{tl}")

            @blk.tensor
            def _(t, tl=tl):
                wait(t, f"shv_{tl}")
                ins = t.transpose(pb[5][:, 0:128], sh128[:], ident32[:])
                bump(ins, "ST", 1, f"shtr_{tl}")

            @blk.vector
            def _(v, tl=tl):
                wait(v, f"shtr_{tl}")
                for bq in range(4):
                    v.tensor_copy(szpad[0:1, 128 * bq:128 * (bq + 1)],
                                  pb[5][32 * bq:32 * bq + 1, 0:128])
                v.drain()
                bump(v.nop(), "SV", 1, f"szp_{tl}")

            if dbg and tl == 0:
                @blk.gpsimd
                def _(g):
                    wait(g, "b_trev_0_3_3")
                    dma(g, d_xqf, b_xqf[:], "SG")
                    wait(g, "mm_0_21")
                    dma(g, d_hid, b_hid[:], "SG", mark="dump_hid")
                    g.wait_ge(SEM["SG"], cnt["SG"])

            @blk.tensor
            def _(t, tl=tl):
                wait(t, f"szp_{tl}")
                if tl == 0:
                    wait(t, "v_init")
                    if dbg:
                        wait(t, "dump_hid")
                ins = t.matmul(pb[6][:], ones128[:], szpad[:], start=True, stop=True)
                bump(ins, "ST", 1, f"bcq_{tl}")

            for fc in range(NFC):
                @blk.vector
                def _(v, tl=tl, fc=fc):
                    if fc == 0:
                        wait(v, f"bcq_{tl}")
                    hc = b_hid[:, 512 * fc:512 * (fc + 1)]
                    v.tensor_mul(b_h1[:], hc, pb[6][:])
                    v.drain()
                    bump(v.nop(), "SV", 1, f"hb_{tl}_{fc}")

                @blk.scalar
                def _(s, tl=tl, fc=fc):
                    wait(s, f"hb_{tl}_{fc}")
                    s.activation(b_h2[:], b_h1[:], AF.Identity, bias=cvec[:, 2:3])
                    s.drain()
                    bump(s.nop(), "SS", 1, f"hu_{tl}_{fc}")

                @blk.vector
                def _(v, tl=tl, fc=fc):
                    wait(v, f"hu_{tl}_{fc}")
                    v.tensor_copy(hq_i1[:], b_h2[:])
                    v.drain()
                    bump(v.nop(), "SV", 1, f"hi1_{tl}_{fc}")

                @blk.vector
                def _(v, tl=tl, fc=fc):
                    wait(v, f"hi1_{tl}_{fc}")
                    v.scalar_tensor_tensor(hq_c1[:], hq_i1[:], 0.5, b_h2[:],
                                           op0=ALU.add, op1=ALU.is_le)
                    v.drain()
                    bump(v.nop(), "SV", 1, f"hc1_{tl}_{fc}")

                @blk.vector
                def _(v, tl=tl, fc=fc):
                    wait(v, f"hc1_{tl}_{fc}")
                    hc = b_hid[:, 512 * fc:512 * (fc + 1)]
                    v.scalar_tensor_tensor(hc, hq_i1[:], -128.0, hq_c1[:],
                                           op0=ALU.add, op1=ALU.add)
                    v.drain()
                    bump(v.nop(), "SV", 1, f"hq_{tl}_{fc}")

            if dbg and tl == 0:
                @blk.gpsimd
                def _(g):
                    wait(g, "shv_0")
                    dma(g, d_lnv, lnv[:], "SG")
                    dma(g, d_osq, osq[:], "SG")
                    g.wait_ge(SEM["SG"], cnt["SG"])

            for mp in range(2):
                for fc in range(NFC):
                    @blk.sync
                    def _(sy, tl=tl, mp=mp, fc=fc):
                        if tl == 0 and mp == 0 and fc == 0:
                            wait(sy, "ag_dn")
                        k = (2 * tl + mp) * NFC + fc
                        if k >= 2:
                            kk = k - 2
                            wait(sy, f"dmm_{kk // (2 * NFC)}_"
                                     f"{(kk % (2 * NFC)) // NFC}_{kk % NFC}")
                        dma(sy, b_wd[fc % 2][:],
                            wdn_view[128 * fc:128 * (fc + 1), :], "SX",
                            mark=f"wd_{tl}_{mp}_{fc}")

                    @blk.tensor
                    def _(t, tl=tl, mp=mp, fc=fc):
                        wait(t, f"wd_{tl}_{mp}_{fc}")
                        if fc == 0:
                            wait(t, f"hq_{tl}_{NFC - 1}")
                            if (tl, mp) != (0, 0):
                                ptl, pmp = (tl, 0) if mp == 1 else (tl - 1, 1)
                                wait(t, f"dev_{ptl}_{pmp}_7")
                        ins = None
                        for mi in range(2):
                            mmx = 2 * mp + mi
                            lhs = b_hid[:, 512 * fc + 128 * mmx:
                                        512 * fc + 128 * (mmx + 1)]
                            for oc in range(4):
                                ins = t.matmul(pb[4 * mi + oc][:], lhs,
                                               b_wd[fc % 2][:, 512 * oc:512 * (oc + 1)],
                                               start=(fc == 0), stop=(fc == NFC - 1))
                        bump(ins, "ST", 1, f"dmm_{tl}_{mp}_{fc}")

                for mi in range(2):
                    @blk.scalar
                    def _(s, tl=tl, mp=mp, mi=mi):
                        wait(s, f"dmm_{tl}_{mp}_{NFC - 1}")
                        mmx = 2 * mp + mi
                        for oc in range(4):
                            s.activation(b_fs[mi][:, 512 * oc:512 * (oc + 1)],
                                         pb[4 * mi + oc][:], AF.Identity,
                                         scale=osq[:, mmx:mmx + 1])
                            s.drain()
                            bump(s.nop(), "SS", 1, f"dev_{tl}_{mp}_{4 * mi + oc}")
                        bump(s.nop(), "SS", 1, f"ffn_{tl}_{mp}_{mi}")

                    @blk.gpsimd
                    def _(g, tl=tl, mp=mp, mi=mi, t0=t0):
                        wait(g, f"ffn_{tl}_{mp}_{mi}")
                        wait(g, f"out_{tl}")
                        mmx = 2 * mp + mi
                        dma(g, out_hbm[t0 + 128 * mmx:t0 + 128 * (mmx + 1), :],
                            b_fs[mi][:], "SG", accum_op=ALU.add,
                            mark=f"facc_{tl}_{mp}_{mi}")

        @blk.sync
        def _(sy):
            sy.wait_ge(SEM["SX"], cnt["SX"])
            sy.wait_ge(SEM["SG"], cnt["SG"])

        @blk.gpsimd
        def _(g):
            g.wait_ge(SEM["SG"], cnt["SG"])

    return nc, es


_CACHE = {}


def kernel(L0_attn_x, L0_attn_w_qkv, L0_attn_w_proj, L0_attn_b_proj,
           L0_ffn_w_gate_up, L0_ffn_w_down, L0_ffn_norm_weight):
    x = np.ascontiguousarray(np.asarray(L0_attn_x, dtype=np.float32))
    w_qkv = np.asarray(L0_attn_w_qkv, dtype=np.float32)
    w_proj = np.asarray(L0_attn_w_proj, dtype=np.float32)
    b_proj = np.asarray(L0_attn_b_proj, dtype=np.float32)
    w_gu = np.asarray(L0_ffn_w_gate_up, dtype=np.float32)
    w_dn = np.asarray(L0_ffn_w_down, dtype=np.float32)
    gam = np.asarray(L0_ffn_norm_weight, dtype=np.float32)

    tkv_h = np.ascontiguousarray(w_qkv[CD:3 * CD, :].T)
    tq_h = np.ascontiguousarray(w_qkv[0:CD, :].T)
    tproj_h = np.ascontiguousarray(w_proj.T)
    tgu_h = np.ascontiguousarray(w_gu.T)
    tdown_h = np.ascontiguousarray(w_dn.T)
    gam_b_h = np.ascontiguousarray(np.broadcast_to(gam[None, :], (128, CD)))
    bpr_b_h = np.ascontiguousarray(np.broadcast_to(b_proj[None, :], (128, CD)))
    id_h = np.eye(128, dtype=np.float32)
    es_h = np.zeros((64, NCJ * 128), dtype=np.float32)
    for cj in range(NCJ):
        for i in range(4):
            es_h[4 * cj + i, 128 * cj + 32 * i:128 * cj + 32 * (i + 1)] = 1.0

    in_maps = []
    for c in range(8):
        b, s0, q = c // 4, (c % 4) * TOKC, c % 4
        in_maps.append({
            "x_in": np.ascontiguousarray(x[b, s0:s0 + TOKC, :]),
            "tkv": tkv_h, "tq": tq_h, "tproj": tproj_h,
            "agu_in": np.ascontiguousarray(tgu_h[512 * q:512 * (q + 1), :]),
            "adn_in": np.ascontiguousarray(tdown_h[1408 * q:1408 * (q + 1), :]),
            "qgu_in": np.ascontiguousarray(tgu_h[256 * c:256 * (c + 1), :]),
            "qdn_in": np.ascontiguousarray(tdown_h[704 * c:704 * (c + 1), :]),
            "gam_in": gam_b_h, "bpr_in": bpr_b_h,
            "id_in": id_h, "es_in": es_h,
        })

    if "nc" not in _CACHE:
        _CACHE["nc"] = build_program()
    nc, es = _CACHE["nc"]

    res = run_bass_kernel_spmd(nc, in_maps, core_ids=list(range(8)))
    out = np.empty((B, NSEQ, CD), dtype=np.float32)
    for c in range(8):
        b, s0 = c // 4, (c % 4) * TOKC
        out[b, s0:s0 + TOKC, :] = res.results[c]["out_hbm"]
    return out
